# version 04
# baseline (speedup 1.0000x reference)
import numpy as np
import ml_dtypes
from contextlib import ExitStack

import concourse.bass as bass
import concourse.mybir as mybir
from concourse.bass_utils import run_bass_kernel_spmd

F32 = mybir.dt.float32
BF16 = mybir.dt.bfloat16
I32 = mybir.dt.int32
AF = mybir.ActivationFunctionType
ALU = mybir.AluOpType
AX = mybir.AxisListType

T = 2048
D = 1024
NT = 16
NCH = 8
NH = 16
HD = 64
ADIM = 256
HID = 4096
DEPTH_B = 4
MAGIC = 0x5F3759DF
EPS_OUT = 1e-5
EPS_LN1 = 1e-6
EPS_A = 1e-6
SCALE_A = 0.125

ENGINES = ("tensor", "vector", "scalar", "gpsimd", "sync")

DEBUG_STOP = None


class Prog:

    def __init__(self, nc, ctx):
        self.nc = nc
        self.q = {e: [] for e in ENGINES}
        self.psem = {e: ctx.enter_context(nc.semaphore(f"pg_{e}")) for e in ENGINES}
        self.pcnt = {e: 0 for e in ENGINES}
        self.dsem = {}
        self.dcnt = {}
        self.ctx = ctx
        self.marks = {}
        self.lastwait = {e: {} for e in ENGINES}
        self.ngroups = 0
        self.log = []

    def chan(self, name):
        if name not in self.dsem:
            self.dsem[name] = self.ctx.enter_context(self.nc.semaphore(f"dc_{name}"))
            self.dcnt[name] = 0
        return self.dsem[name]

    def _needs(self, eng, reads, writes):
        need = {}
        def add(mk):
            if mk is None:
                return
            key, hdl, val = mk
            if key in need:
                if val > need[key][1]:
                    need[key] = (hdl, val)
            else:
                need[key] = (hdl, val)
        for r in reads:
            m = self.marks.get(r)
            if m:
                add(m["w"])
        for w in writes:
            m = self.marks.get(w)
            if m:
                add(m["w"])
                for k, (hdl, val) in m["r"].items():
                    add((k, hdl, val))
        out = []
        for key, (hdl, val) in need.items():
            if key == ("e", eng):
                pass
            lw = self.lastwait[eng].get(key, -1)
            if val > lw:
                out.append((key, hdl, val))
        return out

    def _after(self, eng, reads, writes, key, hdl, val):
        for w in writes:
            self.marks[w] = {"w": (key, hdl, val), "r": {}}
        for r in reads:
            m = self.marks.setdefault(r, {"w": None, "r": {}})
            old = m["r"].get(key)
            if old is None or val > old[1]:
                m["r"][key] = (hdl, val)

    def g(self, eng, reads, writes, fn, note=""):
        waits = self._needs(eng, reads, writes)
        for key, hdl, val in waits:
            self.lastwait[eng][key] = val
        self.pcnt[eng] += 1
        myval = self.pcnt[eng]
        mysem = self.psem[eng]
        self._after(eng, reads, writes, ("e", eng), mysem, myval)
        self.ngroups += 1
        self.log.append((eng, note, [(k, v) for k, _, v in waits],
                         ("e", eng, myval), reads, writes))

        def emit(engine, waits=waits, fn=fn, mysem=mysem, note=note):
            for key, hdl, val in waits:
                engine.wait_ge(hdl, val)
            last = fn(engine)
            assert last is not None, f"group fn returned None ({note})"
            last.then_inc(mysem, 1)
        self.q[eng].append(emit)

    def dma(self, eng, channel, reads, writes, fn, n=1, note=""):
        hdl = self.chan(channel)
        waits = self._needs(eng, reads, writes)
        for key, hdl2, val in waits:
            self.lastwait[eng][key] = val
        self.dcnt[channel] += 16 * n
        myval = self.dcnt[channel]
        self._after(eng, reads, writes, ("d", channel), hdl, myval)
        self.ngroups += 1
        self.log.append((eng, note, [(k, v) for k, _, v in waits],
                         ("d", channel, myval), reads, writes))

        def emit(engine, waits=waits, fn=fn, hdl=hdl, note=note):
            for key, hdl2, val in waits:
                engine.wait_ge(hdl2, val)
            insts = fn(engine)
            for i in insts:
                i.then_inc(hdl, 16)
        self.q[eng].append(emit)

    def fence(self, delay=0):
        snap = dict(self.pcnt)
        dsnap = dict(self.dcnt)
        for e in ENGINES:
            items = []
            for e2 in ENGINES:
                if e2 != e and snap[e2] > self.lastwait[e].get(("e", e2), -1):
                    items.append((("e", e2), self.psem[e2], snap[e2]))
            for c, v in dsnap.items():
                if c == "pre":
                    continue
                if v > self.lastwait[e].get(("d", c), -1):
                    items.append((("d", c), self.dsem[c], v))
            for key, hdl, val in items:
                self.lastwait[e][key] = val

            def emit(engine, items=items, delay=delay):
                engine.drain()
                for key, hdl, val in items:
                    engine.wait_ge(hdl, val)
                if delay:
                    engine.nop(cycle_cnt=delay)
                engine.drain()
            self.q[e].append(emit)

    def flush(self, block):
        qs = self.q
        self.q = {e: [] for e in ENGINES}

        @block.tensor
        def _(eng):
            for f in qs["tensor"]:
                f(eng)

        @block.vector
        def _(eng):
            for f in qs["vector"]:
                f(eng)

        @block.scalar
        def _(eng):
            for f in qs["scalar"]:
                f(eng)

        @block.gpsimd
        def _(eng):
            for f in qs["gpsimd"]:
                f(eng)

        @block.sync
        def _(eng):
            for f in qs["sync"]:
                f(eng)


def bank_resources(name, col0, col1, dtype_bytes=4):
    b0 = (col0 * dtype_bytes) // 2048
    b1 = (col1 * dtype_bytes - 1) // 2048
    return [f"{name}:b{i}" for i in range(b0, b1 + 1)]



def build_program(debug_stop=None):
    nc = bass.Bass("TRN2", target_bir_lowering=False)

    x_in = nc.dram_tensor("x", [T, D], F32, kind="ExternalInput").ap()
    wqkvT = nc.dram_tensor("wqkvT", [D, 3 * D], F32, kind="ExternalInput").ap()
    wkyT = nc.dram_tensor("wkyT", [D, D], F32, kind="ExternalInput").ap()
    wprojT = nc.dram_tensor("wprojT", [D, D], F32, kind="ExternalInput").ap()
    wrT = nc.dram_tensor("wrT", [D, D], F32, kind="ExternalInput").ap()
    wk2T = nc.dram_tensor("wk2T", [D, D], F32, kind="ExternalInput").ap()
    wvT = nc.dram_tensor("wvT", [D, D], F32, kind="ExternalInput").ap()
    mu_r = nc.dram_tensor("mu_r", [D], F32, kind="ExternalInput").ap()
    mu_k = nc.dram_tensor("mu_k", [D], F32, kind="ExternalInput").ap()
    mu_r_c = nc.dram_tensor("mu_r_c", [128, 8], F32, kind="ExternalInput").ap()
    mu_k_c = nc.dram_tensor("mu_k_c", [128, 8], F32, kind="ExternalInput").ap()
    ln1_g_c = nc.dram_tensor("ln1_g_c", [DEPTH_B, 128, 8], F32, kind="ExternalInput").ap()
    ln1_b_c = nc.dram_tensor("ln1_b_c", [DEPTH_B, 128, 8], F32, kind="ExternalInput").ap()
    ln2_g_c = nc.dram_tensor("ln2_g_c", [DEPTH_B, 128, 8], F32, kind="ExternalInput").ap()
    ln2_b_c = nc.dram_tensor("ln2_b_c", [DEPTH_B, 128, 8], F32, kind="ExternalInput").ap()
    ln1_g = nc.dram_tensor("ln1_g", [DEPTH_B, D], F32, kind="ExternalInput").ap()
    ln1_b = nc.dram_tensor("ln1_b", [DEPTH_B, D], F32, kind="ExternalInput").ap()
    Wq = nc.dram_tensor("Wq", [DEPTH_B, D, ADIM], F32, kind="ExternalInput").ap()
    bq = nc.dram_tensor("bq", [DEPTH_B, ADIM], F32, kind="ExternalInput").ap()
    Wk = nc.dram_tensor("Wk", [DEPTH_B, D, ADIM], F32, kind="ExternalInput").ap()
    bk = nc.dram_tensor("bk", [DEPTH_B, ADIM], F32, kind="ExternalInput").ap()
    w_g = nc.dram_tensor("w_g", [DEPTH_B, ADIM], F32, kind="ExternalInput").ap()
    Wp = nc.dram_tensor("Wp", [DEPTH_B, ADIM, ADIM], F32, kind="ExternalInput").ap()
    bp = nc.dram_tensor("bp", [DEPTH_B, ADIM], F32, kind="ExternalInput").ap()
    Wf = nc.dram_tensor("Wf", [DEPTH_B, ADIM, D], F32, kind="ExternalInput").ap()
    bf_ = nc.dram_tensor("bf", [DEPTH_B, D], F32, kind="ExternalInput").ap()
    ln2_g = nc.dram_tensor("ln2_g", [DEPTH_B, D], F32, kind="ExternalInput").ap()
    ln2_b = nc.dram_tensor("ln2_b", [DEPTH_B, D], F32, kind="ExternalInput").ap()
    W1 = nc.dram_tensor("W1", [DEPTH_B, D, HID], F32, kind="ExternalInput").ap()
    b1 = nc.dram_tensor("b1", [DEPTH_B, HID], F32, kind="ExternalInput").ap()
    W2 = nc.dram_tensor("W2", [DEPTH_B, HID, D], F32, kind="ExternalInput").ap()
    b2 = nc.dram_tensor("b2", [DEPTH_B, D], F32, kind="ExternalInput").ap()
    out = nc.dram_tensor("out", [T, D], F32, kind="ExternalOutput").ap()

    aout_d = nc.dram_tensor("aout_d", [T, D], BF16)
    x2_d = nc.dram_tensor("x2_d", [T, D], F32)
    w1bf_d = nc.dram_tensor("w1bf_d", [DEPTH_B, D, HID], BF16)
    w2bf_d = nc.dram_tensor("w2bf_d", [DEPTH_B, HID, D], BF16)

    ctx = ExitStack()
    with ctx:
        sb = lambda n, shape, dt: ctx.enter_context(nc.sbuf_tensor(n, shape, dt))
        P = Prog(nc, ctx)

        id_bf = sb("id_bf", [128, 128], BF16)
        id_f32 = sb("id_f32", [128, 128], F32)
        maskneg = sb("maskneg", [128, 128], F32)
        tril01 = sb("tril01", [128, 128], F32)
        ones_row = sb("ones_row", [1, 512], BF16)
        stat = sb("stat", [128, 64], F32)
        stat2 = sb("stat2", [128, 64], F32)
        stat3 = sb("stat3", [128, 64], F32)
        stat4 = sb("stat4", [128, 64], F32)
        stati = sb("stati", [128, 64], I32)

        def init_consts(g):
            g.memset(id_bf[:, :], 1.0)
            g.affine_select(id_bf[:, :], id_bf[:, :], [[-1, 128]], ALU.is_equal, 0.0,
                            base=0, channel_multiplier=1)
            g.memset(id_f32[:, :], 1.0)
            g.affine_select(id_f32[:, :], id_f32[:, :], [[-1, 128]], ALU.is_equal, 0.0,
                            base=0, channel_multiplier=1)
            g.memset(maskneg[:, :], 0.0)
            g.affine_select(maskneg[:, :], maskneg[:, :], [[-1, 128]], ALU.is_ge,
                            -1e30, base=0, channel_multiplier=1)
            g.memset(tril01[:, :], 1.0)
            g.affine_select(tril01[:, :], tril01[:, :], [[-1, 128]], ALU.is_ge, 0.0,
                            base=0, channel_multiplier=1)
            return g.memset(ones_row[:, :], 1.0)
        P.g("gpsimd", [], ["consts"], init_consts, "init_consts")

        for l in range(DEPTH_B):
            P.dma("gpsimd", "pre", [], [f"w1bf{l}"],
                  lambda e, l=l: [e.dma_start(w1bf_d.ap()[l], W1[l])],
                  note=f"pre1.{l}")
            P.dma("gpsimd", "pre", [], [f"w2bf{l}"],
                  lambda e, l=l: [e.dma_start(w2bf_d.ap()[l], W2[l])],
                  note=f"pre2.{l}")

        def nr_rsqrt(cols_ap, n, res, eng="vector", parts=128):
            t1 = stat2[0:parts, 56:56 + n]
            t2 = stat3[0:parts, 56:56 + n]
            if not isinstance(res, list):
                res = [res]
            P.g(eng, res, ["nr_t1"],
                lambda e, a=cols_ap: e.tensor_scalar(
                    t1.bitcast(I32), a.bitcast(I32), -0.5, MAGIC, ALU.mult, ALU.add),
                "nr_seed")
            for it in range(4):
                P.g(eng, ["nr_t1"], ["nr_t2"],
                    lambda e: e.tensor_mul(t2, t1, t1), "nr_sq")
                P.g(eng, ["nr_t2"] + res, ["nr_t2"],
                    lambda e, a=cols_ap: e.tensor_mul(t2, t2, a), "nr_mv")
                P.g(eng, ["nr_t2"], ["nr_t2"],
                    lambda e: e.tensor_scalar(t2, t2, -0.5, 1.5,
                                              ALU.mult, ALU.add), "nr_aff")
                P.g(eng, ["nr_t1", "nr_t2"], ["nr_t1"],
                    lambda e: e.tensor_mul(t1, t1, t2), "nr_up")
            P.g(eng, ["nr_t1"], res,
                lambda e, a=cols_ap: e.tensor_copy(a, t1), "nr_out")

        def alias(srcres, dstres_list):
            import copy as _copy
            m = P.marks.get(srcres)
            if m is None:
                return
            for d in dstres_list:
                P.marks[d] = {"w": m["w"], "r": dict(m["r"])}

        skip_early = debug_stop is not None and debug_stop.startswith("gb")
        skip_f1 = skip_early and debug_stop != "gbf1"
        with ExitStack() as sA:
            sbA = lambda n, shape, dt: sA.enter_context(nc.sbuf_tensor(n, shape, dt))
            psA = lambda n, shape, dt: sA.enter_context(nc.psum_tensor(n, shape, dt))

            hfm = sbA("hfm", [128, NCH * T], BF16)
            qfm = sbA("qfm", [128, NCH * T], BF16)
            kfm = sbA("kfm", [128, NCH * T], BF16)
            kyfm = sbA("kyfm", [128, NCH * T], BF16)
            vtm = sbA("vtm", [128, NT * D], BF16)
            xt = sbA("xt", [128, D], F32)
            htm = sbA("htm", [128, D], BF16)
            wbuf = [sbA(f"wbuf{i}", [128, NCH * 512], BF16) for i in range(2)]
            vscr = sbA("vscr", [128, 512], BF16)
            amscr = sbA("amscr", [128, 1024], F32)
            wdscr = sbA("wdscr", [128, 128], F32)

            WR = (0, 2048)
            WTC = 4096
            ESC = 6144
            AMT = 7168
            AOT = 9216
            wrow_ap = lambda hi, a, b: hfm[:, WR[hi] + a: WR[hi] + b]
            wT_ap = lambda a, b: hfm[:, WTC + a: WTC + b]
            escr_ap = lambda n: hfm[:, ESC: ESC + n]
            amt_ap = lambda a, b: hfm[:, AMT + a: AMT + b]
            aot_ap = lambda a, b: hfm[:, AOT + a: AOT + b]

            pRow = [psA(f"pRow{i}", [128, 1024], F32) for i in range(3)]
            pT = psA("pT_A", [128, 1024], BF16)
            pM = psA("pM_A", [128, 512], F32)

            for t in range(NT if not skip_early else 0):
                P.dma("sync", "xin", [], ["xt"],
                      lambda e, t=t: [e.dma_start(xt[:, :], x_in[t * 128:(t + 1) * 128, :])],
                      note=f"x{t}")
                P.g("vector", ["xt"], [f"sx{t}"],
                    lambda e, t=t: e.reduce_sum(stat[:, t:t + 1], xt[:, :], AX.X),
                    f"sx{t}")
                P.g("scalar", ["xt"], [f"ssq{t}", "sqscr"],
                    lambda e, t=t: e.activation(amscr[:, :], xt[:, :], AF.Square,
                                                accum_out=stat[:, 16 + t:17 + t]),
                    f"ssq{t}")
                P.g("vector", [f"sx{t}"], [f"mu{t}"],
                    lambda e, t=t: e.tensor_scalar(stat[:, 32 + t:33 + t],
                                                   stat[:, t:t + 1], 1.0 / D, None,
                                                   ALU.mult), f"mu{t}")
                P.g("vector", [f"mu{t}"], [f"musq{t}"],
                    lambda e, t=t: e.tensor_mul(stat2[:, 32 + t:33 + t],
                                                stat[:, 32 + t:33 + t],
                                                stat[:, 32 + t:33 + t]), f"musq{t}")
                P.g("vector", [f"ssq{t}", f"musq{t}"], [f"veps{t}"],
                    lambda e, t=t: e.tensor_scalar(stat3[:, 32 + t:33 + t],
                                                   stat[:, 16 + t:17 + t], 1.0 / D,
                                                   EPS_OUT, ALU.mult, ALU.add),
                    f"v1_{t}")
                P.g("vector", [f"veps{t}", f"musq{t}"], [f"veps{t}"],
                    lambda e, t=t: e.tensor_sub(stat3[:, 32 + t:33 + t],
                                                stat3[:, 32 + t:33 + t],
                                                stat2[:, 32 + t:33 + t]), f"v2_{t}")
                P.g("scalar", [f"veps{t}"], [f"lnv{t}"],
                    lambda e, t=t: e.activation(stat2[:, 48 + t:49 + t],
                                                stat3[:, 32 + t:33 + t], AF.Ln),
                    f"lnv{t}")
                P.g("scalar", [f"lnv{t}"], [f"rstd{t}"],
                    lambda e, t=t: e.activation(stat4[:, t:t + 1],
                                                stat2[:, 48 + t:49 + t], AF.Exp,
                                                scale=-0.5), f"rstd{t}")
                P.g("vector", ["xt", f"mu{t}", f"rstd{t}"], ["htm"],
                    lambda e, t=t: e.tensor_scalar(htm[:, :], xt[:, :],
                                                   stat[:, 32 + t:33 + t],
                                                   stat4[:, t:t + 1],
                                                   ALU.subtract, ALU.mult), f"ln{t}")
                def tr(e, t=t):
                    last = None
                    for c in range(NCH):
                        last = e.transpose(pT[:, c * 128:(c + 1) * 128],
                                           htm[:, c * 128:(c + 1) * 128], id_bf[:, :])
                    return last
                P.g("tensor", ["htm", "consts"], ["pT_A"], tr, f"trh{t}")
                def ev(e, t=t):
                    last = None
                    for c in range(NCH):
                        last = e.tensor_copy(hfm[:, c * T + t * 128: c * T + (t + 1) * 128],
                                             pT[:, c * 128:(c + 1) * 128])
                    return last
                P.g("vector", ["pT_A"], ["hfm"], ev, f"evh{t}")

            for s in range(8 if not skip_early else 0):
                wb = wbuf[s % 2]
                if s < 6:
                    src = wqkvT[:, s * 512:(s + 1) * 512]
                else:
                    src = wkyT[:, (s - 6) * 512:(s - 5) * 512]
                P.dma("gpsimd", "wa", [], [f"wbuf{s % 2}"],
                      lambda e, src=src, wb=wb: [e.dma_start(
                          wb[:, :].rearrange("p (c f) -> p c f", c=NCH),
                          src.rearrange("(c p) f -> p c f", p=128))],
                      note=f"wslab{s}")
                is_q = s < 2
                is_v = 4 <= s < 6
                if s < 2:
                    dst, dname = qfm, "qfm"
                elif s < 4:
                    dst, dname = kfm, "kfm"
                elif s < 6:
                    dst, dname = vtm, "vtm"
                else:
                    dst, dname = kyfm, "kyfm"
                for fc in range(4):
                    oc = (s % 2) * 4 + fc if s < 6 else (s - 6) * 4 + fc
                    for tg in range(4):
                        pp = pRow[(fc * 4 + tg) % 3]
                        pcols = (fc % 2) * 512
                        prs = bank_resources(pp.name, pcols, pcols + 512)
                        def mms(e, wb=wb, fc=fc, tg=tg, pp=pp, pcols=pcols):
                            last = None
                            for c in range(NCH):
                                last = e.matmul(
                                    pp[:, pcols:pcols + 512],
                                    wb[:, c * 512 + fc * 128: c * 512 + (fc + 1) * 128],
                                    hfm[:, c * T + tg * 512: c * T + (tg + 1) * 512],
                                    start=(c == 0), stop=(c == NCH - 1))
                            return last
                        P.g("tensor", [f"wbuf{s % 2}", "hfm"], prs, mms,
                            f"qkv{s}.{fc}.{tg}")
                        if not is_v:
                            def ev(e, dst=dst, oc=oc, tg=tg, pp=pp, pcols=pcols,
                                   is_q=is_q):
                                dcol = oc * T + tg * 512
                                if is_q:
                                    return e.tensor_scalar(
                                        dst[:, dcol:dcol + 512], pp[:, pcols:pcols + 512],
                                        SCALE_A, None, ALU.mult)
                                return e.tensor_copy(dst[:, dcol:dcol + 512],
                                                     pp[:, pcols:pcols + 512])
                            P.g("vector", prs, [dname], ev, f"ev{s}.{fc}.{tg}")
                        else:
                            P.g("vector", prs, ["vscr"],
                                lambda e, pp=pp, pcols=pcols: e.tensor_copy(
                                    vscr[:, :], pp[:, pcols:pcols + 512]),
                                f"evv{s}.{fc}.{tg}")
                            def trv(e):
                                last = None
                                for tau in range(4):
                                    last = e.transpose(pT[:, tau * 128:(tau + 1) * 128],
                                                       vscr[:, tau * 128:(tau + 1) * 128],
                                                       id_bf[:, :])
                                return last
                            P.g("tensor", ["vscr", "consts"], ["pT_A"], trv,
                                f"trv{s}.{fc}.{tg}")
                            def evv(e, oc=oc, tg=tg):
                                last = None
                                for tau in range(4):
                                    kt = tg * 4 + tau
                                    last = e.tensor_copy(
                                        vtm[:, kt * D + oc * 128: kt * D + (oc + 1) * 128],
                                        pT[:, tau * 128:(tau + 1) * 128])
                                return last
                            P.g("vector", ["pT_A"], ["vtm"], evv, f"evvt{s}.{fc}.{tg}")

            for fh in range(2 if not skip_early else 0):
                P.dma("gpsimd", "wa", [], [f"wbuf{fh}"],
                      lambda e, fh=fh: [e.dma_start(
                          wbuf[fh][:, :].rearrange("p (c f) -> p c f", c=NCH),
                          wprojT[:, fh * 512:(fh + 1) * 512].rearrange(
                              "(c p) f -> p c f", p=128))],
                      note=f"wproj{fh}")

            if debug_stop in ("hfm", "qfm", "kfm", "kyfm", "vtm"):
                srcs = {"hfm": hfm, "qfm": qfm, "kfm": kfm, "kyfm": kyfm,
                        "vtm": vtm}
                st = srcs[debug_stop]
                P.dma("gpsimd", "dbg", [debug_stop], ["outd"],
                      lambda e, st=st: [e.dma_start(
                          out.rearrange("(a p) b -> p a b", p=128),
                          st[:, :].rearrange("p (a b) -> p a b", a=16))],
                      note="dump")
                P.fence()
                with nc.Block() as block:
                    P.flush(block)
                return nc

            alias("hfm", ["wrow0", "wrow1", "wTsb", "escr", "attnm_t", "aoutt"])

            rot = 0
            for j in range(NT if not skip_early else 0):
                nkc = j + 1
                slabs = [(0, min(nkc, 8))]
                if nkc > 8:
                    slabs.append((8, nkc))
                def emit_pass3(h, j, nkc):
                    hi = h % 2
                    for g0 in range(0, nkc, 8):
                        g1 = min(g0 + 8, nkc)
                        def trw(e, hi=hi, g0=g0, g1=g1):
                            last = None
                            for kc in range(g0, g1):
                                last = e.transpose(
                                    pT[:, (kc - g0) * 128:(kc - g0 + 1) * 128],
                                    wrow_ap(hi, kc * 128, (kc + 1) * 128),
                                    id_bf[:, :])
                            return last
                        P.g("tensor", [f"wrow{hi}", "consts"], ["pT_A"], trw,
                            f"trw{j}.{h}.{g0}")
                        P.g("vector", ["pT_A"], ["wTsb"],
                            lambda e, g0=g0, g1=g1: e.tensor_copy(
                                wT_ap(g0 * 128, g1 * 128),
                                pT[:, 0:(g1 - g0) * 128]), f"evw{j}.{h}.{g0}")
                    def av(e, h=h, nkc=nkc):
                        last = None
                        for kc in range(nkc):
                            last = e.matmul(
                                pM[:, 0:64],
                                wT_ap(kc * 128, (kc + 1) * 128),
                                vtm[:, kc * D + h * 64: kc * D + h * 64 + 64],
                                start=(kc == 0), stop=(kc == nkc - 1))
                        return last
                    P.g("tensor", ["wTsb", "vtm"], ["pM_A"], av, f"av{j}.{h}")
                    P.g("vector", ["pM_A", f"rsw{hi}"], ["attnm_t"],
                        lambda e, j=j, h=h, hi=hi: e.tensor_scalar(
                            amt_ap((j % 2) * D + h * 64, (j % 2) * D + h * 64 + 64),
                            pM[:, 0:64], stat4[:, 20 + hi:21 + hi], None,
                            ALU.mult), f"evav{j}.{h}")

                for h in range(NH):
                    hi = h % 2
                    base = 64 * (h % 2)
                    ch = h // 2
                    aggcols = []
                    lyrows = []
                    for si, (c0, c1) in enumerate(slabs):
                        pp = pRow[rot % 3]; rot += 1
                        ncol = (c1 - c0) * 128
                        prs = bank_resources(pp.name, 0, ncol)
                        def mms(e, pp=pp, c0=c0, c1=c1, base=base, ch=ch, j=j):
                            last = None
                            ncol = (c1 - c0) * 128
                            qap = qfm[base:base + 64,
                                      ch * T + j * 128: ch * T + (j + 1) * 128]
                            done = 0
                            while done < ncol:
                                w = min(512, ncol - done)
                                last = e.matmul(
                                    pp[:, done:done + w], qap,
                                    kyfm[base:base + 64,
                                         ch * T + c0 * 128 + done:
                                         ch * T + c0 * 128 + done + w],
                                    start=True, stop=True)
                                done += w
                            return last
                        P.g("tensor", ["qfm", "kyfm"], prs, mms, f"ly{j}.{h}.{si}")
                        if c1 == nkc:
                            dc = (nkc - 1 - c0) * 128
                            P.g("vector", prs + ["consts"], prs,
                                lambda e, pp=pp, dc=dc: e.tensor_add(
                                    pp[:, dc:dc + 128], pp[:, dc:dc + 128],
                                    maskneg[:, :]), f"lym{j}.{h}")
                        lyrows.append((pp, ncol, prs))
                    lrows = []
                    for si, (c0, c1) in enumerate(slabs):
                        pp = pRow[rot % 3]; rot += 1
                        ncol = (c1 - c0) * 128
                        prs = bank_resources(pp.name, 0, ncol)
                        def mms2(e, pp=pp, c0=c0, c1=c1, base=base, ch=ch, j=j):
                            last = None
                            ncol = (c1 - c0) * 128
                            qap = qfm[base:base + 64,
                                      ch * T + j * 128: ch * T + (j + 1) * 128]
                            done = 0
                            while done < ncol:
                                w = min(512, ncol - done)
                                last = e.matmul(
                                    pp[:, done:done + w], qap,
                                    kfm[base:base + 64,
                                        ch * T + c0 * 128 + done:
                                        ch * T + c0 * 128 + done + w],
                                    start=True, stop=True)
                                done += w
                            return last
                        P.g("tensor", ["qfm", "kfm"], prs, mms2, f"l{j}.{h}.{si}")
                        lrows.append((pp, c0, c1, ncol, prs))
                    if h > 0:
                        emit_pass3(h - 1, j, nkc)
                    for si, (pp, ncol, prs) in enumerate(lyrows):
                        acol = 48 + 2 * hi + si
                        aggcols.append(acol)
                        P.g("scalar", prs, [f"aggp{2 * hi + si}", "escr"],
                            lambda e, pp=pp, ncol=ncol, acol=acol: e.activation(
                                escr_ap(ncol), pp[:, 0:ncol], AF.Exp,
                                accum_out=stat[:, acol:acol + 1]),
                            f"exp{j}.{h}.{si}")
                    a0, a1 = aggcols[0], (aggcols[1] if len(aggcols) == 2 else None)
                    if a1 is not None:
                        P.g("vector",
                            [f"aggp{a0 - 48}", f"aggp{a1 - 48}"], [f"agg{hi}"],
                            lambda e, a0=a0, a1=a1, hi=hi: e.tensor_add(
                                stat2[:, hi:hi + 1], stat[:, a0:a0 + 1],
                                stat[:, a1:a1 + 1]), f"agg{j}.{h}")
                        P.g("vector", [f"agg{hi}"], [f"agg{hi}"],
                            lambda e, hi=hi: e.tensor_scalar(
                                stat2[:, hi:hi + 1], stat2[:, hi:hi + 1], 1.0,
                                EPS_A, ALU.mult, ALU.add), f"agge{j}.{h}")
                    else:
                        P.g("vector", [f"aggp{a0 - 48}"], [f"agg{hi}"],
                            lambda e, a0=a0, hi=hi: e.tensor_scalar(
                                stat2[:, hi:hi + 1], stat[:, a0:a0 + 1], 1.0,
                                EPS_A, ALU.mult, ALU.add), f"agg{j}.{h}")
                    P.g("vector", [f"agg{hi}"], [f"ragg{hi}"],
                        lambda e, hi=hi: e.reciprocal(stat4[:, 16 + hi:17 + hi],
                                                      stat2[:, hi:hi + 1]),
                        f"ragg{j}.{h}")
                    P.g("vector", [f"ragg{hi}"], [f"lnb{hi}"],
                        lambda e, hi=hi: e.tensor_scalar(
                            stat4[:, 18 + hi:19 + hi], stat4[:, 16 + hi:17 + hi],
                            EPS_A, 1.0, ALU.mult, ALU.add), f"lnb{j}.{h}")
                    sumcols = []
                    for (pp, c0, c1, ncol, prs) in lrows:
                        P.g("scalar", prs + [f"ragg{hi}"], ["amscr"],
                            lambda e, pp=pp, ncol=ncol, hi=hi: e.activation(
                                amscr[:, 0:ncol], pp[:, 0:ncol], AF.Abs,
                                scale=stat4[:, 16 + hi:17 + hi]),
                            f"abs{j}.{h}")
                        has_diag = (c1 == nkc)
                        full_cols = ncol - 128 if has_diag else ncol
                        if full_cols > 0:
                            acol = 52 + 2 * hi + (0 if c0 == 0 else 1)
                            sumcols.append(acol)
                            P.g("scalar", ["amscr", f"lnb{hi}"],
                                [f"wrow{hi}", f"us{acol}"],
                                lambda e, hi=hi, c0=c0, fc=full_cols,
                                       acol=acol: e.activation(
                                    wrow_ap(hi, c0 * 128, c0 * 128 + fc),
                                    amscr[:, 0:fc], AF.Ln,
                                    bias=stat4[:, 18 + hi:19 + hi],
                                    accum_out=stat[:, acol:acol + 1]),
                                f"lnf{j}.{h}")
                        if has_diag:
                            P.g("scalar", ["amscr", f"lnb{hi}"], ["wdscr"],
                                lambda e, fc=full_cols, hi=hi: e.activation(
                                    wdscr[:, :], amscr[:, fc:fc + 128], AF.Ln,
                                    bias=stat4[:, 18 + hi:19 + hi]),
                                f"lnd{j}.{h}")
                            acol = 56 + hi
                            sumcols.append(acol)
                            P.g("vector", ["wdscr", "consts"],
                                [f"wrow{hi}", f"us{acol}"],
                                lambda e, hi=hi, j=j, acol=acol:
                                e.scalar_tensor_tensor(
                                    wrow_ap(hi, j * 128, (j + 1) * 128),
                                    wdscr[:, :], 1.0, tril01[:, :],
                                    ALU.mult, ALU.mult,
                                    accum_out=stat[:, acol:acol + 1]),
                                f"wd{j}.{h}")
                    rs = [f"us{c}" for c in sumcols]
                    if len(sumcols) == 1:
                        P.g("vector", rs, [f"sw{hi}"],
                            lambda e, sc=sumcols, hi=hi: e.tensor_scalar(
                                stat2[:, 8 + hi:9 + hi], stat[:, sc[0]:sc[0] + 1],
                                1.0, EPS_A, ALU.mult, ALU.add), f"sw{j}.{h}")
                    else:
                        P.g("vector", rs, [f"sw{hi}"],
                            lambda e, sc=sumcols, hi=hi: e.tensor_add(
                                stat2[:, 8 + hi:9 + hi], stat[:, sc[0]:sc[0] + 1],
                                stat[:, sc[1]:sc[1] + 1]), f"swa{j}.{h}")
                        if len(sumcols) == 3:
                            P.g("vector", rs + [f"sw{hi}"], [f"sw{hi}"],
                                lambda e, sc=sumcols, hi=hi: e.tensor_add(
                                    stat2[:, 8 + hi:9 + hi],
                                    stat2[:, 8 + hi:9 + hi],
                                    stat[:, sc[2]:sc[2] + 1]), f"swb{j}.{h}")
                        P.g("vector", [f"sw{hi}"], [f"sw{hi}"],
                            lambda e, hi=hi: e.tensor_scalar(
                                stat2[:, 8 + hi:9 + hi], stat2[:, 8 + hi:9 + hi],
                                1.0, EPS_A, ALU.mult, ALU.add), f"swe{j}.{h}")
                    P.g("vector", [f"sw{hi}"], [f"rsw{hi}"],
                        lambda e, hi=hi: e.reciprocal(stat4[:, 20 + hi:21 + hi],
                                                      stat2[:, 8 + hi:9 + hi]),
                        f"rsw{j}.{h}")
                emit_pass3(NH - 1, j, nkc)
                def tra(e, j=j):
                    last = None
                    for c in range(NCH):
                        last = e.transpose(pT[:, c * 128:(c + 1) * 128],
                                           amt_ap((j % 2) * D + c * 128,
                                                  (j % 2) * D + (c + 1) * 128),
                                           id_bf[:, :])
                    return last
                P.g("tensor", ["attnm_t", "consts"], ["pT_A"], tra, f"tra{j}")
                P.g("vector", ["pT_A"], ["wrow0"],
                    lambda e: e.tensor_copy(wrow_ap(0, 0, 1024), pT[:, :]), f"evam{j}")
                for fh in range(2):
                    pp = pRow[fh]
                    prs = bank_resources(pp.name, 0, 512)
                    def pmm(e, fh=fh, pp=pp):
                        last = None
                        for c in range(NCH):
                            last = e.matmul(
                                pp[:, 0:512],
                                wrow_ap(0, c * 128, (c + 1) * 128),
                                wbuf[fh][:, c * 512:(c + 1) * 512],
                                start=(c == 0), stop=(c == NCH - 1))
                        return last
                    P.g("tensor", ["wrow0", "wbuf0", "wbuf1"], prs, pmm, f"proj{j}.{fh}")
                    P.g("vector", prs, ["aoutt"],
                        lambda e, j=j, fh=fh, pp=pp: e.tensor_copy(
                            aot_ap((j % 2) * D + fh * 512, (j % 2) * D + (fh + 1) * 512),
                            pp[:, 0:512]), f"evp{j}.{fh}")
                P.dma("sync", "ao", ["aoutt"], [f"aod{j}"],
                      lambda e, j=j: [e.dma_start(
                          aout_d.ap()[j * 128:(j + 1) * 128, :],
                          aot_ap((j % 2) * D, (j % 2 + 1) * D))], note=f"ao{j}")

            P.fence()
            with nc.Block() as block:
                P.flush(block)

        with ExitStack() as sR:
            sbR = lambda n, shape, dt: sR.enter_context(nc.sbuf_tensor(n, shape, dt))
            psR = lambda n, shape, dt: sR.enter_context(nc.psum_tensor(n, shape, dt))

            z = sbR("z", [128, NT * D], F32)
            pY0 = psR("pY0", [128, 2048], F32)
            pY1 = psR("pY1", [128, 1024], F32)
            pM2 = psR("pM2", [128, 512], F32)
            pT2 = psR("pT2", [128, 1024], BF16)

            def pv(pt, c0, c1):
                return pt[:, c0:c1], bank_resources(pt.name, c0, c1)

            zt = lambda t: z[:, t * D:(t + 1) * D]
            zr = lambda t: f"z{t}"




            htm2 = sbR("htm2", [128, D], BF16)
            cscr = sbR("cscr", [128, D], F32)
            rowscr = sbR("rowscr", [1, D], F32)
            rowbf = sbR("rowbf", [1, D], BF16)

            def load_cols(tag, col_src, n, dst_ap, dst_res):
                P.dma("sync", "sm", [], [dst_res],
                      lambda e, col_src=col_src, dst_ap=dst_ap: [
                          e.dma_start(dst_ap, col_src)], note=f"{tag}ld")

            def ln_ln(tag, tiles, src_fn, eps, apply_fn):
                for qi in range(0, len(tiles), 4):
                    quad = tiles[qi:qi + 4]
                    for t in quad:
                        ap, res = src_fn(t)
                        P.g("vector", [res], [f"sx{t}"],
                            lambda e, ap=ap, t=t: e.reduce_sum(
                                stat[:, t:t + 1], ap, AX.X), f"{tag}sx{t}")
                        P.g("scalar", [res], [f"ssq{t}", "cscr"],
                            lambda e, ap=ap, t=t: e.activation(
                                cscr[:, :], ap, AF.Square,
                                accum_out=stat[:, 16 + t:17 + t]), f"{tag}ssq{t}")
                        P.g("vector", [f"sx{t}"], [f"mu{t}"],
                            lambda e, t=t: e.tensor_scalar(
                                stat[:, 32 + t:33 + t], stat[:, t:t + 1],
                                1.0 / D, None, ALU.mult), f"{tag}mu{t}")
                        P.g("vector", [f"mu{t}"], [f"musq{t}"],
                            lambda e, t=t: e.tensor_mul(
                                stat2[:, 32 + t:33 + t], stat[:, 32 + t:33 + t],
                                stat[:, 32 + t:33 + t]), f"{tag}musq{t}")
                        P.g("vector", [f"ssq{t}", f"musq{t}"], [f"veps{t}"],
                            lambda e, t=t, eps=eps: e.tensor_scalar(
                                stat3[:, 32 + t:33 + t], stat[:, 16 + t:17 + t],
                                1.0 / D, eps, ALU.mult, ALU.add), f"{tag}v1{t}")
                        P.g("vector", [f"veps{t}", f"musq{t}"], [f"veps{t}"],
                            lambda e, t=t: e.tensor_sub(
                                stat3[:, 32 + t:33 + t], stat3[:, 32 + t:33 + t],
                                stat2[:, 32 + t:33 + t]), f"{tag}v2{t}")
                    c0 = 32 + (quad[0] % 16)
                    nr_rsqrt(stat3[:, c0:c0 + len(quad)], len(quad),
                             [f"veps{t}" for t in quad])
                    for t in quad:
                        apply_fn(t)

            def transpose_tile(tag, src_ap, src_res, dst_fn, gcol=None, bcol=None,
                               gres=[]):
                def tr(e, src_ap=src_ap):
                    last = None
                    for c in range(NCH):
                        last = e.transpose(pT2[:, c * 128:(c + 1) * 128],
                                           src_ap[:, c * 128:(c + 1) * 128],
                                           id_bf[:, :])
                    return last
                P.g("tensor", [src_res, "consts"], ["pT2"], tr, f"{tag}tr")
                if gcol is None:
                    def ev(e):
                        dst_ap, _ = dst_fn(None)
                        return e.tensor_copy(
                            dst_ap, pT2[:, :].rearrange("p (c x) -> p c x", x=128))
                    _, dres = dst_fn(None)
                    P.g("vector", ["pT2"], dres, ev, f"{tag}ev")
                else:
                    for c in range(NCH):
                        dst_ap, dres = dst_fn(c)
                        P.g("vector", ["pT2"] + gres, dres,
                            lambda e, c=c, dst_ap=dst_ap: e.tensor_scalar(
                                dst_ap, pT2[:, c * 128:(c + 1) * 128],
                                gcol[:, c:c + 1], bcol[:, c:c + 1],
                                ALU.mult, ALU.add), f"{tag}ev{c}")

            def emit_ffn(tag):
                sF = sR if True else None
                for wname, wsrc in (("wr_res", wrT), ("wk_res", wk2T),
                                    ("wv_res", wvT)):
                    P.dma("gpsimd", "wa", [], [wname],
                          lambda e, wsrc=wsrc, wname=wname: [e.dma_start(
                              ffn_w[wname][:, :].rearrange("p (c f) -> p c f", c=NCH),
                              wsrc.rearrange("(c p) f -> p c f", p=128))],
                          note=f"{tag}{wname}")
                for mname, msrc in (("murc", mu_r_c), ("mukc", mu_k_c)):
                    load_cols(f"{tag}{mname}", msrc, 8,
                              mucols[mname][:, :], mname)
                    P.g("vector", [mname], [mname + "o"],
                        lambda e, mname=mname: e.tensor_scalar(
                            mucols[mname + "o"][:, :], mucols[mname][:, :],
                            -1.0, 1.0, ALU.mult, ALU.add), f"{tag}{mname}o")
                def zf(e):
                    last = None
                    for c in range(NCH):
                        last = e.memset(yfm[:, c * 2049:c * 2049 + 1], 0.0)
                    return last
                P.g("gpsimd", [], ["yfm0"], zf, f"{tag}zf")

                yv = yfm.rearrange("p (c x) -> p c x", x=2049)

                def apply_y(t):
                    P.g("vector", [zr(t), f"mu{t}", f"veps{t}"], ["htm2"],
                        lambda e, t=t: e.tensor_scalar(
                            htm2[:, :], zt(t), stat[:, 32 + t:33 + t],
                            stat3[:, 32 + t:33 + t], ALU.subtract, ALU.mult),
                        f"{tag}lna{t}")
                    transpose_tile(
                        f"{tag}y{t}", htm2, "htm2",
                        lambda c, t=t: (yv[:, :, 1 + t * 128: 1 + (t + 1) * 128],
                                        ["yfm"]))
                ln_ln(f"{tag}ln", list(range(NT)), lambda t: (zt(t), zr(t)),
                      EPS_OUT, apply_y)

                for g in range(4):
                    for c in range(NCH):
                        ya = yv[:, c, 1 + g * 512: 1 + (g + 1) * 512]
                        yp = yv[:, c, g * 512: (g + 1) * 512]
                        P.g("vector", ["yfm", "yfm0"], [f"dl{c}"],
                            lambda e, ya=ya, yp=yp, c=c: e.tensor_sub(
                                dscr[:, c * 512:(c + 1) * 512], yp, ya),
                            f"{tag}dl{g}.{c}")
                        P.g("vector", [f"dl{c}", "murco", "yfm"], [f"xr{c}"],
                            lambda e, ya=ya, c=c: e.scalar_tensor_tensor(
                                xrb[:, c * 512:(c + 1) * 512],
                                dscr[:, c * 512:(c + 1) * 512],
                                mucols["murco"][:, c:c + 1], ya,
                                ALU.mult, ALU.add), f"{tag}xr{g}.{c}")
                        P.g("vector", [f"dl{c}", "mukco", "yfm"], [f"xk{c}"],
                            lambda e, ya=ya, c=c: e.scalar_tensor_tensor(
                                xkb[:, c * 512:(c + 1) * 512],
                                dscr[:, c * 512:(c + 1) * 512],
                                mucols["mukco"][:, c:c + 1], ya,
                                ALU.mult, ALU.add), f"{tag}xk{g}.{c}")
                    for fc in range(NCH):
                        ppa, prs = pv(pY0, (fc % 4) * 512, (fc % 4) * 512 + 512)
                        def kmm(e, fc=fc, ppa=ppa):
                            last = None
                            for c in range(NCH):
                                last = e.matmul(
                                    ppa, ffn_w["wk_res"][:, c * 1024 + fc * 128:
                                                         c * 1024 + (fc + 1) * 128],
                                    xkb[:, c * 512:(c + 1) * 512],
                                    start=(c == 0), stop=(c == NCH - 1))
                            return last
                        P.g("tensor", ["wk_res"] + [f"xk{c}" for c in range(NCH)],
                            prs, kmm, f"{tag}kp{g}.{fc}")
                        P.g("vector", prs, ["cscr"],
                            lambda e, ppa=ppa: e.tensor_scalar_max(
                                cscr[:, 0:512], ppa, 0.0), f"{tag}krl{g}.{fc}")
                        P.g("scalar", ["cscr"], [f"ka{fc}"],
                            lambda e, fc=fc: e.activation(
                                kab[:, fc * 512:(fc + 1) * 512], cscr[:, 0:512],
                                AF.Square), f"{tag}ksq{g}.{fc}")
                    for tr_ in range(4):
                        t = g * 4 + tr_
                        ppr, rrs = pv(pY1, 0, 1024)
                        def rmm(e, tr_=tr_, ppr=ppr):
                            last = None
                            for n in range(2):
                                for c in range(NCH):
                                    last = e.matmul(
                                        ppr[:, n * 512:(n + 1) * 512],
                                        xrb[:, c * 512 + tr_ * 128:
                                            c * 512 + tr_ * 128 + 128],
                                        ffn_w["wr_res"][:, c * 1024 + n * 512:
                                                        c * 1024 + (n + 1) * 512],
                                        start=(c == 0), stop=(c == NCH - 1))
                            return last
                        P.g("tensor", ["wr_res"] + [f"xr{c}" for c in range(NCH)],
                            rrs, rmm, f"{tag}rp{t}")
                        P.g("scalar", rrs, ["rtm"],
                            lambda e, ppr=ppr: e.activation(
                                rtm[:, :], ppr, AF.Sigmoid), f"{tag}sig{t}")
                        ppv, vrs = pv(pY0, 0, 1024)
                        def vmm(e, tr_=tr_, ppv=ppv):
                            last = None
                            for n in range(2):
                                for c in range(NCH):
                                    last = e.matmul(
                                        ppv[:, n * 512:(n + 1) * 512],
                                        kab[:, c * 512 + tr_ * 128:
                                            c * 512 + tr_ * 128 + 128],
                                        ffn_w["wv_res"][:, c * 1024 + n * 512:
                                                        c * 1024 + (n + 1) * 512],
                                        start=(c == 0), stop=(c == NCH - 1))
                            return last
                        P.g("tensor", ["wv_res"] + [f"ka{c}" for c in range(NCH)],
                            vrs, vmm, f"{tag}vp{t}")
                        P.g("vector", vrs + ["rtm"], ["cscr"],
                            lambda e, ppv=ppv: e.scalar_tensor_tensor(
                                cscr[:, :], ppv, 1.0, rtm[:, :],
                                ALU.mult, ALU.mult), f"{tag}cmb{t}")
                        P.g("gpsimd", ["cscr", zr(t)], [zr(t)],
                            lambda e, t=t: e.tensor_add(zt(t), zt(t), cscr[:, :]),
                            f"{tag}zadd{t}")

            def alloc_ffn_bufs(sx, sfx):
                sbX = lambda n, shape, dt: sx.enter_context(
                    nc.sbuf_tensor(n + sfx, shape, dt))
                fw = {n: sbX(n, [128, NCH * 1024], BF16)
                      for n in ("wr_res", "wk_res", "wv_res")}
                mc = {n: sbX(n, [128, 8], F32)
                      for n in ("murc", "mukc", "murco", "mukco")}
                return (fw, mc, sbX("yfm", [128, NCH * 2049], BF16),
                        sbX("dscr", [128, NCH * 512], BF16),
                        sbX("xrb", [128, NCH * 512], BF16),
                        sbX("xkb", [128, NCH * 512], BF16),
                        sbX("kab", [128, NCH * 512], BF16),
                        sbX("rtm", [128, D], F32))

            sF1 = ExitStack()
            sF1.__enter__()
            (ffn_w, mucols, yfm, dscr, xrb, xkb, kab, rtm) = alloc_ffn_bufs(sF1, "")

            if skip_f1:
                for t in range(NT):
                    P.g("gpsimd", [], [zr(t)],
                        lambda e, t=t: e.memset(zt(t), 0.01 * (t + 1)), f"zm{t}")
            if skip_early and not skip_f1:
                for t in range(NT):
                    P.g("gpsimd", [], [zr(t)],
                        lambda e, t=t: e.memset(zt(t), 0.01 * (t + 1)), f"zm{t}")
            for t in range(NT if not skip_early else 0):
                P.dma("sync", "xin", [], ["cscr"],
                      lambda e, t=t: [e.dma_start(cscr[:, :],
                                                  x_in[t * 128:(t + 1) * 128, :])],
                      note=f"zx{t}")
                P.dma("sync", "ao", [f"aod{t}"], ["htm2"],
                      lambda e, t=t: [e.dma_start(
                          htm2[:, :], aout_d.ap()[t * 128:(t + 1) * 128, :])],
                      note=f"za{t}")
                P.g("vector", ["cscr", "htm2"], [zr(t)],
                    lambda e, t=t: e.tensor_add(zt(t), cscr[:, :], htm2[:, :]),
                    f"zinit{t}")

            if not skip_f1:
                emit_ffn("f1")

            for t in range(NT if not skip_early else 0):
                P.dma("sync", "x2", [zr(t)], [f"x2d{t}"],
                      lambda e, t=t: [e.dma_start(x2_d.ap()[t * 128:(t + 1) * 128, :],
                                                  zt(t))], note=f"x2w{t}")

            if debug_stop == "x2":
                for t in range(NT):
                    P.dma("sync", "out", [zr(t)], [f"out{t}"],
                          lambda e, t=t: [e.dma_start(out[t * 128:(t + 1) * 128, :],
                                                      zt(t))])
                P.fence()
                with nc.Block() as block:
                    P.flush(block)
                sF1.__exit__(None, None, None)
                return nc

            def apply_lnb(t):
                P.g("vector", [zr(t), f"mu{t}", f"veps{t}"], [zr(t)],
                    lambda e, t=t: e.tensor_scalar(
                        zt(t), zt(t), stat[:, 32 + t:33 + t],
                        stat3[:, 32 + t:33 + t], ALU.subtract, ALU.mult),
                    f"lnb{t}")
            if not skip_f1:
                ln_ln("lnb", list(range(NT)), lambda t: (zt(t), zr(t)), EPS_OUT,
                      apply_lnb)

            P.fence()
            with nc.Block() as block:
                P.flush(block)
            sF1.__exit__(None, None, None)

            sB2 = ExitStack()
            sB2.__enter__()
            sbB = lambda n, shape, dt: sB2.enter_context(
                nc.sbuf_tensor(n, shape, dt))
            hfmt = [sbB(f"hfmt{i}", [128, NCH * 128], BF16) for i in range(2)]
            qs_scr = sbB("qs_scr", [128, 4 * ADIM], F32)
            ks_scr = sbB("ks_scr", [128, 4 * ADIM], F32)
            ds_scr = sbB("ds_scr", [128, ADIM], F32)
            A_bf = sbB("A_bf", [128, NT], BF16)
            qn = sbB("qn", [128, NT * ADIM], BF16)
            kn = sbB("kn", [128, NT * ADIM], BF16)
            kfmt = sbB("kfmt", [128, ADIM], BF16)
            otb = sbB("otb", [128, ADIM], BF16)
            ofmt = sbB("ofmt", [128, ADIM], BF16)
            wq_l = [sbB("wq_l0", [128, NCH * ADIM], BF16)]
            wk_l = [sbB("wk_l0", [128, NCH * ADIM], BF16)]
            wp_l = [sbB("wp_l0", [128, 2 * ADIM], BF16)]
            wf_l = [sbB("wf_l0", [128, 2 * D], BF16)]
            brow = [sbB("brow0", [1, 5 * ADIM + 2 * D], BF16)]
            b1row = [sbB("b1row0", [1, HID], BF16)]
            gbcol = [sbB("gbcol0", [128, 32], F32)]
            wgb = [sbB("wgb0", [128, ADIM], F32)]
            wp_s = sbB("wp_s", [128, 2 * ADIM], BF16)
            gcols = sbB("gcols", [128, 2], F32)
            h2fm = sbB("h2fm", [128, NCH * 512], BF16)
            gfm = sbB("gfm", [128, 32 * 512], BF16)
            w1buf = [sbB(f"w1buf{i}", [128, NCH * 512], BF16) for i in range(2)]
            w2buf = [sbB(f"w2buf{i}", [128, 4 * 512], BF16) for i in range(2)]

            def load_layer_weights(l):
                p = 0
                P.dma("gpsimd", "wb", [], [f"wq{p}"],
                      lambda e, l=l, p=p: [e.dma_start(
                          wq_l[p][:, :].rearrange("p (c f) -> p c f", c=NCH),
                          Wq[l].rearrange("(c p) f -> p c f", p=128))],
                      note=f"Lw q{l}")
                P.dma("gpsimd", "wb", [], [f"wk{p}"],
                      lambda e, l=l, p=p: [e.dma_start(
                          wk_l[p][:, :].rearrange("p (c f) -> p c f", c=NCH),
                          Wk[l].rearrange("(c p) f -> p c f", p=128))],
                      note=f"Lw k{l}")
                P.dma("gpsimd", "wb", [], [f"wpb{p}"],
                      lambda e, l=l, p=p: [e.dma_start(
                          wp_l[p][:, :].rearrange("p (c f) -> p c f", c=2),
                          Wp[l].rearrange("(c p) f -> p c f", p=128))],
                      note=f"Lw p{l}")
                P.dma("gpsimd", "wb", [], [f"wfb{p}"],
                      lambda e, l=l, p=p: [e.dma_start(
                          wf_l[p][:, :].rearrange("p (c f) -> p c f", c=2),
                          Wf[l].rearrange("(c p) f -> p c f", p=128))],
                      note=f"Lw f{l}")
                P.dma("gpsimd", "wb", [], [f"brow{p}"],
                      lambda e, l=l, p=p: [
                          e.dma_start(brow[p][0:1, 0:ADIM], bq[l:l + 1, :]),
                          e.dma_start(brow[p][0:1, ADIM:2 * ADIM], bk[l:l + 1, :]),
                          e.dma_start(brow[p][0:1, 2 * ADIM:3 * ADIM], bp[l:l + 1, :]),
                          e.dma_start(brow[p][0:1, 3 * ADIM:3 * ADIM + D],
                                      bf_[l:l + 1, :]),
                          e.dma_start(brow[p][0:1, 3 * ADIM + D:3 * ADIM + 2 * D],
                                      b2[l:l + 1, :])],
                      n=5, note=f"Lw b{l}")
                P.dma("gpsimd", "wb", [], [f"b1row{p}"],
                      lambda e, l=l, p=p: [e.dma_start(b1row[p][0:1, :],
                                                       b1[l:l + 1, :])],
                      note=f"Lw b1{l}")
                P.dma("gpsimd", "wb", [], ["rowscr"],
                      lambda e, l=l, p=p: [e.dma_start(rowscr[0:1, 0:ADIM],
                                                       w_g[l:l + 1, :])],
                      note=f"Lw wg{l}")
                P.g("vector", ["rowscr"], ["rowbf"],
                    lambda e: e.tensor_copy(rowbf[0:1, 0:ADIM], rowscr[0:1, 0:ADIM]),
                    f"wgbf{l}")
                P.g("tensor", ["rowbf", "consts"], ["pM2:b0"],
                    lambda e: e.matmul(pM2[:, 0:ADIM], ones_row[0:1, 0:128],
                                       rowbf[0:1, 0:ADIM], start=True, stop=True),
                    f"wgmm{l}")
                P.g("vector", ["pM2:b0"], [f"wgb{p}"],
                    lambda e, p=p: e.tensor_copy(wgb[p][:, :], pM2[:, 0:ADIM]),
                    f"wgev{l}")
                for i, (vec, cc) in enumerate(((ln1_g_c, 0), (ln1_b_c, 8),
                                               (ln2_g_c, 16), (ln2_b_c, 24))):
                    load_cols(f"Lcol{l}.{i}", vec[l], 8,
                              gbcol[p][:, cc:cc + 8], f"gb{p}_{cc}")

            load_layer_weights(0)

            def emit_attnb_layer(l):
                p = 0
                tag = f"B{l}"
                g1c = gbcol[p][:, 0:8]
                b1c = gbcol[p][:, 8:16]
                g2c = gbcol[p][:, 16:24]
                b2c = gbcol[p][:, 24:32]
                gres = [f"gb{p}_0", f"gb{p}_8", f"gb{p}_16", f"gb{p}_24"]

                if debug_stop in ("gbonly", "gbf1", "gbfull"):
                    P.dma("gpsimd", "dbg",
                          [f"gb{p}_0", f"gb{p}_8", f"gb{p}_16", f"gb{p}_24"],
                          ["outd"],
                          lambda e: [e.dma_start(out[768:896, 0:32],
                                                 gbcol[p][:, :])], n=1)
                    P.fence()
                    return True

                qk_slots = [(pY0, 0), (pY0, 512), (pY0, 1024), (pY0, 1536),
                            (pY1, 0), (pY1, 512)]
                def apply_h(t):
                    P.g("vector", [zr(t), f"mu{t}", f"veps{t}"], ["htm2"],
                        lambda e, t=t: e.tensor_scalar(
                            htm2[:, :], zt(t), stat[:, 32 + t:33 + t],
                            stat3[:, 32 + t:33 + t], ALU.subtract, ALU.mult),
                        f"{tag}h{t}")
                    hb = hfmt[t % 2]
                    transpose_tile(f"{tag}ht{t}", htm2, "htm2",
                                   lambda c, t=t: (
                                       hfmt[t % 2][:, c * 128:(c + 1) * 128],
                                       [f"hfmt{t % 2}"]),
                                   gcol=g1c, bcol=b1c, gres=gres)
                    sq = qk_slots[(2 * t) % 6]
                    sk = qk_slots[(2 * t + 1) % 6]
                    qpa, qrs = pv(sq[0], sq[1], sq[1] + ADIM)
                    kpa, krs2 = pv(sk[0], sk[1], sk[1] + ADIM)
                    for nm, wl, ppx, rsx, wres in (
                            ("q", wq_l[p], qpa, qrs, f"wq{p}"),
                            ("k", wk_l[p], kpa, krs2, f"wk{p}")):
                        def qmm(e, wl=wl, ppx=ppx, t=t, nm=nm, p=p):
                            for c in range(NCH):
                                e.matmul(ppx, hfmt[t % 2][:, c * 128:(c + 1) * 128],
                                         wl[:, c * ADIM:(c + 1) * ADIM],
                                         start=(c == 0), stop=False)
                            boff = 0 if nm == "q" else ADIM
                            return e.matmul(ppx, ones_row[0:1, 0:128],
                                            brow[p][0:1, boff:boff + ADIM]
                                            if nm == "q" else
                                            brow[p][0:1, ADIM:2 * ADIM],
                                            start=False, stop=True)
                        P.g("tensor", [f"hfmt{t % 2}", wres, f"brow{p}", "consts"],
                            rsx, qmm, f"{tag}{nm}mm{t}")
                        scr = qs_scr if nm == "q" else ks_scr
                        scol = (t % 4) * ADIM
                        P.g("scalar", rsx, [f"{nm}s{t % 4}"],
                            lambda e, ppx=ppx, scr=scr, scol=scol: e.activation(
                                scr[:, scol:scol + ADIM], ppx, AF.Identity),
                            f"{tag}{nm}ev{t}")
                        ssqc = 48 + (t % 4) if nm == "q" else 52 + (t % 4)
                        P.g("vector", [f"{nm}s{t % 4}"], [f"{nm}ssq{t % 4}"],
                            lambda e, scr=scr, scol=scol, ssqc=ssqc:
                            e.scalar_tensor_tensor(
                                ds_scr[:, :], scr[:, scol:scol + ADIM], 1.0,
                                scr[:, scol:scol + ADIM], ALU.mult, ALU.mult,
                                accum_out=stat[:, ssqc:ssqc + 1]),
                            f"{tag}{nm}sq{t}")

                def finish_quad(qd):
                    P.g("vector", [f"qssq{i}" for i in range(4)] +
                        [f"kssq{i}" for i in range(4)], ["qkss"],
                        lambda e: e.tensor_scalar_max(stat[:, 48:56],
                                                      stat[:, 48:56], 1e-24),
                        f"{tag}mx{qd}")
                    nr_rsqrt(stat[:, 48:56], 8, ["qkss"])
                    for i in range(4):
                        t = qd * 4 + i
                        for nm, scr, store, sres, ssqc in (
                                ("q", qs_scr, qn, "qn", 48 + i),
                                ("k", ks_scr, kn, "kn", 52 + i)):
                            P.g("vector", ["qkss", f"{nm}s{i}"], [sres],
                                lambda e, scr=scr, i=i, t=t, store=store,
                                       ssqc=ssqc: e.tensor_scalar(
                                    store[:, t * ADIM:(t + 1) * ADIM],
                                    scr[:, i * ADIM:(i + 1) * ADIM],
                                    stat[:, ssqc:ssqc + 1], None, ALU.mult),
                                f"{tag}{nm}n{t}")
                        P.g("vector", ["qn", f"wgb{p}"], [f"Acol{t}"],
                            lambda e, t=t, p=p: e.scalar_tensor_tensor(
                                ds_scr[:, :], qn[:, t * ADIM:(t + 1) * ADIM], 1.0,
                                wgb[p][:, :], ALU.mult, ALU.mult,
                                accum_out=stat2[:, t:t + 1]), f"{tag}A{t}")

                for qd in range(4):
                    ln_ln(f"{tag}l1q{qd}", list(range(qd * 4, qd * 4 + 4)),
                          lambda t: (zt(t), zr(t)), EPS_LN1, apply_h)
                    finish_quad(qd)
                    if debug_stop in (f"s1r_{l}", "gbs1") and qd == 0:
                        P.dma("gpsimd", "dbg",
                              ["qs0", "qs1", "qs2", "qs3", "qkss", "qn", "hfmt0",
                               "gb0_0", "gb0_8", "gb0_16", "gb0_24"],
                              ["outd"],
                              lambda e: [
                                  e.dma_start(out[0:128, :], qs_scr[:, :]),
                                  e.dma_start(out[512:640, 0:8], stat[:, 48:56]),
                                  e.dma_start(out[640:768, :],
                                              hfmt[0][:, :]),
                                  e.dma_start(out[768:896, 0:32],
                                              gbcol[0][:, :])], n=4)
                        P.fence()
                        return True

                if debug_stop == f"s1_{l}":
                    P.dma("gpsimd", "dbg", ["qn", "kn"], ["outd"],
                          lambda e: [
                              e.dma_start(out[0:512, :].rearrange(
                                  "(a p) b -> p a b", p=128),
                                  qn[:, :].rearrange("p (a b) -> p a b", a=4)),
                              e.dma_start(out[512:1024, :].rearrange(
                                  "(a p) b -> p a b", p=128),
                                  kn[:, :].rearrange("p (a b) -> p a b", a=4))],
                          n=2)
                    P.fence()
                    return True

                P.g("vector", [f"Acol{t}" for t in range(NT)], ["A_bf"],
                    lambda e: e.tensor_copy(A_bf[:, :], stat2[:, 0:NT]),
                    f"{tag}Abf")
                def ssA(e):
                    last = None
                    for t in range(NT):
                        last = e.matmul(pM2[0:1, 0:1], A_bf[:, t:t + 1],
                                        A_bf[:, t:t + 1], start=(t == 0),
                                        stop=(t == NT - 1))
                    return last
                P.g("tensor", ["A_bf"], ["pM2:b0"], ssA, f"{tag}ssA")
                P.g("vector", ["pM2:b0"], ["ssa"],
                    lambda e: e.tensor_scalar_max(stat2[0:1, 20:21], pM2[0:1, 0:1],
                                                  1e-24), f"{tag}ssAe")
                nr_rsqrt(stat2[0:1, 20:21], 1, ["ssa"], parts=1)
                for fc in range(2):
                    ppg, grs = pv(pY1, fc * 512, fc * 512 + 1)
                    def gmm(e, fc=fc, ppg=ppg):
                        last = None
                        for t in range(NT):
                            last = e.matmul(
                                ppg, qn[:, t * ADIM + fc * 128:
                                        t * ADIM + (fc + 1) * 128],
                                A_bf[:, t:t + 1], start=(t == 0),
                                stop=(t == NT - 1))
                        return last
                    P.g("tensor", ["qn", "A_bf"], grs, gmm, f"{tag}G{fc}")
                    P.g("vector", grs, [f"Gc{fc}"],
                        lambda e, fc=fc, ppg=ppg: e.tensor_copy(
                            gcols[:, fc:fc + 1], ppg), f"{tag}Gev{fc}")
                P.g("vector", ["ssa"], ["rowbf"],
                    lambda e: e.tensor_copy(rowbf[0:1, 0:1], stat2[0:1, 20:21]),
                    f"{tag}rnbf")
                P.g("tensor", ["rowbf", "consts"], ["pM2:b0"],
                    lambda e: e.matmul(pM2[:, 0:1], ones_row[0:1, 0:128],
                                       rowbf[0:1, 0:1], start=True, stop=True),
                    f"{tag}rnmm")
                P.g("vector", ["pM2:b0"], ["rncol"],
                    lambda e: e.tensor_copy(stat2[:, 21:22], pM2[:, 0:1]),
                    f"{tag}rncol")
                P.g("vector", ["rncol", "Gc0", "Gc1"], ["Gs"],
                    lambda e: e.tensor_scalar(gcols[:, 0:2], gcols[:, 0:2],
                                              stat2[:, 21:22], None, ALU.mult),
                    f"{tag}Gs")
                for c in range(2):
                    P.g("gpsimd", ["Gs", f"wpb{p}"], ["wp_s"],
                        lambda e, c=c: e.tensor_scalar(
                            wp_s[:, c * ADIM:(c + 1) * ADIM],
                            wp_l[p][:, c * ADIM:(c + 1) * ADIM],
                            gcols[:, c:c + 1], None, ALU.mult), f"{tag}wps{c}")

                for t in range(NT):
                    def trk(e, t=t):
                        last = None
                        for c in range(2):
                            last = e.transpose(
                                pT2[:, c * 128:(c + 1) * 128],
                                kn[:, t * ADIM + c * 128: t * ADIM + (c + 1) * 128],
                                id_bf[:, :])
                        return last
                    P.g("tensor", ["kn", "consts"], ["pT2"], trk, f"{tag}ktr{t}")
                    P.g("vector", ["pT2"], ["kfmt"],
                        lambda e: e.tensor_copy(kfmt[:, :], pT2[:, 0:ADIM]),
                        f"{tag}kev{t}")
                    so = qk_slots[t % 6]
                    opa, ors = pv(so[0], so[1], so[1] + ADIM)
                    def omm(e, t=t, opa=opa, p=p):
                        for c in range(2):
                            e.matmul(opa, kfmt[:, c * 128:(c + 1) * 128],
                                     wp_s[:, c * ADIM:(c + 1) * ADIM],
                                     start=(c == 0), stop=False)
                        e.matmul(opa, ones_row[0:1, 0:128],
                                 brow[p][0:1, 2 * ADIM:3 * ADIM], start=False,
                                 stop=False)
                        return e.matmul(opa, id_bf[:, :],
                                        qn[:, t * ADIM:(t + 1) * ADIM],
                                        start=False, stop=True)
                    P.g("tensor", ["kfmt", "wp_s", f"brow{p}", "qn", "consts"],
                        ors, omm, f"{tag}omm{t}")
                    P.g("vector", ors, ["otb"],
                        lambda e, opa=opa: e.tensor_copy(otb[:, :], opa),
                        f"{tag}oev{t}")
                    def tro(e):
                        last = None
                        for c in range(2):
                            last = e.transpose(pT2[:, c * 128:(c + 1) * 128],
                                               otb[:, c * 128:(c + 1) * 128],
                                               id_bf[:, :])
                        return last
                    P.g("tensor", ["otb", "consts"], ["pT2"], tro, f"{tag}otr{t}")
                    P.g("vector", ["pT2"], ["ofmt"],
                        lambda e: e.tensor_copy(ofmt[:, :], pT2[:, 0:ADIM]),
                        f"{tag}oevt{t}")
                    apa, ars = pv(pY1, 0, 1024) if t % 2 == 0 else pv(pY0, 1024, 2048)
                    def amm(e, apa=apa, p=p):
                        last = None
                        for n in range(2):
                            for c in range(2):
                                e.matmul(apa[:, n * 512:(n + 1) * 512],
                                         ofmt[:, c * 128:(c + 1) * 128],
                                         wf_l[p][:, c * D + n * 512:
                                                 c * D + (n + 1) * 512],
                                         start=(c == 0), stop=False)
                            last = e.matmul(
                                apa[:, n * 512:(n + 1) * 512],
                                ones_row[0:1, 0:128],
                                brow[p][0:1, 3 * ADIM + n * 512:
                                        3 * ADIM + (n + 1) * 512],
                                start=False, stop=True)
                        return last
                    P.g("tensor", ["ofmt", f"wfb{p}", f"brow{p}", "consts"],
                        ars, amm, f"{tag}fmm{t}")
                    P.g("vector", ars + [zr(t)], [zr(t)],
                        lambda e, t=t, apa=apa: e.tensor_add(zt(t), zt(t), apa),
                        f"{tag}zadd{t}")

                if debug_stop == f"s3_{l}":
                    for t in range(NT):
                        P.dma("sync", "out", [zr(t)], [f"out{t}"],
                              lambda e, t=t: [e.dma_start(
                                  out[t * 128:(t + 1) * 128, :], zt(t))])
                    P.fence()
                    return True
                if debug_stop == f"s2_{l}":
                    P.dma("gpsimd", "dbg", ["Gs", "A_bf"], ["outd"],
                          lambda e: [
                              e.dma_start(out[0:128, 0:2], gcols[:, :]),
                              e.dma_start(out[128:256, 0:16], A_bf[:, :])], n=2)
                    P.fence()
                    return True

                for g in range(4):
                    def apply_h2(t):
                        P.g("vector", [zr(t), f"mu{t}", f"veps{t}"], ["htm2"],
                            lambda e, t=t: e.tensor_scalar(
                                htm2[:, :], zt(t), stat[:, 32 + t:33 + t],
                                stat3[:, 32 + t:33 + t], ALU.subtract, ALU.mult),
                            f"{tag}h2{t}")
                        tr_ = t % 4
                        transpose_tile(f"{tag}h2t{t}", htm2, "htm2",
                                       lambda c, tr_=tr_: (
                                           h2fm[:, c * 512 + tr_ * 128:
                                                c * 512 + tr_ * 128 + 128],
                                           ["h2fm"]),
                                       gcol=g2c, bcol=b2c, gres=gres)
                    ln_ln(f"{tag}l2g{g}", list(range(g * 4, g * 4 + 4)),
                          lambda t: (zt(t), zr(t)), EPS_LN1, apply_h2)

                    rounds = []
                    fc0 = 0
                    tog = 0
                    while fc0 < 32:
                        sz = 4 if tog == 0 else 2
                        sz = min(sz, 32 - fc0)
                        rounds.append((fc0, sz, tog))
                        fc0 += sz
                        tog ^= 1
                    for (fc0, sz, tog) in rounds:
                        for fc in range(fc0, fc0 + sz):
                            sl = fc // 4
                            if fc % 4 == 0:
                                P.dma("gpsimd", f"w1{sl % 2}", [f"w1bf{l}"],
                                      [f"w1s{sl % 2}"],
                                      lambda e, l=l, sl=sl: [e.dma_start(
                                          w1buf[sl % 2][:, :].rearrange(
                                              "p (c f) -> p c f", c=NCH),
                                          w1bf_d.ap()[l][:, sl * 512:(sl + 1) * 512]
                                          .rearrange("(c p) f -> p c f", p=128))],
                                      note=f"{tag}w1s{g}.{sl}")
                        ppt = pY0 if tog == 0 else pY1
                        for i, fc in enumerate(range(fc0, fc0 + sz)):
                            sl = fc // 4
                            ppa, prs = pv(ppt, i * 512, (i + 1) * 512)
                            def ymm(e, fc=fc, sl=sl, ppa=ppa, p=p):
                                for c in range(NCH):
                                    e.matmul(
                                        ppa,
                                        w1buf[sl % 2][:, c * 512 + (fc % 4) * 128:
                                                      c * 512 + (fc % 4) * 128 + 128],
                                        h2fm[:, c * 512:(c + 1) * 512],
                                        start=(c == 0), stop=False)
                                return e.matmul(
                                    ppa, b1row[p][0:1, fc * 128:(fc + 1) * 128],
                                    ones_row[0:1, 0:512], start=False, stop=True)
                            P.g("tensor", [f"w1s{sl % 2}", "h2fm", f"b1row{p}",
                                           "consts"], prs, ymm, f"{tag}y1.{g}.{fc}")
                        allrs = bank_resources(ppt.name, 0, sz * 512)
                        P.g("scalar", allrs, [f"gfm{fc0}"],
                            lambda e, fc0=fc0, sz=sz, ppt=ppt: e.activation(
                                gfm[:, fc0 * 512:(fc0 + sz) * 512],
                                ppt[:, 0:sz * 512], AF.Gelu), f"{tag}gel{g}.{fc0}")

                    if debug_stop == f"s4g_{l}" and g == 0:
                        P.dma("gpsimd", "dbg",
                              [f"gfm{fc0}" for (fc0, _, _) in rounds] + ["h2fm"],
                              ["outd"],
                              lambda e: [
                                  e.dma_start(out[0:2048, 0:512].rearrange(
                                      "(a p) b -> p a b", p=128),
                                      gfm[:, 0:16 * 512].rearrange(
                                          "p (a b) -> p a b", b=512)),
                                  e.dma_start(out[0:128, 512:1024],
                                              h2fm[:, 0:512])], n=2)
                        P.fence()
                        return True

                    gres_all = [f"gfm{fc0}" for (fc0, _, _) in rounds]
                    for n in range(2):
                        for qtr in range(8):
                            P.dma("gpsimd", f"w2{qtr % 2}", [f"w2bf{l}"],
                                  [f"w2s{qtr % 2}"],
                                  lambda e, l=l, n=n, qtr=qtr: [e.dma_start(
                                      w2buf[qtr % 2][:, :].rearrange(
                                          "p (c f) -> p c f", c=4),
                                      w2bf_d.ap()[l][qtr * 512:(qtr + 1) * 512,
                                                     n * 512:(n + 1) * 512]
                                      .rearrange("(c p) f -> p c f", p=128))],
                                  note=f"{tag}w2s{g}.{n}.{qtr}")
                            for tr_ in range(4):
                                t = g * 4 + tr_
                                ppa, prs = pv(pY0, tr_ * 512, (tr_ + 1) * 512)
                                def y2mm(e, n=n, qtr=qtr, tr_=tr_, ppa=ppa, p=p):
                                    last = None
                                    for kq in range(4):
                                        kc = qtr * 4 + kq
                                        last = e.matmul(
                                            ppa,
                                            gfm[:, kc * 512 + tr_ * 128:
                                                kc * 512 + tr_ * 128 + 128],
                                            w2buf[qtr % 2][:, kq * 512:
                                                           (kq + 1) * 512],
                                            start=(qtr == 0 and kq == 0),
                                            stop=False)
                                    if qtr == 7:
                                        last = e.matmul(
                                            ppa, ones_row[0:1, 0:128],
                                            brow[p][0:1, 3 * ADIM + D + n * 512:
                                                    3 * ADIM + D + (n + 1) * 512],
                                            start=False, stop=True)
                                    return last
                                P.g("tensor",
                                    gres_all + [f"w2s{qtr % 2}", f"brow{p}",
                                                "consts"],
                                    prs, y2mm, f"{tag}y2.{g}.{n}.{qtr}.{tr_}")
                        for tr_ in range(4):
                            t = g * 4 + tr_
                            ppa, prs = pv(pY0, tr_ * 512, (tr_ + 1) * 512)
                            P.g("vector", prs + [zr(t)], [zr(t)],
                                lambda e, t=t, n=n, ppa=ppa: e.tensor_add(
                                    zt(t)[:, n * 512:(n + 1) * 512],
                                    zt(t)[:, n * 512:(n + 1) * 512], ppa),
                                f"{tag}z2{t}.{n}")

                if l + 1 < DEPTH_B:
                    load_layer_weights(l + 1)

                if debug_stop == f"zb{l}":
                    for t in range(NT):
                        P.dma("sync", "out", [zr(t)], [f"out{t}"],
                              lambda e, t=t: [e.dma_start(
                                  out[t * 128:(t + 1) * 128, :], zt(t))])
                    P.fence()
                    return True
                return False

            stopped = False
            for l in range(DEPTH_B):
                if emit_attnb_layer(l):
                    stopped = True
                    break
            with nc.Block() as block:
                P.flush(block)
            sB2.__exit__(None, None, None)
            if stopped:
                return nc

            for t in range(NT):
                P.dma("sync", "xin", [f"x2d{t}"], ["cscr"],
                      lambda e, t=t: [e.dma_start(
                          cscr[:, :], x2_d.ap()[t * 128:(t + 1) * 128, :])],
                      note=f"x3r{t}")
                P.g("vector", ["cscr", zr(t)], [zr(t)],
                    lambda e, t=t: e.tensor_add(zt(t), zt(t), cscr[:, :]),
                    f"x3{t}")

            if debug_stop == "x3":
                for t in range(NT):
                    P.dma("sync", "out", [zr(t)], [f"out{t}"],
                          lambda e, t=t: [e.dma_start(out[t * 128:(t + 1) * 128, :],
                                                      zt(t))])
                P.fence()
                with nc.Block() as block:
                    P.flush(block)
                return nc

            sF2 = ExitStack()
            sF2.__enter__()
            (ffn_w, mucols, yfm, dscr, xrb, xkb, kab, rtm) = alloc_ffn_bufs(sF2, "_2")
            emit_ffn("f2")

            for t in range(NT):
                P.dma("sync", "out", [zr(t)], [f"out{t}"],
                      lambda e, t=t: [e.dma_start(out[t * 128:(t + 1) * 128, :],
                                                  zt(t))], note=f"fin{t}")
            P.fence()
            with nc.Block() as block:
                P.flush(block)
            sF2.__exit__(None, None, None)

    return nc



def kernel(**inputs):
    n = 8
    x = np.ascontiguousarray(inputs["L0_attn_a_x"])
    in_map_common = {
        "wqkvT": np.ascontiguousarray(inputs["L0_attn_a_w_qkv"].T),
        "wkyT": np.ascontiguousarray(inputs["L0_attn_a_w_ky"].T),
        "wprojT": np.ascontiguousarray(inputs["L0_attn_a_w_proj"].T),
        "wrT": np.ascontiguousarray(inputs["L0_ffn_W_r"].T),
        "wk2T": np.ascontiguousarray(inputs["L0_ffn_W_k"].T),
        "wvT": np.ascontiguousarray(inputs["L0_ffn_W_v"].T),
        "mu_r": np.ascontiguousarray(inputs["L0_ffn_mu_r"]),
        "mu_k": np.ascontiguousarray(inputs["L0_ffn_mu_k"]),
        "mu_r_c": np.ascontiguousarray(
            inputs["L0_ffn_mu_r"].reshape(8, 128).T),
        "mu_k_c": np.ascontiguousarray(
            inputs["L0_ffn_mu_k"].reshape(8, 128).T),
        "ln1_g_c": np.ascontiguousarray(
            inputs["L0_attn_b_ln1_g"].reshape(4, 8, 128).transpose(0, 2, 1)),
        "ln1_b_c": np.ascontiguousarray(
            inputs["L0_attn_b_ln1_b"].reshape(4, 8, 128).transpose(0, 2, 1)),
        "ln2_g_c": np.ascontiguousarray(
            inputs["L0_attn_b_ln2_g"].reshape(4, 8, 128).transpose(0, 2, 1)),
        "ln2_b_c": np.ascontiguousarray(
            inputs["L0_attn_b_ln2_b"].reshape(4, 8, 128).transpose(0, 2, 1)),
        "ln1_g": np.ascontiguousarray(inputs["L0_attn_b_ln1_g"]),
        "ln1_b": np.ascontiguousarray(inputs["L0_attn_b_ln1_b"]),
        "Wq": np.ascontiguousarray(inputs["L0_attn_b_Wq"]),
        "bq": np.ascontiguousarray(inputs["L0_attn_b_bq"]),
        "Wk": np.ascontiguousarray(inputs["L0_attn_b_Wk"]),
        "bk": np.ascontiguousarray(inputs["L0_attn_b_bk"]),
        "w_g": np.ascontiguousarray(inputs["L0_attn_b_w_g"][:, :, 0]),
        "Wp": np.ascontiguousarray(inputs["L0_attn_b_Wp"]),
        "bp": np.ascontiguousarray(inputs["L0_attn_b_bp"]),
        "Wf": np.ascontiguousarray(inputs["L0_attn_b_Wf"]),
        "bf": np.ascontiguousarray(inputs["L0_attn_b_bf"]),
        "ln2_g": np.ascontiguousarray(inputs["L0_attn_b_ln2_g"]),
        "ln2_b": np.ascontiguousarray(inputs["L0_attn_b_ln2_b"]),
        "W1": np.ascontiguousarray(inputs["L0_attn_b_W1"]),
        "b1": np.ascontiguousarray(inputs["L0_attn_b_b1"]),
        "W2": np.ascontiguousarray(inputs["L0_attn_b_W2"]),
        "b2": np.ascontiguousarray(inputs["L0_attn_b_b2"]),
    }
    nc = build_program(debug_stop=DEBUG_STOP)
    in_maps = []
    for i in range(n):
        m = dict(in_map_common)
        m["x"] = np.ascontiguousarray(x[i])
        in_maps.append(m)
    res = run_bass_kernel_spmd(nc, in_maps, core_ids=list(range(n)))
    return np.stack([r["out"] for r in res.results], axis=0)


# version 06 (final)
# speedup vs baseline: 1.0272x; 1.0272x over previous
import numpy as np
import ml_dtypes
from contextlib import ExitStack

import concourse.bass as bass
import concourse.mybir as mybir
from concourse.bass_utils import run_bass_kernel_spmd

F32 = mybir.dt.float32
BF16 = mybir.dt.bfloat16
I32 = mybir.dt.int32
AF = mybir.ActivationFunctionType
ALU = mybir.AluOpType
AX = mybir.AxisListType

T = 2048
D = 1024
NT = 16
NCH = 8
NH = 16
HD = 64
ADIM = 256
HID = 4096
DEPTH_B = 4
MAGIC = 0x5F3759DF
EPS_OUT = 1e-5
EPS_LN1 = 1e-6
EPS_A = 1e-6
SCALE_A = 0.125

ENGINES = ("tensor", "vector", "scalar", "gpsimd", "sync")

DEBUG_STOP = None


class Prog:

    def __init__(self, nc, ctx):
        self.nc = nc
        self.q = {e: [] for e in ENGINES}
        self.psem = {e: ctx.enter_context(nc.semaphore(f"pg_{e}")) for e in ENGINES}
        self.pcnt = {e: 0 for e in ENGINES}
        self.dsem = {}
        self.dcnt = {}
        self.ctx = ctx
        self.marks = {}
        self.lastwait = {e: {} for e in ENGINES}
        self.ngroups = 0
        self.log = []

    def chan(self, name):
        if name not in self.dsem:
            self.dsem[name] = self.ctx.enter_context(self.nc.semaphore(f"dc_{name}"))
            self.dcnt[name] = 0
        return self.dsem[name]

    def _needs(self, eng, reads, writes):
        need = {}
        def add(mk):
            if mk is None:
                return
            key, hdl, val = mk
            if key in need:
                if val > need[key][1]:
                    need[key] = (hdl, val)
            else:
                need[key] = (hdl, val)
        for r in reads:
            m = self.marks.get(r)
            if m:
                add(m["w"])
        for w in writes:
            m = self.marks.get(w)
            if m:
                add(m["w"])
                for k, (hdl, val) in m["r"].items():
                    add((k, hdl, val))
        out = []
        for key, (hdl, val) in need.items():
            if key == ("e", eng):
                pass
            lw = self.lastwait[eng].get(key, -1)
            if val > lw:
                out.append((key, hdl, val))
        return out

    def _after(self, eng, reads, writes, key, hdl, val):
        for w in writes:
            self.marks[w] = {"w": (key, hdl, val), "r": {}}
        for r in reads:
            m = self.marks.setdefault(r, {"w": None, "r": {}})
            old = m["r"].get(key)
            if old is None or val > old[1]:
                m["r"][key] = (hdl, val)

    def g(self, eng, reads, writes, fn, note=""):
        waits = self._needs(eng, reads, writes)
        for key, hdl, val in waits:
            self.lastwait[eng][key] = val
        self.pcnt[eng] += 1
        myval = self.pcnt[eng]
        mysem = self.psem[eng]
        self._after(eng, reads, writes, ("e", eng), mysem, myval)
        self.ngroups += 1
        self.log.append((eng, note, [(k, v) for k, _, v in waits],
                         ("e", eng, myval), reads, writes))

        def emit(engine, waits=waits, fn=fn, mysem=mysem, note=note):
            for key, hdl, val in waits:
                engine.wait_ge(hdl, val)
            last = fn(engine)
            assert last is not None, f"group fn returned None ({note})"
            last.then_inc(mysem, 1)
        self.q[eng].append(emit)

    def dma(self, eng, channel, reads, writes, fn, n=1, note=""):
        hdl = self.chan(channel)
        waits = self._needs(eng, reads, writes)
        for key, hdl2, val in waits:
            self.lastwait[eng][key] = val
        self.dcnt[channel] += 16 * n
        myval = self.dcnt[channel]
        self._after(eng, reads, writes, ("d", channel), hdl, myval)
        self.ngroups += 1
        self.log.append((eng, note, [(k, v) for k, _, v in waits],
                         ("d", channel, myval), reads, writes))

        def emit(engine, waits=waits, fn=fn, hdl=hdl, note=note):
            for key, hdl2, val in waits:
                engine.wait_ge(hdl2, val)
            insts = fn(engine)
            for i in insts:
                i.then_inc(hdl, 16)
        self.q[eng].append(emit)

    def fence(self, delay=0):
        snap = dict(self.pcnt)
        dsnap = dict(self.dcnt)
        for e in ENGINES:
            items = []
            for e2 in ENGINES:
                if e2 != e and snap[e2] > self.lastwait[e].get(("e", e2), -1):
                    items.append((("e", e2), self.psem[e2], snap[e2]))
            for c, v in dsnap.items():
                if c == "pre":
                    continue
                if v > self.lastwait[e].get(("d", c), -1):
                    items.append((("d", c), self.dsem[c], v))
            for key, hdl, val in items:
                self.lastwait[e][key] = val

            def emit(engine, items=items, delay=delay):
                engine.drain()
                for key, hdl, val in items:
                    engine.wait_ge(hdl, val)
                if delay:
                    engine.nop(cycle_cnt=delay)
                engine.drain()
            self.q[e].append(emit)

    def flush(self, block):
        qs = self.q
        self.q = {e: [] for e in ENGINES}

        @block.tensor
        def _(eng):
            for f in qs["tensor"]:
                f(eng)

        @block.vector
        def _(eng):
            for f in qs["vector"]:
                f(eng)

        @block.scalar
        def _(eng):
            for f in qs["scalar"]:
                f(eng)

        @block.gpsimd
        def _(eng):
            for f in qs["gpsimd"]:
                f(eng)

        @block.sync
        def _(eng):
            for f in qs["sync"]:
                f(eng)


def bank_resources(name, col0, col1, dtype_bytes=4):
    b0 = (col0 * dtype_bytes) // 2048
    b1 = (col1 * dtype_bytes - 1) // 2048
    return [f"{name}:b{i}" for i in range(b0, b1 + 1)]



def build_program(debug_stop=None):
    nc = bass.Bass("TRN2", target_bir_lowering=False)

    x_in = nc.dram_tensor("x", [T, D], F32, kind="ExternalInput").ap()
    wqkvT = nc.dram_tensor("wqkvT", [D, 3 * D], F32, kind="ExternalInput").ap()
    wkyT = nc.dram_tensor("wkyT", [D, D], F32, kind="ExternalInput").ap()
    wprojT = nc.dram_tensor("wprojT", [D, D], F32, kind="ExternalInput").ap()
    wrT = nc.dram_tensor("wrT", [D, D], F32, kind="ExternalInput").ap()
    wk2T = nc.dram_tensor("wk2T", [D, D], F32, kind="ExternalInput").ap()
    wvT = nc.dram_tensor("wvT", [D, D], F32, kind="ExternalInput").ap()
    mu_r = nc.dram_tensor("mu_r", [D], F32, kind="ExternalInput").ap()
    mu_k = nc.dram_tensor("mu_k", [D], F32, kind="ExternalInput").ap()
    mu_r_c = nc.dram_tensor("mu_r_c", [128, 8], F32, kind="ExternalInput").ap()
    mu_k_c = nc.dram_tensor("mu_k_c", [128, 8], F32, kind="ExternalInput").ap()
    ln1_g_c = nc.dram_tensor("ln1_g_c", [DEPTH_B, 128, 8], F32, kind="ExternalInput").ap()
    ln1_b_c = nc.dram_tensor("ln1_b_c", [DEPTH_B, 128, 8], F32, kind="ExternalInput").ap()
    ln2_g_c = nc.dram_tensor("ln2_g_c", [DEPTH_B, 128, 8], F32, kind="ExternalInput").ap()
    ln2_b_c = nc.dram_tensor("ln2_b_c", [DEPTH_B, 128, 8], F32, kind="ExternalInput").ap()
    ln1_g = nc.dram_tensor("ln1_g", [DEPTH_B, D], F32, kind="ExternalInput").ap()
    ln1_b = nc.dram_tensor("ln1_b", [DEPTH_B, D], F32, kind="ExternalInput").ap()
    Wq = nc.dram_tensor("Wq", [DEPTH_B, D, ADIM], F32, kind="ExternalInput").ap()
    bq = nc.dram_tensor("bq", [DEPTH_B, ADIM], F32, kind="ExternalInput").ap()
    Wk = nc.dram_tensor("Wk", [DEPTH_B, D, ADIM], F32, kind="ExternalInput").ap()
    bk = nc.dram_tensor("bk", [DEPTH_B, ADIM], F32, kind="ExternalInput").ap()
    w_g = nc.dram_tensor("w_g", [DEPTH_B, ADIM], F32, kind="ExternalInput").ap()
    Wp = nc.dram_tensor("Wp", [DEPTH_B, ADIM, ADIM], F32, kind="ExternalInput").ap()
    bp = nc.dram_tensor("bp", [DEPTH_B, ADIM], F32, kind="ExternalInput").ap()
    Wf = nc.dram_tensor("Wf", [DEPTH_B, ADIM, D], F32, kind="ExternalInput").ap()
    bf_ = nc.dram_tensor("bf", [DEPTH_B, D], F32, kind="ExternalInput").ap()
    ln2_g = nc.dram_tensor("ln2_g", [DEPTH_B, D], F32, kind="ExternalInput").ap()
    ln2_b = nc.dram_tensor("ln2_b", [DEPTH_B, D], F32, kind="ExternalInput").ap()
    W1 = nc.dram_tensor("W1", [DEPTH_B, D, HID], F32, kind="ExternalInput").ap()
    b1 = nc.dram_tensor("b1", [DEPTH_B, HID], F32, kind="ExternalInput").ap()
    W2 = nc.dram_tensor("W2", [DEPTH_B, HID, D], F32, kind="ExternalInput").ap()
    b2 = nc.dram_tensor("b2", [DEPTH_B, D], F32, kind="ExternalInput").ap()
    out = nc.dram_tensor("out", [T, D], F32, kind="ExternalOutput").ap()

    aout_d = nc.dram_tensor("aout_d", [T, D], BF16)
    x2_d = nc.dram_tensor("x2_d", [T, D], F32)
    w1bf_d = nc.dram_tensor("w1bf_d", [DEPTH_B, D, HID], BF16)
    w2bf_d = nc.dram_tensor("w2bf_d", [DEPTH_B, HID, D], BF16)

    ctx = ExitStack()
    with ctx:
        sb = lambda n, shape, dt: ctx.enter_context(nc.sbuf_tensor(n, shape, dt))
        P = Prog(nc, ctx)

        id_bf = sb("id_bf", [128, 128], BF16)
        id_f32 = sb("id_f32", [128, 128], F32)
        maskneg = sb("maskneg", [128, 128], F32)
        tril01 = sb("tril01", [128, 128], F32)
        ones_row = sb("ones_row", [1, 512], BF16)
        stat = sb("stat", [128, 64], F32)
        stat2 = sb("stat2", [128, 64], F32)
        stat3 = sb("stat3", [128, 64], F32)
        stat4 = sb("stat4", [128, 64], F32)
        stati = sb("stati", [128, 64], I32)

        P.g("gpsimd", [], ["c_idbf"],
            lambda e: e.memset(id_bf[:, :], 1.0), "ms_idbf")
        P.g("gpsimd", ["c_idbf"], ["c_idbf"],
            lambda e: e.affine_select(id_bf[:, :], id_bf[:, :], [[-1, 128]],
                                      ALU.is_equal, 0.0, base=0,
                                      channel_multiplier=1), "as_idbf")
        P.g("gpsimd", [], ["c_idf"],
            lambda e: e.memset(id_f32[:, :], 1.0), "ms_idf")
        P.g("gpsimd", ["c_idf"], ["c_idf"],
            lambda e: e.affine_select(id_f32[:, :], id_f32[:, :], [[-1, 128]],
                                      ALU.is_equal, 0.0, base=0,
                                      channel_multiplier=1), "as_idf")
        P.g("gpsimd", [], ["c_mn"],
            lambda e: e.memset(maskneg[:, :], 0.0), "ms_mn")
        P.g("gpsimd", ["c_mn"], ["c_mn"],
            lambda e: e.affine_select(maskneg[:, :], maskneg[:, :], [[-1, 128]],
                                      ALU.is_ge, -1e30, base=0,
                                      channel_multiplier=1), "as_mn")
        P.g("gpsimd", [], ["c_tril"],
            lambda e: e.memset(tril01[:, :], 1.0), "ms_tril")
        P.g("gpsimd", ["c_tril"], ["c_tril"],
            lambda e: e.affine_select(tril01[:, :], tril01[:, :], [[-1, 128]],
                                      ALU.is_ge, 0.0, base=0,
                                      channel_multiplier=1), "as_tril")
        P.g("gpsimd", ["c_idbf", "c_idf", "c_mn", "c_tril"], ["consts"],
            lambda e: e.memset(ones_row[:, :], 1.0), "ms_ones")

        for l in range(DEPTH_B):
            P.dma("gpsimd", "pre", [], [f"w1bf{l}"],
                  lambda e, l=l: [e.dma_start(w1bf_d.ap()[l], W1[l])],
                  note=f"pre1.{l}")
            P.dma("gpsimd", "pre", [], [f"w2bf{l}"],
                  lambda e, l=l: [e.dma_start(w2bf_d.ap()[l], W2[l])],
                  note=f"pre2.{l}")

        def nr_rsqrt(cols_ap, n, res, eng="vector", parts=128):
            t1 = stat2[0:parts, 56:56 + n]
            t2 = stat3[0:parts, 56:56 + n]
            if not isinstance(res, list):
                res = [res]
            P.g(eng, res, ["nr_t1"],
                lambda e, a=cols_ap: e.tensor_scalar(
                    t1.bitcast(I32), a.bitcast(I32), -0.5, MAGIC, ALU.mult, ALU.add),
                "nr_seed")
            for it in range(4):
                P.g(eng, ["nr_t1"], ["nr_t2"],
                    lambda e: e.tensor_mul(t2, t1, t1), "nr_sq")
                P.g(eng, ["nr_t2"] + res, ["nr_t2"],
                    lambda e, a=cols_ap: e.tensor_mul(t2, t2, a), "nr_mv")
                P.g(eng, ["nr_t2"], ["nr_t2"],
                    lambda e: e.tensor_scalar(t2, t2, -0.5, 1.5,
                                              ALU.mult, ALU.add), "nr_aff")
                P.g(eng, ["nr_t1", "nr_t2"], ["nr_t1"],
                    lambda e: e.tensor_mul(t1, t1, t2), "nr_up")
            P.g(eng, ["nr_t1"], res,
                lambda e, a=cols_ap: e.tensor_copy(a, t1), "nr_out")

        def alias(srcres, dstres_list):
            import copy as _copy
            m = P.marks.get(srcres)
            if m is None:
                return
            for d in dstres_list:
                P.marks[d] = {"w": m["w"], "r": dict(m["r"])}

        skip_early = debug_stop is not None and debug_stop.startswith("gb")
        skip_f1 = skip_early and debug_stop != "gbf1"
        with ExitStack() as sA:
            sbA = lambda n, shape, dt: sA.enter_context(nc.sbuf_tensor(n, shape, dt))
            psA = lambda n, shape, dt: sA.enter_context(nc.psum_tensor(n, shape, dt))

            hfm = sbA("hfm", [128, NCH * T], BF16)
            qfm = sbA("qfm", [128, NCH * T], BF16)
            kfm = sbA("kfm", [128, NCH * T], BF16)
            kyfm = sbA("kyfm", [128, NCH * T], BF16)
            vtm = sbA("vtm", [128, NT * D], BF16)
            xt = sbA("xt", [128, D], F32)
            htm = sbA("htm", [128, D], BF16)
            wbuf = [sbA(f"wbuf{i}", [128, NCH * 512], BF16) for i in range(2)]
            vscr = sbA("vscr", [128, 512], BF16)
            amscr = sbA("amscr", [128, 1024], F32)
            wdscr = sbA("wdscr", [128, 128], F32)

            WR = (0, 2048)
            WTC = 4096
            ESC = 6144
            AMT = 7168
            AOT = 9216
            wrow_ap = lambda hi, a, b: hfm[:, WR[hi] + a: WR[hi] + b]
            wT_ap = lambda a, b: hfm[:, WTC + a: WTC + b]
            escr_ap = lambda n: hfm[:, ESC: ESC + n]
            amt_ap = lambda a, b: hfm[:, AMT + a: AMT + b]
            aot_ap = lambda a, b: hfm[:, AOT + a: AOT + b]

            pRow = [psA(f"pRow{i}", [128, 1024], F32) for i in range(3)]
            pT = psA("pT_A", [128, 1024], BF16)
            pM = psA("pM_A", [128, 512], F32)

            for t in range(NT if not skip_early else 0):
                P.dma("sync", "xin", [], ["xt"],
                      lambda e, t=t: [e.dma_start(xt[:, :], x_in[t * 128:(t + 1) * 128, :])],
                      note=f"x{t}")
                P.g("vector", ["xt"], [f"sx{t}"],
                    lambda e, t=t: e.reduce_sum(stat[:, t:t + 1], xt[:, :], AX.X),
                    f"sx{t}")
                P.g("scalar", ["xt"], [f"ssq{t}", "sqscr"],
                    lambda e, t=t: e.activation(amscr[:, :], xt[:, :], AF.Square,
                                                accum_out=stat[:, 16 + t:17 + t]),
                    f"ssq{t}")
                P.g("vector", [f"sx{t}"], [f"mu{t}"],
                    lambda e, t=t: e.tensor_scalar(stat[:, 32 + t:33 + t],
                                                   stat[:, t:t + 1], 1.0 / D, None,
                                                   ALU.mult), f"mu{t}")
                P.g("vector", [f"mu{t}"], [f"musq{t}"],
                    lambda e, t=t: e.tensor_mul(stat2[:, 32 + t:33 + t],
                                                stat[:, 32 + t:33 + t],
                                                stat[:, 32 + t:33 + t]), f"musq{t}")
                P.g("vector", [f"ssq{t}", f"musq{t}"], [f"veps{t}"],
                    lambda e, t=t: e.tensor_scalar(stat3[:, 32 + t:33 + t],
                                                   stat[:, 16 + t:17 + t], 1.0 / D,
                                                   EPS_OUT, ALU.mult, ALU.add),
                    f"v1_{t}")
                P.g("vector", [f"veps{t}", f"musq{t}"], [f"veps{t}"],
                    lambda e, t=t: e.tensor_sub(stat3[:, 32 + t:33 + t],
                                                stat3[:, 32 + t:33 + t],
                                                stat2[:, 32 + t:33 + t]), f"v2_{t}")
                P.g("scalar", [f"veps{t}"], [f"lnv{t}"],
                    lambda e, t=t: e.activation(stat2[:, 48 + t:49 + t],
                                                stat3[:, 32 + t:33 + t], AF.Ln),
                    f"lnv{t}")
                P.g("scalar", [f"lnv{t}"], [f"rstd{t}"],
                    lambda e, t=t: e.activation(stat4[:, t:t + 1],
                                                stat2[:, 48 + t:49 + t], AF.Exp,
                                                scale=-0.5), f"rstd{t}")
                P.g("vector", ["xt", f"mu{t}", f"rstd{t}"], ["htm"],
                    lambda e, t=t: e.tensor_scalar(htm[:, :], xt[:, :],
                                                   stat[:, 32 + t:33 + t],
                                                   stat4[:, t:t + 1],
                                                   ALU.subtract, ALU.mult), f"ln{t}")
                def tr(e, t=t):
                    last = None
                    for c in range(NCH):
                        last = e.transpose(pT[:, c * 128:(c + 1) * 128],
                                           htm[:, c * 128:(c + 1) * 128], id_bf[:, :])
                    return last
                P.g("tensor", ["htm", "consts"], ["pT_A"], tr, f"trh{t}")
                def ev(e, t=t):
                    last = None
                    for c in range(NCH):
                        last = e.tensor_copy(hfm[:, c * T + t * 128: c * T + (t + 1) * 128],
                                             pT[:, c * 128:(c + 1) * 128])
                    return last
                P.g("vector", ["pT_A"], ["hfm"], ev, f"evh{t}")

            for s in range(8 if not skip_early else 0):
                wb = wbuf[s % 2]
                if s < 6:
                    src = wqkvT[:, s * 512:(s + 1) * 512]
                else:
                    src = wkyT[:, (s - 6) * 512:(s - 5) * 512]
                P.dma("gpsimd", "wa", [], [f"wbuf{s % 2}"],
                      lambda e, src=src, wb=wb: [e.dma_start(
                          wb[:, :].rearrange("p (c f) -> p c f", c=NCH),
                          src.rearrange("(c p) f -> p c f", p=128))],
                      note=f"wslab{s}")
                is_q = s < 2
                is_v = 4 <= s < 6
                if s < 2:
                    dst, dname = qfm, "qfm"
                elif s < 4:
                    dst, dname = kfm, "kfm"
                elif s < 6:
                    dst, dname = vtm, "vtm"
                else:
                    dst, dname = kyfm, "kyfm"
                for fc in range(4):
                    oc = (s % 2) * 4 + fc if s < 6 else (s - 6) * 4 + fc
                    for tg in range(4):
                        pp = pRow[(fc * 4 + tg) % 3]
                        pcols = (fc % 2) * 512
                        prs = bank_resources(pp.name, pcols, pcols + 512)
                        def mms(e, wb=wb, fc=fc, tg=tg, pp=pp, pcols=pcols):
                            last = None
                            for c in range(NCH):
                                last = e.matmul(
                                    pp[:, pcols:pcols + 512],
                                    wb[:, c * 512 + fc * 128: c * 512 + (fc + 1) * 128],
                                    hfm[:, c * T + tg * 512: c * T + (tg + 1) * 512],
                                    start=(c == 0), stop=(c == NCH - 1))
                            return last
                        P.g("tensor", [f"wbuf{s % 2}", "hfm"], prs, mms,
                            f"qkv{s}.{fc}.{tg}")
                        if not is_v:
                            def ev(e, dst=dst, oc=oc, tg=tg, pp=pp, pcols=pcols,
                                   is_q=is_q):
                                dcol = oc * T + tg * 512
                                if is_q:
                                    return e.tensor_scalar(
                                        dst[:, dcol:dcol + 512], pp[:, pcols:pcols + 512],
                                        SCALE_A, None, ALU.mult)
                                return e.tensor_copy(dst[:, dcol:dcol + 512],
                                                     pp[:, pcols:pcols + 512])
                            P.g("vector", prs, [dname], ev, f"ev{s}.{fc}.{tg}")
                        else:
                            P.g("vector", prs, ["vscr"],
                                lambda e, pp=pp, pcols=pcols: e.tensor_copy(
                                    vscr[:, :], pp[:, pcols:pcols + 512]),
                                f"evv{s}.{fc}.{tg}")
                            def trv(e):
                                last = None
                                for tau in range(4):
                                    last = e.transpose(pT[:, tau * 128:(tau + 1) * 128],
                                                       vscr[:, tau * 128:(tau + 1) * 128],
                                                       id_bf[:, :])
                                return last
                            P.g("tensor", ["vscr", "consts"], ["pT_A"], trv,
                                f"trv{s}.{fc}.{tg}")
                            def evv(e, oc=oc, tg=tg):
                                last = None
                                for tau in range(4):
                                    kt = tg * 4 + tau
                                    last = e.tensor_copy(
                                        vtm[:, kt * D + oc * 128: kt * D + (oc + 1) * 128],
                                        pT[:, tau * 128:(tau + 1) * 128])
                                return last
                            P.g("vector", ["pT_A"], ["vtm"], evv, f"evvt{s}.{fc}.{tg}")

            for fh in range(2 if not skip_early else 0):
                P.dma("gpsimd", "wa", [], [f"wbuf{fh}"],
                      lambda e, fh=fh: [e.dma_start(
                          wbuf[fh][:, :].rearrange("p (c f) -> p c f", c=NCH),
                          wprojT[:, fh * 512:(fh + 1) * 512].rearrange(
                              "(c p) f -> p c f", p=128))],
                      note=f"wproj{fh}")

            if debug_stop in ("hfm", "qfm", "kfm", "kyfm", "vtm"):
                srcs = {"hfm": hfm, "qfm": qfm, "kfm": kfm, "kyfm": kyfm,
                        "vtm": vtm}
                st = srcs[debug_stop]
                P.dma("gpsimd", "dbg", [debug_stop], ["outd"],
                      lambda e, st=st: [e.dma_start(
                          out.rearrange("(a p) b -> p a b", p=128),
                          st[:, :].rearrange("p (a b) -> p a b", a=16))],
                      note="dump")
                P.fence()
                with nc.Block() as block:
                    P.flush(block)
                return nc

            alias("hfm", ["wrow0", "wrow1", "wTsb", "escr", "attnm_t", "aoutt"])

            rot = 0
            for j in range(NT if not skip_early else 0):
                nkc = j + 1
                slabs = [(0, min(nkc, 8))]
                if nkc > 8:
                    slabs.append((8, nkc))
                def emit_pass3(h, j, nkc):
                    hi = h % 2
                    for g0 in range(0, nkc, 8):
                        g1 = min(g0 + 8, nkc)
                        def trw(e, hi=hi, g0=g0, g1=g1):
                            last = None
                            for kc in range(g0, g1):
                                last = e.transpose(
                                    pT[:, (kc - g0) * 128:(kc - g0 + 1) * 128],
                                    wrow_ap(hi, kc * 128, (kc + 1) * 128),
                                    id_bf[:, :])
                            return last
                        P.g("tensor", [f"wrow{hi}", "consts"], ["pT_A"], trw,
                            f"trw{j}.{h}.{g0}")
                        P.g("vector", ["pT_A"], ["wTsb"],
                            lambda e, g0=g0, g1=g1: e.tensor_copy(
                                wT_ap(g0 * 128, g1 * 128),
                                pT[:, 0:(g1 - g0) * 128]), f"evw{j}.{h}.{g0}")
                    def av(e, h=h, nkc=nkc):
                        last = None
                        for kc in range(nkc):
                            last = e.matmul(
                                pM[:, 0:64],
                                wT_ap(kc * 128, (kc + 1) * 128),
                                vtm[:, kc * D + h * 64: kc * D + h * 64 + 64],
                                start=(kc == 0), stop=(kc == nkc - 1))
                        return last
                    P.g("tensor", ["wTsb", "vtm"], ["pM_A"], av, f"av{j}.{h}")
                    P.g("vector", ["pM_A", f"rsw{hi}"], ["attnm_t"],
                        lambda e, j=j, h=h, hi=hi: e.tensor_scalar(
                            amt_ap((j % 2) * D + h * 64, (j % 2) * D + h * 64 + 64),
                            pM[:, 0:64], stat4[:, 20 + hi:21 + hi], None,
                            ALU.mult), f"evav{j}.{h}")

                def emit_pe_stage(h):
                    hi = h % 2
                    base = 64 * (h % 2)
                    ch = h // 2
                    st = {"ly": [], "l": []}
                    for si, (c0, c1) in enumerate(slabs):
                        if deep:
                            pp = pRow[0]
                        else:
                            nonlocal_rot = None
                            pp = pRow[rotbox[0] % 3]; rotbox[0] += 1
                        ncol = (c1 - c0) * 128
                        prs = bank_resources(pp.name, 0, ncol)
                        def mms(e, pp=pp, c0=c0, c1=c1, base=base, ch=ch, j=j):
                            last = None
                            ncol = (c1 - c0) * 128
                            qap = qfm[base:base + 64,
                                      ch * T + j * 128: ch * T + (j + 1) * 128]
                            done = 0
                            while done < ncol:
                                w = min(512, ncol - done)
                                last = e.matmul(
                                    pp[:, done:done + w], qap,
                                    kyfm[base:base + 64,
                                         ch * T + c0 * 128 + done:
                                         ch * T + c0 * 128 + done + w],
                                    start=True, stop=True)
                                done += w
                            return last
                        P.g("tensor", ["qfm", "kyfm"], prs, mms, f"ly{j}.{h}.{si}")
                        if c1 == nkc:
                            dc = (nkc - 1 - c0) * 128
                            P.g("vector", prs + ["consts"], prs,
                                lambda e, pp=pp, dc=dc: e.tensor_add(
                                    pp[:, dc:dc + 128], pp[:, dc:dc + 128],
                                    maskneg[:, :]), f"lym{j}.{h}")
                        st["ly"].append((pp, ncol, prs))
                    for si, (c0, c1) in enumerate(slabs):
                        if deep:
                            pp = pRow[1 + (h % 2)]
                        else:
                            pp = pRow[rotbox[0] % 3]; rotbox[0] += 1
                        ncol = (c1 - c0) * 128
                        prs = bank_resources(pp.name, 0, ncol)
                        def mms2(e, pp=pp, c0=c0, c1=c1, base=base, ch=ch, j=j):
                            last = None
                            ncol = (c1 - c0) * 128
                            qap = qfm[base:base + 64,
                                      ch * T + j * 128: ch * T + (j + 1) * 128]
                            done = 0
                            while done < ncol:
                                w = min(512, ncol - done)
                                last = e.matmul(
                                    pp[:, done:done + w], qap,
                                    kfm[base:base + 64,
                                        ch * T + c0 * 128 + done:
                                        ch * T + c0 * 128 + done + w],
                                    start=True, stop=True)
                                done += w
                            return last
                        P.g("tensor", ["qfm", "kfm"], prs, mms2, f"l{j}.{h}.{si}")
                        st["l"].append((pp, c0, c1, ncol, prs))
                    return st

                def emit_exp_stage(h, st):
                    hi = h % 2
                    aggcols = []
                    for si, (pp, ncol, prs) in enumerate(st["ly"]):
                        acol = 48 + 2 * hi + si
                        aggcols.append(acol)
                        P.g("scalar", prs, [f"aggp{2 * hi + si}", "escr"],
                            lambda e, pp=pp, ncol=ncol, acol=acol: e.activation(
                                escr_ap(ncol), pp[:, 0:ncol], AF.Exp,
                                accum_out=stat[:, acol:acol + 1]),
                            f"exp{j}.{h}.{si}")
                    a0 = aggcols[0]
                    a1 = aggcols[1] if len(aggcols) == 2 else None
                    if a1 is not None:
                        P.g("vector",
                            [f"aggp{a0 - 48}", f"aggp{a1 - 48}"], [f"agg{hi}"],
                            lambda e, a0=a0, a1=a1, hi=hi: e.tensor_add(
                                stat2[:, hi:hi + 1], stat[:, a0:a0 + 1],
                                stat[:, a1:a1 + 1]), f"agg{j}.{h}")
                        P.g("vector", [f"agg{hi}"], [f"agg{hi}"],
                            lambda e, hi=hi: e.tensor_scalar(
                                stat2[:, hi:hi + 1], stat2[:, hi:hi + 1], 1.0,
                                EPS_A, ALU.mult, ALU.add), f"agge{j}.{h}")
                    else:
                        P.g("vector", [f"aggp{a0 - 48}"], [f"agg{hi}"],
                            lambda e, a0=a0, hi=hi: e.tensor_scalar(
                                stat2[:, hi:hi + 1], stat[:, a0:a0 + 1], 1.0,
                                EPS_A, ALU.mult, ALU.add), f"agg{j}.{h}")
                    P.g("vector", [f"agg{hi}"], [f"ragg{hi}"],
                        lambda e, hi=hi: e.reciprocal(stat4[:, 16 + hi:17 + hi],
                                                      stat2[:, hi:hi + 1]),
                        f"ragg{j}.{h}")
                    P.g("vector", [f"ragg{hi}"], [f"lnb{hi}"],
                        lambda e, hi=hi: e.tensor_scalar(
                            stat4[:, 18 + hi:19 + hi], stat4[:, 16 + hi:17 + hi],
                            EPS_A, 1.0, ALU.mult, ALU.add), f"lnb{j}.{h}")

                def emit_pass2(h, st):
                    hi = h % 2
                    sumcols = []
                    for (pp, c0, c1, ncol, prs) in st["l"]:
                        P.g("scalar", prs + [f"ragg{hi}"], ["amscr"],
                            lambda e, pp=pp, ncol=ncol, hi=hi: e.activation(
                                amscr[:, 0:ncol], pp[:, 0:ncol], AF.Abs,
                                scale=stat4[:, 16 + hi:17 + hi]),
                            f"abs{j}.{h}")
                        has_diag = (c1 == nkc)
                        full_cols = ncol - 128 if has_diag else ncol
                        if full_cols > 0:
                            acol = 52 + 2 * hi + (0 if c0 == 0 else 1)
                            sumcols.append(acol)
                            P.g("scalar", ["amscr", f"lnb{hi}"],
                                [f"wrow{hi}", f"us{acol}"],
                                lambda e, hi=hi, c0=c0, fc=full_cols,
                                       acol=acol: e.activation(
                                    wrow_ap(hi, c0 * 128, c0 * 128 + fc),
                                    amscr[:, 0:fc], AF.Ln,
                                    bias=stat4[:, 18 + hi:19 + hi],
                                    accum_out=stat[:, acol:acol + 1]),
                                f"lnf{j}.{h}")
                        if has_diag:
                            P.g("scalar", ["amscr", f"lnb{hi}"], ["wdscr"],
                                lambda e, fc=full_cols, hi=hi: e.activation(
                                    wdscr[:, :], amscr[:, fc:fc + 128], AF.Ln,
                                    bias=stat4[:, 18 + hi:19 + hi]),
                                f"lnd{j}.{h}")
                            acol = 56 + hi
                            sumcols.append(acol)
                            P.g("vector", ["wdscr", "consts"],
                                [f"wrow{hi}", f"us{acol}"],
                                lambda e, hi=hi, j=j, acol=acol:
                                e.scalar_tensor_tensor(
                                    wrow_ap(hi, j * 128, (j + 1) * 128),
                                    wdscr[:, :], 1.0, tril01[:, :],
                                    ALU.mult, ALU.mult,
                                    accum_out=stat[:, acol:acol + 1]),
                                f"wd{j}.{h}")
                    rs = [f"us{c}" for c in sumcols]
                    if len(sumcols) == 1:
                        P.g("vector", rs, [f"sw{hi}"],
                            lambda e, sc=sumcols, hi=hi: e.tensor_scalar(
                                stat2[:, 8 + hi:9 + hi], stat[:, sc[0]:sc[0] + 1],
                                1.0, EPS_A, ALU.mult, ALU.add), f"sw{j}.{h}")
                    else:
                        P.g("vector", rs, [f"sw{hi}"],
                            lambda e, sc=sumcols, hi=hi: e.tensor_add(
                                stat2[:, 8 + hi:9 + hi], stat[:, sc[0]:sc[0] + 1],
                                stat[:, sc[1]:sc[1] + 1]), f"swa{j}.{h}")
                        if len(sumcols) == 3:
                            P.g("vector", rs + [f"sw{hi}"], [f"sw{hi}"],
                                lambda e, sc=sumcols, hi=hi: e.tensor_add(
                                    stat2[:, 8 + hi:9 + hi],
                                    stat2[:, 8 + hi:9 + hi],
                                    stat[:, sc[2]:sc[2] + 1]), f"swb{j}.{h}")
                        P.g("vector", [f"sw{hi}"], [f"sw{hi}"],
                            lambda e, hi=hi: e.tensor_scalar(
                                stat2[:, 8 + hi:9 + hi], stat2[:, 8 + hi:9 + hi],
                                1.0, EPS_A, ALU.mult, ALU.add), f"swe{j}.{h}")
                    P.g("vector", [f"sw{hi}"], [f"rsw{hi}"],
                        lambda e, hi=hi: e.reciprocal(stat4[:, 20 + hi:21 + hi],
                                                      stat2[:, 8 + hi:9 + hi]),
                        f"rsw{j}.{h}")

                deep = (nkc <= 8)
                rotbox = [rot]
                if deep:
                    sts = {}
                    for h in range(NH):
                        sts[h] = emit_pe_stage(h)
                        if h >= 2:
                            emit_pass3(h - 2, j, nkc)
                        emit_exp_stage(h, sts[h])
                        if h >= 1:
                            emit_pass2(h - 1, sts[h - 1])
                    emit_pass2(NH - 1, sts[NH - 1])
                    emit_pass3(NH - 2, j, nkc)
                else:
                    for h in range(NH):
                        st = emit_pe_stage(h)
                        if h > 0:
                            emit_pass3(h - 1, j, nkc)
                        emit_exp_stage(h, st)
                        emit_pass2(h, st)
                rot = rotbox[0]
                emit_pass3(NH - 1, j, nkc)
                def tra(e, j=j):
                    last = None
                    for c in range(NCH):
                        last = e.transpose(pT[:, c * 128:(c + 1) * 128],
                                           amt_ap((j % 2) * D + c * 128,
                                                  (j % 2) * D + (c + 1) * 128),
                                           id_bf[:, :])
                    return last
                P.g("tensor", ["attnm_t", "consts"], ["pT_A"], tra, f"tra{j}")
                P.g("vector", ["pT_A"], ["wrow0"],
                    lambda e: e.tensor_copy(wrow_ap(0, 0, 1024), pT[:, :]), f"evam{j}")
                for fh in range(2):
                    pp = pRow[fh]
                    prs = bank_resources(pp.name, 0, 512)
                    def pmm(e, fh=fh, pp=pp):
                        last = None
                        for c in range(NCH):
                            last = e.matmul(
                                pp[:, 0:512],
                                wrow_ap(0, c * 128, (c + 1) * 128),
                                wbuf[fh][:, c * 512:(c + 1) * 512],
                                start=(c == 0), stop=(c == NCH - 1))
                        return last
                    P.g("tensor", ["wrow0", "wbuf0", "wbuf1"], prs, pmm, f"proj{j}.{fh}")
                    P.g("vector", prs, ["aoutt"],
                        lambda e, j=j, fh=fh, pp=pp: e.tensor_copy(
                            aot_ap((j % 2) * D + fh * 512, (j % 2) * D + (fh + 1) * 512),
                            pp[:, 0:512]), f"evp{j}.{fh}")
                P.dma("sync", "ao", ["aoutt"], [f"aod{j}"],
                      lambda e, j=j: [e.dma_start(
                          aout_d.ap()[j * 128:(j + 1) * 128, :],
                          aot_ap((j % 2) * D, (j % 2 + 1) * D))], note=f"ao{j}")

            P.fence()
            with nc.Block() as block:
                P.flush(block)

        with ExitStack() as sR:
            sbR = lambda n, shape, dt: sR.enter_context(nc.sbuf_tensor(n, shape, dt))
            psR = lambda n, shape, dt: sR.enter_context(nc.psum_tensor(n, shape, dt))

            z = sbR("z", [128, NT * D], F32)
            pY0 = psR("pY0", [128, 2048], F32)
            pY1 = psR("pY1", [128, 1024], F32)
            pM2 = psR("pM2", [128, 512], F32)
            pT2 = psR("pT2", [128, 1024], BF16)

            def pv(pt, c0, c1):
                return pt[:, c0:c1], bank_resources(pt.name, c0, c1)

            zt = lambda t: z[:, t * D:(t + 1) * D]
            zr = lambda t: f"z{t}"




            htm2 = sbR("htm2", [128, D], BF16)
            cscr = sbR("cscr", [128, D], F32)
            rowscr = sbR("rowscr", [1, D], F32)
            rowbf = sbR("rowbf", [1, D], BF16)

            def load_cols(tag, col_src, n, dst_ap, dst_res):
                P.dma("sync", "sm", [], [dst_res],
                      lambda e, col_src=col_src, dst_ap=dst_ap: [
                          e.dma_start(dst_ap, col_src)], note=f"{tag}ld")

            def ln_ln(tag, tiles, src_fn, eps, apply_fn):
                for qi in range(0, len(tiles), 4):
                    quad = tiles[qi:qi + 4]
                    for t in quad:
                        ap, res = src_fn(t)
                        P.g("vector", [res], [f"sx{t}"],
                            lambda e, ap=ap, t=t: e.reduce_sum(
                                stat[:, t:t + 1], ap, AX.X), f"{tag}sx{t}")
                        P.g("scalar", [res], [f"ssq{t}", "cscr"],
                            lambda e, ap=ap, t=t: e.activation(
                                cscr[:, :], ap, AF.Square,
                                accum_out=stat[:, 16 + t:17 + t]), f"{tag}ssq{t}")
                        P.g("vector", [f"sx{t}"], [f"mu{t}"],
                            lambda e, t=t: e.tensor_scalar(
                                stat[:, 32 + t:33 + t], stat[:, t:t + 1],
                                1.0 / D, None, ALU.mult), f"{tag}mu{t}")
                        P.g("vector", [f"mu{t}"], [f"musq{t}"],
                            lambda e, t=t: e.tensor_mul(
                                stat2[:, 32 + t:33 + t], stat[:, 32 + t:33 + t],
                                stat[:, 32 + t:33 + t]), f"{tag}musq{t}")
                        P.g("vector", [f"ssq{t}", f"musq{t}"], [f"veps{t}"],
                            lambda e, t=t, eps=eps: e.tensor_scalar(
                                stat3[:, 32 + t:33 + t], stat[:, 16 + t:17 + t],
                                1.0 / D, eps, ALU.mult, ALU.add), f"{tag}v1{t}")
                        P.g("vector", [f"veps{t}", f"musq{t}"], [f"veps{t}"],
                            lambda e, t=t: e.tensor_sub(
                                stat3[:, 32 + t:33 + t], stat3[:, 32 + t:33 + t],
                                stat2[:, 32 + t:33 + t]), f"{tag}v2{t}")
                    c0 = 32 + (quad[0] % 16)
                    nr_rsqrt(stat3[:, c0:c0 + len(quad)], len(quad),
                             [f"veps{t}" for t in quad])
                    for t in quad:
                        apply_fn(t)

            def transpose_tile(tag, src_ap, src_res, dst_fn, gcol=None, bcol=None,
                               gres=[]):
                def tr(e, src_ap=src_ap):
                    last = None
                    for c in range(NCH):
                        last = e.transpose(pT2[:, c * 128:(c + 1) * 128],
                                           src_ap[:, c * 128:(c + 1) * 128],
                                           id_bf[:, :])
                    return last
                P.g("tensor", [src_res, "consts"], ["pT2"], tr, f"{tag}tr")
                if gcol is None:
                    def ev(e):
                        dst_ap, _ = dst_fn(None)
                        return e.tensor_copy(
                            dst_ap, pT2[:, :].rearrange("p (c x) -> p c x", x=128))
                    _, dres = dst_fn(None)
                    P.g("vector", ["pT2"], dres, ev, f"{tag}ev")
                else:
                    for c in range(NCH):
                        dst_ap, dres = dst_fn(c)
                        P.g("vector", ["pT2"] + gres, dres,
                            lambda e, c=c, dst_ap=dst_ap: e.tensor_scalar(
                                dst_ap, pT2[:, c * 128:(c + 1) * 128],
                                gcol[:, c:c + 1], bcol[:, c:c + 1],
                                ALU.mult, ALU.add), f"{tag}ev{c}")

            def emit_ffn(tag):
                sF = sR if True else None
                for wname, wsrc in (("wr_res", wrT), ("wk_res", wk2T),
                                    ("wv_res", wvT)):
                    P.dma("gpsimd", "wa", [], [wname],
                          lambda e, wsrc=wsrc, wname=wname: [e.dma_start(
                              ffn_w[wname][:, :].rearrange("p (c f) -> p c f", c=NCH),
                              wsrc.rearrange("(c p) f -> p c f", p=128))],
                          note=f"{tag}{wname}")
                for mname, msrc in (("murc", mu_r_c), ("mukc", mu_k_c)):
                    load_cols(f"{tag}{mname}", msrc, 8,
                              mucols[mname][:, :], mname)
                    P.g("vector", [mname], [mname + "o"],
                        lambda e, mname=mname: e.tensor_scalar(
                            mucols[mname + "o"][:, :], mucols[mname][:, :],
                            -1.0, 1.0, ALU.mult, ALU.add), f"{tag}{mname}o")
                def zf(e):
                    last = None
                    for c in range(NCH):
                        last = e.memset(yfm[:, c * 2049:c * 2049 + 1], 0.0)
                    return last
                P.g("gpsimd", [], ["yfm0"], zf, f"{tag}zf")

                yv = yfm.rearrange("p (c x) -> p c x", x=2049)

                def apply_y(t):
                    P.g("vector", [zr(t), f"mu{t}", f"veps{t}"], ["htm2"],
                        lambda e, t=t: e.tensor_scalar(
                            htm2[:, :], zt(t), stat[:, 32 + t:33 + t],
                            stat3[:, 32 + t:33 + t], ALU.subtract, ALU.mult),
                        f"{tag}lna{t}")
                    transpose_tile(
                        f"{tag}y{t}", htm2, "htm2",
                        lambda c, t=t: (yv[:, :, 1 + t * 128: 1 + (t + 1) * 128],
                                        ["yfm"]))
                ln_ln(f"{tag}ln", list(range(NT)), lambda t: (zt(t), zr(t)),
                      EPS_OUT, apply_y)

                for g in range(4):
                    for c in range(NCH):
                        ya = yv[:, c, 1 + g * 512: 1 + (g + 1) * 512]
                        yp = yv[:, c, g * 512: (g + 1) * 512]
                        P.g("vector", ["yfm", "yfm0"], [f"dl{c}"],
                            lambda e, ya=ya, yp=yp, c=c: e.tensor_sub(
                                dscr[:, c * 512:(c + 1) * 512], yp, ya),
                            f"{tag}dl{g}.{c}")
                        P.g("vector", [f"dl{c}", "murco", "yfm"], [f"xr{c}"],
                            lambda e, ya=ya, c=c: e.scalar_tensor_tensor(
                                xrb[:, c * 512:(c + 1) * 512],
                                dscr[:, c * 512:(c + 1) * 512],
                                mucols["murco"][:, c:c + 1], ya,
                                ALU.mult, ALU.add), f"{tag}xr{g}.{c}")
                        P.g("vector", [f"dl{c}", "mukco", "yfm"], [f"xk{c}"],
                            lambda e, ya=ya, c=c: e.scalar_tensor_tensor(
                                xkb[:, c * 512:(c + 1) * 512],
                                dscr[:, c * 512:(c + 1) * 512],
                                mucols["mukco"][:, c:c + 1], ya,
                                ALU.mult, ALU.add), f"{tag}xk{g}.{c}")
                    for fc in range(NCH):
                        ppa, prs = pv(pY0, (fc % 4) * 512, (fc % 4) * 512 + 512)
                        def kmm(e, fc=fc, ppa=ppa):
                            last = None
                            for c in range(NCH):
                                last = e.matmul(
                                    ppa, ffn_w["wk_res"][:, c * 1024 + fc * 128:
                                                         c * 1024 + (fc + 1) * 128],
                                    xkb[:, c * 512:(c + 1) * 512],
                                    start=(c == 0), stop=(c == NCH - 1))
                            return last
                        P.g("tensor", ["wk_res"] + [f"xk{c}" for c in range(NCH)],
                            prs, kmm, f"{tag}kp{g}.{fc}")
                        P.g("vector", prs, ["cscr"],
                            lambda e, ppa=ppa: e.tensor_scalar_max(
                                cscr[:, 0:512], ppa, 0.0), f"{tag}krl{g}.{fc}")
                        P.g("scalar", ["cscr"], [f"ka{fc}"],
                            lambda e, fc=fc: e.activation(
                                kab[:, fc * 512:(fc + 1) * 512], cscr[:, 0:512],
                                AF.Square), f"{tag}ksq{g}.{fc}")
                    for tr_ in range(4):
                        t = g * 4 + tr_
                        ppr, rrs = pv(pY1, 0, 1024)
                        def rmm(e, tr_=tr_, ppr=ppr):
                            last = None
                            for n in range(2):
                                for c in range(NCH):
                                    last = e.matmul(
                                        ppr[:, n * 512:(n + 1) * 512],
                                        xrb[:, c * 512 + tr_ * 128:
                                            c * 512 + tr_ * 128 + 128],
                                        ffn_w["wr_res"][:, c * 1024 + n * 512:
                                                        c * 1024 + (n + 1) * 512],
                                        start=(c == 0), stop=(c == NCH - 1))
                            return last
                        P.g("tensor", ["wr_res"] + [f"xr{c}" for c in range(NCH)],
                            rrs, rmm, f"{tag}rp{t}")
                        P.g("scalar", rrs, ["rtm"],
                            lambda e, ppr=ppr: e.activation(
                                rtm[:, :], ppr, AF.Sigmoid), f"{tag}sig{t}")
                        ppv, vrs = pv(pY0, 0, 1024)
                        def vmm(e, tr_=tr_, ppv=ppv):
                            last = None
                            for n in range(2):
                                for c in range(NCH):
                                    last = e.matmul(
                                        ppv[:, n * 512:(n + 1) * 512],
                                        kab[:, c * 512 + tr_ * 128:
                                            c * 512 + tr_ * 128 + 128],
                                        ffn_w["wv_res"][:, c * 1024 + n * 512:
                                                        c * 1024 + (n + 1) * 512],
                                        start=(c == 0), stop=(c == NCH - 1))
                            return last
                        P.g("tensor", ["wv_res"] + [f"ka{c}" for c in range(NCH)],
                            vrs, vmm, f"{tag}vp{t}")
                        P.g("vector", vrs + ["rtm"], ["cscr"],
                            lambda e, ppv=ppv: e.scalar_tensor_tensor(
                                cscr[:, :], ppv, 1.0, rtm[:, :],
                                ALU.mult, ALU.mult), f"{tag}cmb{t}")
                        P.g("gpsimd", ["cscr", zr(t)], [zr(t)],
                            lambda e, t=t: e.tensor_add(zt(t), zt(t), cscr[:, :]),
                            f"{tag}zadd{t}")

            def alloc_ffn_bufs(sx, sfx):
                sbX = lambda n, shape, dt: sx.enter_context(
                    nc.sbuf_tensor(n + sfx, shape, dt))
                fw = {n: sbX(n, [128, NCH * 1024], BF16)
                      for n in ("wr_res", "wk_res", "wv_res")}
                mc = {n: sbX(n, [128, 8], F32)
                      for n in ("murc", "mukc", "murco", "mukco")}
                return (fw, mc, sbX("yfm", [128, NCH * 2049], BF16),
                        sbX("dscr", [128, NCH * 512], BF16),
                        sbX("xrb", [128, NCH * 512], BF16),
                        sbX("xkb", [128, NCH * 512], BF16),
                        sbX("kab", [128, NCH * 512], BF16),
                        sbX("rtm", [128, D], F32))

            sF1 = ExitStack()
            sF1.__enter__()
            (ffn_w, mucols, yfm, dscr, xrb, xkb, kab, rtm) = alloc_ffn_bufs(sF1, "")

            if skip_f1:
                for t in range(NT):
                    P.g("gpsimd", [], [zr(t)],
                        lambda e, t=t: e.memset(zt(t), 0.01 * (t + 1)), f"zm{t}")
            if skip_early and not skip_f1:
                for t in range(NT):
                    P.g("gpsimd", [], [zr(t)],
                        lambda e, t=t: e.memset(zt(t), 0.01 * (t + 1)), f"zm{t}")
            for t in range(NT if not skip_early else 0):
                P.dma("sync", "xin", [], ["cscr"],
                      lambda e, t=t: [e.dma_start(cscr[:, :],
                                                  x_in[t * 128:(t + 1) * 128, :])],
                      note=f"zx{t}")
                P.dma("sync", "ao", [f"aod{t}"], ["htm2"],
                      lambda e, t=t: [e.dma_start(
                          htm2[:, :], aout_d.ap()[t * 128:(t + 1) * 128, :])],
                      note=f"za{t}")
                P.g("vector", ["cscr", "htm2"], [zr(t)],
                    lambda e, t=t: e.tensor_add(zt(t), cscr[:, :], htm2[:, :]),
                    f"zinit{t}")

            if not skip_f1:
                emit_ffn("f1")

            for t in range(NT if not skip_early else 0):
                P.dma("sync", "x2", [zr(t)], [f"x2d{t}"],
                      lambda e, t=t: [e.dma_start(x2_d.ap()[t * 128:(t + 1) * 128, :],
                                                  zt(t))], note=f"x2w{t}")

            if debug_stop == "x2":
                for t in range(NT):
                    P.dma("sync", "out", [zr(t)], [f"out{t}"],
                          lambda e, t=t: [e.dma_start(out[t * 128:(t + 1) * 128, :],
                                                      zt(t))])
                P.fence()
                with nc.Block() as block:
                    P.flush(block)
                sF1.__exit__(None, None, None)
                return nc

            def apply_lnb(t):
                P.g("vector", [zr(t), f"mu{t}", f"veps{t}"], [zr(t)],
                    lambda e, t=t: e.tensor_scalar(
                        zt(t), zt(t), stat[:, 32 + t:33 + t],
                        stat3[:, 32 + t:33 + t], ALU.subtract, ALU.mult),
                    f"lnb{t}")
            if not skip_f1:
                ln_ln("lnb", list(range(NT)), lambda t: (zt(t), zr(t)), EPS_OUT,
                      apply_lnb)

            P.fence()
            with nc.Block() as block:
                P.flush(block)
            sF1.__exit__(None, None, None)

            sB2 = ExitStack()
            sB2.__enter__()
            sbB = lambda n, shape, dt: sB2.enter_context(
                nc.sbuf_tensor(n, shape, dt))
            hfmt = [sbB(f"hfmt{i}", [128, NCH * 128], BF16) for i in range(2)]
            qs_scr = sbB("qs_scr", [128, 4 * ADIM], F32)
            ks_scr = sbB("ks_scr", [128, 4 * ADIM], F32)
            ds_scr = sbB("ds_scr", [128, ADIM], F32)
            A_bf = sbB("A_bf", [128, NT], BF16)
            qn = sbB("qn", [128, NT * ADIM], BF16)
            kn = sbB("kn", [128, NT * ADIM], BF16)
            kfmt = sbB("kfmt", [128, ADIM], BF16)
            otb = sbB("otb", [128, ADIM], BF16)
            ofmt = sbB("ofmt", [128, ADIM], BF16)
            wq_l = [sbB("wq_l0", [128, NCH * ADIM], BF16)]
            wk_l = [sbB("wk_l0", [128, NCH * ADIM], BF16)]
            wp_l = [sbB("wp_l0", [128, 2 * ADIM], BF16)]
            wf_l = [sbB("wf_l0", [128, 2 * D], BF16)]
            brow = [sbB("brow0", [1, 5 * ADIM + 2 * D], BF16)]
            b1row = [sbB("b1row0", [1, HID], BF16)]
            gbcol = [sbB("gbcol0", [128, 32], F32)]
            wgb = [sbB("wgb0", [128, ADIM], F32)]
            wp_s = sbB("wp_s", [128, 2 * ADIM], BF16)
            gcols = sbB("gcols", [128, 2], F32)
            h2fm = sbB("h2fm", [128, NCH * 512], BF16)
            gfm = sbB("gfm", [128, 32 * 512], BF16)
            w1buf = [sbB(f"w1buf{i}", [128, NCH * 512], BF16) for i in range(2)]
            w2buf = [sbB(f"w2buf{i}", [128, 4 * 512], BF16) for i in range(2)]

            def load_layer_weights(l):
                p = 0
                P.dma("gpsimd", "wb", [], [f"wq{p}"],
                      lambda e, l=l, p=p: [e.dma_start(
                          wq_l[p][:, :].rearrange("p (c f) -> p c f", c=NCH),
                          Wq[l].rearrange("(c p) f -> p c f", p=128))],
                      note=f"Lw q{l}")
                P.dma("gpsimd", "wb", [], [f"wk{p}"],
                      lambda e, l=l, p=p: [e.dma_start(
                          wk_l[p][:, :].rearrange("p (c f) -> p c f", c=NCH),
                          Wk[l].rearrange("(c p) f -> p c f", p=128))],
                      note=f"Lw k{l}")
                P.dma("gpsimd", "wb", [], [f"wpb{p}"],
                      lambda e, l=l, p=p: [e.dma_start(
                          wp_l[p][:, :].rearrange("p (c f) -> p c f", c=2),
                          Wp[l].rearrange("(c p) f -> p c f", p=128))],
                      note=f"Lw p{l}")
                P.dma("gpsimd", "wb", [], [f"wfb{p}"],
                      lambda e, l=l, p=p: [e.dma_start(
                          wf_l[p][:, :].rearrange("p (c f) -> p c f", c=2),
                          Wf[l].rearrange("(c p) f -> p c f", p=128))],
                      note=f"Lw f{l}")
                P.dma("gpsimd", "wb", [], [f"brow{p}"],
                      lambda e, l=l, p=p: [
                          e.dma_start(brow[p][0:1, 0:ADIM], bq[l:l + 1, :]),
                          e.dma_start(brow[p][0:1, ADIM:2 * ADIM], bk[l:l + 1, :]),
                          e.dma_start(brow[p][0:1, 2 * ADIM:3 * ADIM], bp[l:l + 1, :]),
                          e.dma_start(brow[p][0:1, 3 * ADIM:3 * ADIM + D],
                                      bf_[l:l + 1, :]),
                          e.dma_start(brow[p][0:1, 3 * ADIM + D:3 * ADIM + 2 * D],
                                      b2[l:l + 1, :])],
                      n=5, note=f"Lw b{l}")
                P.dma("gpsimd", "wb", [], [f"b1row{p}"],
                      lambda e, l=l, p=p: [e.dma_start(b1row[p][0:1, :],
                                                       b1[l:l + 1, :])],
                      note=f"Lw b1{l}")
                P.dma("gpsimd", "wb", [], ["rowscr"],
                      lambda e, l=l, p=p: [e.dma_start(rowscr[0:1, 0:ADIM],
                                                       w_g[l:l + 1, :])],
                      note=f"Lw wg{l}")
                P.g("vector", ["rowscr"], ["rowbf"],
                    lambda e: e.tensor_copy(rowbf[0:1, 0:ADIM], rowscr[0:1, 0:ADIM]),
                    f"wgbf{l}")
                P.g("tensor", ["rowbf", "consts"], ["pM2:b0"],
                    lambda e: e.matmul(pM2[:, 0:ADIM], ones_row[0:1, 0:128],
                                       rowbf[0:1, 0:ADIM], start=True, stop=True),
                    f"wgmm{l}")
                P.g("vector", ["pM2:b0"], [f"wgb{p}"],
                    lambda e, p=p: e.tensor_copy(wgb[p][:, :], pM2[:, 0:ADIM]),
                    f"wgev{l}")
                for i, (vec, cc) in enumerate(((ln1_g_c, 0), (ln1_b_c, 8),
                                               (ln2_g_c, 16), (ln2_b_c, 24))):
                    load_cols(f"Lcol{l}.{i}", vec[l], 8,
                              gbcol[p][:, cc:cc + 8], f"gb{p}_{cc}")

            load_layer_weights(0)

            def emit_attnb_layer(l):
                p = 0
                tag = f"B{l}"
                g1c = gbcol[p][:, 0:8]
                b1c = gbcol[p][:, 8:16]
                g2c = gbcol[p][:, 16:24]
                b2c = gbcol[p][:, 24:32]
                gres = [f"gb{p}_0", f"gb{p}_8", f"gb{p}_16", f"gb{p}_24"]

                if debug_stop in ("gbonly", "gbf1", "gbfull"):
                    P.dma("gpsimd", "dbg",
                          [f"gb{p}_0", f"gb{p}_8", f"gb{p}_16", f"gb{p}_24"],
                          ["outd"],
                          lambda e: [e.dma_start(out[768:896, 0:32],
                                                 gbcol[p][:, :])], n=1)
                    P.fence()
                    return True

                qk_slots = [(pY0, 0), (pY0, 512), (pY0, 1024), (pY0, 1536),
                            (pY1, 0), (pY1, 512)]
                def apply_h(t):
                    P.g("vector", [zr(t), f"mu{t}", f"veps{t}"], ["htm2"],
                        lambda e, t=t: e.tensor_scalar(
                            htm2[:, :], zt(t), stat[:, 32 + t:33 + t],
                            stat3[:, 32 + t:33 + t], ALU.subtract, ALU.mult),
                        f"{tag}h{t}")
                    hb = hfmt[t % 2]
                    transpose_tile(f"{tag}ht{t}", htm2, "htm2",
                                   lambda c, t=t: (
                                       hfmt[t % 2][:, c * 128:(c + 1) * 128],
                                       [f"hfmt{t % 2}"]),
                                   gcol=g1c, bcol=b1c, gres=gres)
                    sq = qk_slots[(2 * t) % 6]
                    sk = qk_slots[(2 * t + 1) % 6]
                    qpa, qrs = pv(sq[0], sq[1], sq[1] + ADIM)
                    kpa, krs2 = pv(sk[0], sk[1], sk[1] + ADIM)
                    for nm, wl, ppx, rsx, wres in (
                            ("q", wq_l[p], qpa, qrs, f"wq{p}"),
                            ("k", wk_l[p], kpa, krs2, f"wk{p}")):
                        def qmm(e, wl=wl, ppx=ppx, t=t, nm=nm, p=p):
                            for c in range(NCH):
                                e.matmul(ppx, hfmt[t % 2][:, c * 128:(c + 1) * 128],
                                         wl[:, c * ADIM:(c + 1) * ADIM],
                                         start=(c == 0), stop=False)
                            boff = 0 if nm == "q" else ADIM
                            return e.matmul(ppx, ones_row[0:1, 0:128],
                                            brow[p][0:1, boff:boff + ADIM]
                                            if nm == "q" else
                                            brow[p][0:1, ADIM:2 * ADIM],
                                            start=False, stop=True)
                        P.g("tensor", [f"hfmt{t % 2}", wres, f"brow{p}", "consts"],
                            rsx, qmm, f"{tag}{nm}mm{t}")
                        scr = qs_scr if nm == "q" else ks_scr
                        scol = (t % 4) * ADIM
                        P.g("scalar", rsx, [f"{nm}s{t % 4}"],
                            lambda e, ppx=ppx, scr=scr, scol=scol: e.activation(
                                scr[:, scol:scol + ADIM], ppx, AF.Identity),
                            f"{tag}{nm}ev{t}")
                        ssqc = 48 + (t % 4) if nm == "q" else 52 + (t % 4)
                        P.g("vector", [f"{nm}s{t % 4}"], [f"{nm}ssq{t % 4}"],
                            lambda e, scr=scr, scol=scol, ssqc=ssqc:
                            e.scalar_tensor_tensor(
                                ds_scr[:, :], scr[:, scol:scol + ADIM], 1.0,
                                scr[:, scol:scol + ADIM], ALU.mult, ALU.mult,
                                accum_out=stat[:, ssqc:ssqc + 1]),
                            f"{tag}{nm}sq{t}")

                def finish_quad(qd):
                    P.g("vector", [f"qssq{i}" for i in range(4)] +
                        [f"kssq{i}" for i in range(4)], ["qkss"],
                        lambda e: e.tensor_scalar_max(stat[:, 48:56],
                                                      stat[:, 48:56], 1e-24),
                        f"{tag}mx{qd}")
                    nr_rsqrt(stat[:, 48:56], 8, ["qkss"])
                    for i in range(4):
                        t = qd * 4 + i
                        for nm, scr, store, sres, ssqc in (
                                ("q", qs_scr, qn, "qn", 48 + i),
                                ("k", ks_scr, kn, "kn", 52 + i)):
                            P.g("vector", ["qkss", f"{nm}s{i}"], [sres],
                                lambda e, scr=scr, i=i, t=t, store=store,
                                       ssqc=ssqc: e.tensor_scalar(
                                    store[:, t * ADIM:(t + 1) * ADIM],
                                    scr[:, i * ADIM:(i + 1) * ADIM],
                                    stat[:, ssqc:ssqc + 1], None, ALU.mult),
                                f"{tag}{nm}n{t}")
                        P.g("vector", ["qn", f"wgb{p}"], [f"Acol{t}"],
                            lambda e, t=t, p=p: e.scalar_tensor_tensor(
                                ds_scr[:, :], qn[:, t * ADIM:(t + 1) * ADIM], 1.0,
                                wgb[p][:, :], ALU.mult, ALU.mult,
                                accum_out=stat2[:, t:t + 1]), f"{tag}A{t}")

                for qd in range(4):
                    ln_ln(f"{tag}l1q{qd}", list(range(qd * 4, qd * 4 + 4)),
                          lambda t: (zt(t), zr(t)), EPS_LN1, apply_h)
                    finish_quad(qd)
                    if debug_stop in (f"s1r_{l}", "gbs1") and qd == 0:
                        P.dma("gpsimd", "dbg",
                              ["qs0", "qs1", "qs2", "qs3", "qkss", "qn", "hfmt0",
                               "gb0_0", "gb0_8", "gb0_16", "gb0_24"],
                              ["outd"],
                              lambda e: [
                                  e.dma_start(out[0:128, :], qs_scr[:, :]),
                                  e.dma_start(out[512:640, 0:8], stat[:, 48:56]),
                                  e.dma_start(out[640:768, :],
                                              hfmt[0][:, :]),
                                  e.dma_start(out[768:896, 0:32],
                                              gbcol[0][:, :])], n=4)
                        P.fence()
                        return True

                if debug_stop == f"s1_{l}":
                    P.dma("gpsimd", "dbg", ["qn", "kn"], ["outd"],
                          lambda e: [
                              e.dma_start(out[0:512, :].rearrange(
                                  "(a p) b -> p a b", p=128),
                                  qn[:, :].rearrange("p (a b) -> p a b", a=4)),
                              e.dma_start(out[512:1024, :].rearrange(
                                  "(a p) b -> p a b", p=128),
                                  kn[:, :].rearrange("p (a b) -> p a b", a=4))],
                          n=2)
                    P.fence()
                    return True

                P.g("vector", [f"Acol{t}" for t in range(NT)], ["A_bf"],
                    lambda e: e.tensor_copy(A_bf[:, :], stat2[:, 0:NT]),
                    f"{tag}Abf")
                def ssA(e):
                    last = None
                    for t in range(NT):
                        last = e.matmul(pM2[0:1, 0:1], A_bf[:, t:t + 1],
                                        A_bf[:, t:t + 1], start=(t == 0),
                                        stop=(t == NT - 1))
                    return last
                P.g("tensor", ["A_bf"], ["pM2:b0"], ssA, f"{tag}ssA")
                P.g("vector", ["pM2:b0"], ["ssa"],
                    lambda e: e.tensor_scalar_max(stat2[0:1, 20:21], pM2[0:1, 0:1],
                                                  1e-24), f"{tag}ssAe")
                nr_rsqrt(stat2[0:1, 20:21], 1, ["ssa"], parts=1)
                for fc in range(2):
                    ppg, grs = pv(pY1, fc * 512, fc * 512 + 1)
                    def gmm(e, fc=fc, ppg=ppg):
                        last = None
                        for t in range(NT):
                            last = e.matmul(
                                ppg, qn[:, t * ADIM + fc * 128:
                                        t * ADIM + (fc + 1) * 128],
                                A_bf[:, t:t + 1], start=(t == 0),
                                stop=(t == NT - 1))
                        return last
                    P.g("tensor", ["qn", "A_bf"], grs, gmm, f"{tag}G{fc}")
                    P.g("vector", grs, [f"Gc{fc}"],
                        lambda e, fc=fc, ppg=ppg: e.tensor_copy(
                            gcols[:, fc:fc + 1], ppg), f"{tag}Gev{fc}")
                P.g("vector", ["ssa"], ["rowbf"],
                    lambda e: e.tensor_copy(rowbf[0:1, 0:1], stat2[0:1, 20:21]),
                    f"{tag}rnbf")
                P.g("tensor", ["rowbf", "consts"], ["pM2:b0"],
                    lambda e: e.matmul(pM2[:, 0:1], ones_row[0:1, 0:128],
                                       rowbf[0:1, 0:1], start=True, stop=True),
                    f"{tag}rnmm")
                P.g("vector", ["pM2:b0"], ["rncol"],
                    lambda e: e.tensor_copy(stat2[:, 21:22], pM2[:, 0:1]),
                    f"{tag}rncol")
                P.g("vector", ["rncol", "Gc0", "Gc1"], ["Gs"],
                    lambda e: e.tensor_scalar(gcols[:, 0:2], gcols[:, 0:2],
                                              stat2[:, 21:22], None, ALU.mult),
                    f"{tag}Gs")
                for c in range(2):
                    P.g("gpsimd", ["Gs", f"wpb{p}"], ["wp_s"],
                        lambda e, c=c: e.tensor_scalar(
                            wp_s[:, c * ADIM:(c + 1) * ADIM],
                            wp_l[p][:, c * ADIM:(c + 1) * ADIM],
                            gcols[:, c:c + 1], None, ALU.mult), f"{tag}wps{c}")

                for t in range(NT):
                    def trk(e, t=t):
                        last = None
                        for c in range(2):
                            last = e.transpose(
                                pT2[:, c * 128:(c + 1) * 128],
                                kn[:, t * ADIM + c * 128: t * ADIM + (c + 1) * 128],
                                id_bf[:, :])
                        return last
                    P.g("tensor", ["kn", "consts"], ["pT2"], trk, f"{tag}ktr{t}")
                    P.g("vector", ["pT2"], ["kfmt"],
                        lambda e: e.tensor_copy(kfmt[:, :], pT2[:, 0:ADIM]),
                        f"{tag}kev{t}")
                    so = qk_slots[t % 6]
                    opa, ors = pv(so[0], so[1], so[1] + ADIM)
                    def omm(e, t=t, opa=opa, p=p):
                        for c in range(2):
                            e.matmul(opa, kfmt[:, c * 128:(c + 1) * 128],
                                     wp_s[:, c * ADIM:(c + 1) * ADIM],
                                     start=(c == 0), stop=False)
                        e.matmul(opa, ones_row[0:1, 0:128],
                                 brow[p][0:1, 2 * ADIM:3 * ADIM], start=False,
                                 stop=False)
                        return e.matmul(opa, id_bf[:, :],
                                        qn[:, t * ADIM:(t + 1) * ADIM],
                                        start=False, stop=True)
                    P.g("tensor", ["kfmt", "wp_s", f"brow{p}", "qn", "consts"],
                        ors, omm, f"{tag}omm{t}")
                    P.g("vector", ors, ["otb"],
                        lambda e, opa=opa: e.tensor_copy(otb[:, :], opa),
                        f"{tag}oev{t}")
                    def tro(e):
                        last = None
                        for c in range(2):
                            last = e.transpose(pT2[:, c * 128:(c + 1) * 128],
                                               otb[:, c * 128:(c + 1) * 128],
                                               id_bf[:, :])
                        return last
                    P.g("tensor", ["otb", "consts"], ["pT2"], tro, f"{tag}otr{t}")
                    P.g("vector", ["pT2"], ["ofmt"],
                        lambda e: e.tensor_copy(ofmt[:, :], pT2[:, 0:ADIM]),
                        f"{tag}oevt{t}")
                    apa, ars = pv(pY1, 0, 1024) if t % 2 == 0 else pv(pY0, 1024, 2048)
                    def amm(e, apa=apa, p=p):
                        last = None
                        for n in range(2):
                            for c in range(2):
                                e.matmul(apa[:, n * 512:(n + 1) * 512],
                                         ofmt[:, c * 128:(c + 1) * 128],
                                         wf_l[p][:, c * D + n * 512:
                                                 c * D + (n + 1) * 512],
                                         start=(c == 0), stop=False)
                            last = e.matmul(
                                apa[:, n * 512:(n + 1) * 512],
                                ones_row[0:1, 0:128],
                                brow[p][0:1, 3 * ADIM + n * 512:
                                        3 * ADIM + (n + 1) * 512],
                                start=False, stop=True)
                        return last
                    P.g("tensor", ["ofmt", f"wfb{p}", f"brow{p}", "consts"],
                        ars, amm, f"{tag}fmm{t}")
                    P.g("vector", ars + [zr(t)], [zr(t)],
                        lambda e, t=t, apa=apa: e.tensor_add(zt(t), zt(t), apa),
                        f"{tag}zadd{t}")

                if debug_stop == f"s3_{l}":
                    for t in range(NT):
                        P.dma("sync", "out", [zr(t)], [f"out{t}"],
                              lambda e, t=t: [e.dma_start(
                                  out[t * 128:(t + 1) * 128, :], zt(t))])
                    P.fence()
                    return True
                if debug_stop == f"s2_{l}":
                    P.dma("gpsimd", "dbg", ["Gs", "A_bf"], ["outd"],
                          lambda e: [
                              e.dma_start(out[0:128, 0:2], gcols[:, :]),
                              e.dma_start(out[128:256, 0:16], A_bf[:, :])], n=2)
                    P.fence()
                    return True

                for g in range(4):
                    def apply_h2(t):
                        P.g("vector", [zr(t), f"mu{t}", f"veps{t}"], ["htm2"],
                            lambda e, t=t: e.tensor_scalar(
                                htm2[:, :], zt(t), stat[:, 32 + t:33 + t],
                                stat3[:, 32 + t:33 + t], ALU.subtract, ALU.mult),
                            f"{tag}h2{t}")
                        tr_ = t % 4
                        transpose_tile(f"{tag}h2t{t}", htm2, "htm2",
                                       lambda c, tr_=tr_: (
                                           h2fm[:, c * 512 + tr_ * 128:
                                                c * 512 + tr_ * 128 + 128],
                                           ["h2fm"]),
                                       gcol=g2c, bcol=b2c, gres=gres)
                    ln_ln(f"{tag}l2g{g}", list(range(g * 4, g * 4 + 4)),
                          lambda t: (zt(t), zr(t)), EPS_LN1, apply_h2)

                    rounds = []
                    fc0 = 0
                    tog = 0
                    while fc0 < 32:
                        sz = 4 if tog == 0 else 2
                        sz = min(sz, 32 - fc0)
                        rounds.append((fc0, sz, tog))
                        fc0 += sz
                        tog ^= 1
                    for (fc0, sz, tog) in rounds:
                        for fc in range(fc0, fc0 + sz):
                            sl = fc // 4
                            if fc % 4 == 0:
                                P.dma("gpsimd", f"w1{sl % 2}", [f"w1bf{l}"],
                                      [f"w1s{sl % 2}"],
                                      lambda e, l=l, sl=sl: [e.dma_start(
                                          w1buf[sl % 2][:, :].rearrange(
                                              "p (c f) -> p c f", c=NCH),
                                          w1bf_d.ap()[l][:, sl * 512:(sl + 1) * 512]
                                          .rearrange("(c p) f -> p c f", p=128))],
                                      note=f"{tag}w1s{g}.{sl}")
                        ppt = pY0 if tog == 0 else pY1
                        for i, fc in enumerate(range(fc0, fc0 + sz)):
                            sl = fc // 4
                            ppa, prs = pv(ppt, i * 512, (i + 1) * 512)
                            def ymm(e, fc=fc, sl=sl, ppa=ppa, p=p):
                                for c in range(NCH):
                                    e.matmul(
                                        ppa,
                                        w1buf[sl % 2][:, c * 512 + (fc % 4) * 128:
                                                      c * 512 + (fc % 4) * 128 + 128],
                                        h2fm[:, c * 512:(c + 1) * 512],
                                        start=(c == 0), stop=False)
                                return e.matmul(
                                    ppa, b1row[p][0:1, fc * 128:(fc + 1) * 128],
                                    ones_row[0:1, 0:512], start=False, stop=True)
                            P.g("tensor", [f"w1s{sl % 2}", "h2fm", f"b1row{p}",
                                           "consts"], prs, ymm, f"{tag}y1.{g}.{fc}")
                        allrs = bank_resources(ppt.name, 0, sz * 512)
                        P.g("scalar", allrs, [f"gfm{fc0}"],
                            lambda e, fc0=fc0, sz=sz, ppt=ppt: e.activation(
                                gfm[:, fc0 * 512:(fc0 + sz) * 512],
                                ppt[:, 0:sz * 512], AF.Gelu), f"{tag}gel{g}.{fc0}")

                    if debug_stop == f"s4g_{l}" and g == 0:
                        P.dma("gpsimd", "dbg",
                              [f"gfm{fc0}" for (fc0, _, _) in rounds] + ["h2fm"],
                              ["outd"],
                              lambda e: [
                                  e.dma_start(out[0:2048, 0:512].rearrange(
                                      "(a p) b -> p a b", p=128),
                                      gfm[:, 0:16 * 512].rearrange(
                                          "p (a b) -> p a b", b=512)),
                                  e.dma_start(out[0:128, 512:1024],
                                              h2fm[:, 0:512])], n=2)
                        P.fence()
                        return True

                    gres_all = [f"gfm{fc0}" for (fc0, _, _) in rounds]
                    for n in range(2):
                        for qtr in range(8):
                            P.dma("gpsimd", f"w2{qtr % 2}", [f"w2bf{l}"],
                                  [f"w2s{qtr % 2}"],
                                  lambda e, l=l, n=n, qtr=qtr: [e.dma_start(
                                      w2buf[qtr % 2][:, :].rearrange(
                                          "p (c f) -> p c f", c=4),
                                      w2bf_d.ap()[l][qtr * 512:(qtr + 1) * 512,
                                                     n * 512:(n + 1) * 512]
                                      .rearrange("(c p) f -> p c f", p=128))],
                                  note=f"{tag}w2s{g}.{n}.{qtr}")
                            for tr_ in range(4):
                                t = g * 4 + tr_
                                ppa, prs = pv(pY0, tr_ * 512, (tr_ + 1) * 512)
                                def y2mm(e, n=n, qtr=qtr, tr_=tr_, ppa=ppa, p=p):
                                    last = None
                                    for kq in range(4):
                                        kc = qtr * 4 + kq
                                        last = e.matmul(
                                            ppa,
                                            gfm[:, kc * 512 + tr_ * 128:
                                                kc * 512 + tr_ * 128 + 128],
                                            w2buf[qtr % 2][:, kq * 512:
                                                           (kq + 1) * 512],
                                            start=(qtr == 0 and kq == 0),
                                            stop=False)
                                    if qtr == 7:
                                        last = e.matmul(
                                            ppa, ones_row[0:1, 0:128],
                                            brow[p][0:1, 3 * ADIM + D + n * 512:
                                                    3 * ADIM + D + (n + 1) * 512],
                                            start=False, stop=True)
                                    return last
                                P.g("tensor",
                                    gres_all + [f"w2s{qtr % 2}", f"brow{p}",
                                                "consts"],
                                    prs, y2mm, f"{tag}y2.{g}.{n}.{qtr}.{tr_}")
                        for tr_ in range(4):
                            t = g * 4 + tr_
                            ppa, prs = pv(pY0, tr_ * 512, (tr_ + 1) * 512)
                            P.g("vector", prs + [zr(t)], [zr(t)],
                                lambda e, t=t, n=n, ppa=ppa: e.tensor_add(
                                    zt(t)[:, n * 512:(n + 1) * 512],
                                    zt(t)[:, n * 512:(n + 1) * 512], ppa),
                                f"{tag}z2{t}.{n}")

                if l + 1 < DEPTH_B:
                    load_layer_weights(l + 1)

                if debug_stop == f"zb{l}":
                    for t in range(NT):
                        P.dma("sync", "out", [zr(t)], [f"out{t}"],
                              lambda e, t=t: [e.dma_start(
                                  out[t * 128:(t + 1) * 128, :], zt(t))])
                    P.fence()
                    return True
                return False

            stopped = False
            for l in range(DEPTH_B):
                if emit_attnb_layer(l):
                    stopped = True
                    break
            with nc.Block() as block:
                P.flush(block)
            sB2.__exit__(None, None, None)
            if stopped:
                return nc

            for t in range(NT):
                P.dma("sync", "xin", [f"x2d{t}"], ["cscr"],
                      lambda e, t=t: [e.dma_start(
                          cscr[:, :], x2_d.ap()[t * 128:(t + 1) * 128, :])],
                      note=f"x3r{t}")
                P.g("vector", ["cscr", zr(t)], [zr(t)],
                    lambda e, t=t: e.tensor_add(zt(t), zt(t), cscr[:, :]),
                    f"x3{t}")

            if debug_stop == "x3":
                for t in range(NT):
                    P.dma("sync", "out", [zr(t)], [f"out{t}"],
                          lambda e, t=t: [e.dma_start(out[t * 128:(t + 1) * 128, :],
                                                      zt(t))])
                P.fence()
                with nc.Block() as block:
                    P.flush(block)
                return nc

            sF2 = ExitStack()
            sF2.__enter__()
            (ffn_w, mucols, yfm, dscr, xrb, xkb, kab, rtm) = alloc_ffn_bufs(sF2, "_2")
            emit_ffn("f2")

            for t in range(NT):
                P.dma("sync", "out", [zr(t)], [f"out{t}"],
                      lambda e, t=t: [e.dma_start(out[t * 128:(t + 1) * 128, :],
                                                  zt(t))], note=f"fin{t}")
            P.fence()
            with nc.Block() as block:
                P.flush(block)
            sF2.__exit__(None, None, None)

    return nc



def kernel(**inputs):
    n = 8
    x = np.ascontiguousarray(inputs["L0_attn_a_x"])
    in_map_common = {
        "wqkvT": np.ascontiguousarray(inputs["L0_attn_a_w_qkv"].T),
        "wkyT": np.ascontiguousarray(inputs["L0_attn_a_w_ky"].T),
        "wprojT": np.ascontiguousarray(inputs["L0_attn_a_w_proj"].T),
        "wrT": np.ascontiguousarray(inputs["L0_ffn_W_r"].T),
        "wk2T": np.ascontiguousarray(inputs["L0_ffn_W_k"].T),
        "wvT": np.ascontiguousarray(inputs["L0_ffn_W_v"].T),
        "mu_r": np.ascontiguousarray(inputs["L0_ffn_mu_r"]),
        "mu_k": np.ascontiguousarray(inputs["L0_ffn_mu_k"]),
        "mu_r_c": np.ascontiguousarray(
            inputs["L0_ffn_mu_r"].reshape(8, 128).T),
        "mu_k_c": np.ascontiguousarray(
            inputs["L0_ffn_mu_k"].reshape(8, 128).T),
        "ln1_g_c": np.ascontiguousarray(
            inputs["L0_attn_b_ln1_g"].reshape(4, 8, 128).transpose(0, 2, 1)),
        "ln1_b_c": np.ascontiguousarray(
            inputs["L0_attn_b_ln1_b"].reshape(4, 8, 128).transpose(0, 2, 1)),
        "ln2_g_c": np.ascontiguousarray(
            inputs["L0_attn_b_ln2_g"].reshape(4, 8, 128).transpose(0, 2, 1)),
        "ln2_b_c": np.ascontiguousarray(
            inputs["L0_attn_b_ln2_b"].reshape(4, 8, 128).transpose(0, 2, 1)),
        "ln1_g": np.ascontiguousarray(inputs["L0_attn_b_ln1_g"]),
        "ln1_b": np.ascontiguousarray(inputs["L0_attn_b_ln1_b"]),
        "Wq": np.ascontiguousarray(inputs["L0_attn_b_Wq"]),
        "bq": np.ascontiguousarray(inputs["L0_attn_b_bq"]),
        "Wk": np.ascontiguousarray(inputs["L0_attn_b_Wk"]),
        "bk": np.ascontiguousarray(inputs["L0_attn_b_bk"]),
        "w_g": np.ascontiguousarray(inputs["L0_attn_b_w_g"][:, :, 0]),
        "Wp": np.ascontiguousarray(inputs["L0_attn_b_Wp"]),
        "bp": np.ascontiguousarray(inputs["L0_attn_b_bp"]),
        "Wf": np.ascontiguousarray(inputs["L0_attn_b_Wf"]),
        "bf": np.ascontiguousarray(inputs["L0_attn_b_bf"]),
        "ln2_g": np.ascontiguousarray(inputs["L0_attn_b_ln2_g"]),
        "ln2_b": np.ascontiguousarray(inputs["L0_attn_b_ln2_b"]),
        "W1": np.ascontiguousarray(inputs["L0_attn_b_W1"]),
        "b1": np.ascontiguousarray(inputs["L0_attn_b_b1"]),
        "W2": np.ascontiguousarray(inputs["L0_attn_b_W2"]),
        "b2": np.ascontiguousarray(inputs["L0_attn_b_b2"]),
    }
    nc = build_program(debug_stop=DEBUG_STOP)
    in_maps = []
    for i in range(n):
        m = dict(in_map_common)
        m["x"] = np.ascontiguousarray(x[i])
        in_maps.append(m)
    res = run_bass_kernel_spmd(nc, in_maps, core_ids=list(range(n)))
    return np.stack([r["out"] for r in res.results], axis=0)


# version 07
# speedup vs baseline: 1.0325x; 1.0052x over previous
import numpy as np
import ml_dtypes
from contextlib import ExitStack

import concourse.bass as bass
import concourse.mybir as mybir
from concourse.bass_utils import run_bass_kernel_spmd

F32 = mybir.dt.float32
BF16 = mybir.dt.bfloat16
I32 = mybir.dt.int32
AF = mybir.ActivationFunctionType
ALU = mybir.AluOpType
AX = mybir.AxisListType

T = 2048
D = 1024
NT = 16
NCH = 8
NH = 16
HD = 64
ADIM = 256
HID = 4096
DEPTH_B = 4
MAGIC = 0x5F3759DF
EPS_OUT = 1e-5
EPS_LN1 = 1e-6
EPS_A = 1e-6
SCALE_A = 0.125

ENGINES = ("tensor", "vector", "scalar", "gpsimd", "sync")

DEBUG_STOP = None


class Prog:

    def __init__(self, nc, ctx):
        self.nc = nc
        self.q = {e: [] for e in ENGINES}
        self.psem = {e: ctx.enter_context(nc.semaphore(f"pg_{e}")) for e in ENGINES}
        self.pcnt = {e: 0 for e in ENGINES}
        self.dsem = {}
        self.dcnt = {}
        self.ctx = ctx
        self.marks = {}
        self.lastwait = {e: {} for e in ENGINES}
        self.ngroups = 0
        self.log = []

    def chan(self, name):
        if name not in self.dsem:
            self.dsem[name] = self.ctx.enter_context(self.nc.semaphore(f"dc_{name}"))
            self.dcnt[name] = 0
        return self.dsem[name]

    def _needs(self, eng, reads, writes):
        need = {}
        def add(mk):
            if mk is None:
                return
            key, hdl, val = mk
            if key in need:
                if val > need[key][1]:
                    need[key] = (hdl, val)
            else:
                need[key] = (hdl, val)
        for r in reads:
            m = self.marks.get(r)
            if m:
                add(m["w"])
        for w in writes:
            m = self.marks.get(w)
            if m:
                add(m["w"])
                for k, (hdl, val) in m["r"].items():
                    add((k, hdl, val))
        out = []
        for key, (hdl, val) in need.items():
            if key == ("e", eng):
                pass
            lw = self.lastwait[eng].get(key, -1)
            if val > lw:
                out.append((key, hdl, val))
        return out

    def _after(self, eng, reads, writes, key, hdl, val):
        for w in writes:
            self.marks[w] = {"w": (key, hdl, val), "r": {}}
        for r in reads:
            m = self.marks.setdefault(r, {"w": None, "r": {}})
            old = m["r"].get(key)
            if old is None or val > old[1]:
                m["r"][key] = (hdl, val)

    def g(self, eng, reads, writes, fn, note=""):
        waits = self._needs(eng, reads, writes)
        for key, hdl, val in waits:
            self.lastwait[eng][key] = val
        self.pcnt[eng] += 1
        myval = self.pcnt[eng]
        mysem = self.psem[eng]
        self._after(eng, reads, writes, ("e", eng), mysem, myval)
        self.ngroups += 1
        self.log.append((eng, note, [(k, v) for k, _, v in waits],
                         ("e", eng, myval), reads, writes))

        def emit(engine, waits=waits, fn=fn, mysem=mysem, note=note):
            for key, hdl, val in waits:
                engine.wait_ge(hdl, val)
            last = fn(engine)
            assert last is not None, f"group fn returned None ({note})"
            last.then_inc(mysem, 1)
        self.q[eng].append(emit)

    def dma(self, eng, channel, reads, writes, fn, n=1, note=""):
        hdl = self.chan(channel)
        waits = self._needs(eng, reads, writes)
        for key, hdl2, val in waits:
            self.lastwait[eng][key] = val
        self.dcnt[channel] += 16 * n
        myval = self.dcnt[channel]
        self._after(eng, reads, writes, ("d", channel), hdl, myval)
        self.ngroups += 1
        self.log.append((eng, note, [(k, v) for k, _, v in waits],
                         ("d", channel, myval), reads, writes))

        def emit(engine, waits=waits, fn=fn, hdl=hdl, note=note):
            for key, hdl2, val in waits:
                engine.wait_ge(hdl2, val)
            insts = fn(engine)
            for i in insts:
                i.then_inc(hdl, 16)
        self.q[eng].append(emit)

    def fence(self, delay=0):
        snap = dict(self.pcnt)
        dsnap = dict(self.dcnt)
        for e in ENGINES:
            items = []
            for e2 in ENGINES:
                if e2 != e and snap[e2] > self.lastwait[e].get(("e", e2), -1):
                    items.append((("e", e2), self.psem[e2], snap[e2]))
            for c, v in dsnap.items():
                if c == "pre":
                    continue
                if v > self.lastwait[e].get(("d", c), -1):
                    items.append((("d", c), self.dsem[c], v))
            for key, hdl, val in items:
                self.lastwait[e][key] = val

            def emit(engine, items=items, delay=delay):
                engine.drain()
                for key, hdl, val in items:
                    engine.wait_ge(hdl, val)
                if delay:
                    engine.nop(cycle_cnt=delay)
                engine.drain()
            self.q[e].append(emit)

    def flush(self, block):
        qs = self.q
        self.q = {e: [] for e in ENGINES}

        @block.tensor
        def _(eng):
            for f in qs["tensor"]:
                f(eng)

        @block.vector
        def _(eng):
            for f in qs["vector"]:
                f(eng)

        @block.scalar
        def _(eng):
            for f in qs["scalar"]:
                f(eng)

        @block.gpsimd
        def _(eng):
            for f in qs["gpsimd"]:
                f(eng)

        @block.sync
        def _(eng):
            for f in qs["sync"]:
                f(eng)


def bank_resources(name, col0, col1, dtype_bytes=4):
    b0 = (col0 * dtype_bytes) // 2048
    b1 = (col1 * dtype_bytes - 1) // 2048
    return [f"{name}:b{i}" for i in range(b0, b1 + 1)]



def build_program(debug_stop=None):
    nc = bass.Bass("TRN2", target_bir_lowering=False)

    x_in = nc.dram_tensor("x", [T, D], F32, kind="ExternalInput").ap()
    wqkvT = nc.dram_tensor("wqkvT", [D, 3 * D], F32, kind="ExternalInput").ap()
    wkyT = nc.dram_tensor("wkyT", [D, D], F32, kind="ExternalInput").ap()
    wprojT = nc.dram_tensor("wprojT", [D, D], F32, kind="ExternalInput").ap()
    wrT = nc.dram_tensor("wrT", [D, D], F32, kind="ExternalInput").ap()
    wk2T = nc.dram_tensor("wk2T", [D, D], F32, kind="ExternalInput").ap()
    wvT = nc.dram_tensor("wvT", [D, D], F32, kind="ExternalInput").ap()
    mu_r = nc.dram_tensor("mu_r", [D], F32, kind="ExternalInput").ap()
    mu_k = nc.dram_tensor("mu_k", [D], F32, kind="ExternalInput").ap()
    mu_r_c = nc.dram_tensor("mu_r_c", [128, 8], F32, kind="ExternalInput").ap()
    mu_k_c = nc.dram_tensor("mu_k_c", [128, 8], F32, kind="ExternalInput").ap()
    ln1_g_c = nc.dram_tensor("ln1_g_c", [DEPTH_B, 128, 8], F32, kind="ExternalInput").ap()
    ln1_b_c = nc.dram_tensor("ln1_b_c", [DEPTH_B, 128, 8], F32, kind="ExternalInput").ap()
    ln2_g_c = nc.dram_tensor("ln2_g_c", [DEPTH_B, 128, 8], F32, kind="ExternalInput").ap()
    ln2_b_c = nc.dram_tensor("ln2_b_c", [DEPTH_B, 128, 8], F32, kind="ExternalInput").ap()
    ln1_g = nc.dram_tensor("ln1_g", [DEPTH_B, D], F32, kind="ExternalInput").ap()
    ln1_b = nc.dram_tensor("ln1_b", [DEPTH_B, D], F32, kind="ExternalInput").ap()
    Wq = nc.dram_tensor("Wq", [DEPTH_B, D, ADIM], F32, kind="ExternalInput").ap()
    bq = nc.dram_tensor("bq", [DEPTH_B, ADIM], F32, kind="ExternalInput").ap()
    Wk = nc.dram_tensor("Wk", [DEPTH_B, D, ADIM], F32, kind="ExternalInput").ap()
    bk = nc.dram_tensor("bk", [DEPTH_B, ADIM], F32, kind="ExternalInput").ap()
    w_g = nc.dram_tensor("w_g", [DEPTH_B, ADIM], F32, kind="ExternalInput").ap()
    Wp = nc.dram_tensor("Wp", [DEPTH_B, ADIM, ADIM], F32, kind="ExternalInput").ap()
    bp = nc.dram_tensor("bp", [DEPTH_B, ADIM], F32, kind="ExternalInput").ap()
    Wf = nc.dram_tensor("Wf", [DEPTH_B, ADIM, D], F32, kind="ExternalInput").ap()
    bf_ = nc.dram_tensor("bf", [DEPTH_B, D], F32, kind="ExternalInput").ap()
    ln2_g = nc.dram_tensor("ln2_g", [DEPTH_B, D], F32, kind="ExternalInput").ap()
    ln2_b = nc.dram_tensor("ln2_b", [DEPTH_B, D], F32, kind="ExternalInput").ap()
    W1 = nc.dram_tensor("W1", [DEPTH_B, D, HID], F32, kind="ExternalInput").ap()
    b1 = nc.dram_tensor("b1", [DEPTH_B, HID], F32, kind="ExternalInput").ap()
    W2 = nc.dram_tensor("W2", [DEPTH_B, HID, D], F32, kind="ExternalInput").ap()
    b2 = nc.dram_tensor("b2", [DEPTH_B, D], F32, kind="ExternalInput").ap()
    out = nc.dram_tensor("out", [T, D], F32, kind="ExternalOutput").ap()

    aout_d = nc.dram_tensor("aout_d", [T, D], BF16)
    x2_d = nc.dram_tensor("x2_d", [T, D], F32)
    w1bf_d = nc.dram_tensor("w1bf_d", [DEPTH_B, D, HID], BF16)
    w2bf_d = nc.dram_tensor("w2bf_d", [DEPTH_B, HID, D], BF16)

    ctx = ExitStack()
    with ctx:
        sb = lambda n, shape, dt: ctx.enter_context(nc.sbuf_tensor(n, shape, dt))
        P = Prog(nc, ctx)

        id_bf = sb("id_bf", [128, 128], BF16)
        id_f32 = sb("id_f32", [128, 128], F32)
        maskneg = sb("maskneg", [128, 128], F32)
        tril01 = sb("tril01", [128, 128], F32)
        ones_row = sb("ones_row", [1, 512], BF16)
        stat = sb("stat", [128, 64], F32)
        stat2 = sb("stat2", [128, 64], F32)
        stat3 = sb("stat3", [128, 64], F32)
        stat4 = sb("stat4", [128, 64], F32)
        stati = sb("stati", [128, 64], I32)

        P.g("gpsimd", [], ["c_idbf"],
            lambda e: e.memset(id_bf[:, :], 1.0), "ms_idbf")
        P.g("gpsimd", ["c_idbf"], ["c_idbf"],
            lambda e: e.affine_select(id_bf[:, :], id_bf[:, :], [[-1, 128]],
                                      ALU.is_equal, 0.0, base=0,
                                      channel_multiplier=1), "as_idbf")
        P.g("gpsimd", [], ["c_idf"],
            lambda e: e.memset(id_f32[:, :], 1.0), "ms_idf")
        P.g("gpsimd", ["c_idf"], ["c_idf"],
            lambda e: e.affine_select(id_f32[:, :], id_f32[:, :], [[-1, 128]],
                                      ALU.is_equal, 0.0, base=0,
                                      channel_multiplier=1), "as_idf")
        P.g("gpsimd", [], ["c_mn"],
            lambda e: e.memset(maskneg[:, :], 0.0), "ms_mn")
        P.g("gpsimd", ["c_mn"], ["c_mn"],
            lambda e: e.affine_select(maskneg[:, :], maskneg[:, :], [[-1, 128]],
                                      ALU.is_ge, -1e30, base=0,
                                      channel_multiplier=1), "as_mn")
        P.g("gpsimd", [], ["c_tril"],
            lambda e: e.memset(tril01[:, :], 1.0), "ms_tril")
        P.g("gpsimd", ["c_tril"], ["c_tril"],
            lambda e: e.affine_select(tril01[:, :], tril01[:, :], [[-1, 128]],
                                      ALU.is_ge, 0.0, base=0,
                                      channel_multiplier=1), "as_tril")
        P.g("gpsimd", ["c_idbf", "c_idf", "c_mn", "c_tril"], ["consts"],
            lambda e: e.memset(ones_row[:, :], 1.0), "ms_ones")

        for l in range(DEPTH_B):
            P.dma("gpsimd", "pre", [], [f"w1bf{l}"],
                  lambda e, l=l: [e.dma_start(w1bf_d.ap()[l], W1[l])],
                  note=f"pre1.{l}")
            P.dma("gpsimd", "pre", [], [f"w2bf{l}"],
                  lambda e, l=l: [e.dma_start(w2bf_d.ap()[l], W2[l])],
                  note=f"pre2.{l}")

        def nr_rsqrt(cols_ap, n, res, eng="gpsimd", parts=128):
            t1 = stat2[0:parts, 56:56 + n]
            t2 = stat3[0:parts, 56:56 + n]
            if not isinstance(res, list):
                res = [res]
            P.g(eng, res, ["nr_t1"],
                lambda e, a=cols_ap: e.tensor_scalar(
                    t1.bitcast(I32), a.bitcast(I32), -0.5, MAGIC, ALU.mult, ALU.add),
                "nr_seed")
            for it in range(4):
                P.g(eng, ["nr_t1"], ["nr_t2"],
                    lambda e: e.tensor_mul(t2, t1, t1), "nr_sq")
                P.g(eng, ["nr_t2"] + res, ["nr_t2"],
                    lambda e, a=cols_ap: e.tensor_mul(t2, t2, a), "nr_mv")
                P.g(eng, ["nr_t2"], ["nr_t2"],
                    lambda e: e.tensor_scalar(t2, t2, -0.5, 1.5,
                                              ALU.mult, ALU.add), "nr_aff")
                P.g(eng, ["nr_t1", "nr_t2"], ["nr_t1"],
                    lambda e: e.tensor_mul(t1, t1, t2), "nr_up")
            P.g(eng, ["nr_t1"], res,
                lambda e, a=cols_ap: e.tensor_copy(a, t1), "nr_out")

        def alias(srcres, dstres_list):
            import copy as _copy
            m = P.marks.get(srcres)
            if m is None:
                return
            for d in dstres_list:
                P.marks[d] = {"w": m["w"], "r": dict(m["r"])}

        skip_early = debug_stop is not None and debug_stop.startswith("gb")
        skip_f1 = skip_early and debug_stop != "gbf1"
        with ExitStack() as sA:
            sbA = lambda n, shape, dt: sA.enter_context(nc.sbuf_tensor(n, shape, dt))
            psA = lambda n, shape, dt: sA.enter_context(nc.psum_tensor(n, shape, dt))

            hfm = sbA("hfm", [128, NCH * T], BF16)
            qfm = sbA("qfm", [128, NCH * T], BF16)
            kfm = sbA("kfm", [128, NCH * T], BF16)
            kyfm = sbA("kyfm", [128, NCH * T], BF16)
            vtm = sbA("vtm", [128, NT * D], BF16)
            xt2 = [sbA(f"xt{i}", [128, D], F32) for i in range(2)]
            htm = sbA("htm", [128, D], BF16)
            wbuf = [sbA(f"wbuf{i}", [128, NCH * 512], BF16) for i in range(2)]
            vscr = sbA("vscr", [128, 512], BF16)
            amscr = sbA("amscr", [128, 1024], F32)
            wdscr = sbA("wdscr", [128, 128], F32)

            WR = (0, 2048)
            WTC = 4096
            ESC = 6144
            AMT = 7168
            AOT = 9216
            wrow_ap = lambda hi, a, b: hfm[:, WR[hi] + a: WR[hi] + b]
            wT_ap = lambda a, b: hfm[:, WTC + a: WTC + b]
            escr_ap = lambda n: hfm[:, ESC: ESC + n]
            amt_ap = lambda a, b: hfm[:, AMT + a: AMT + b]
            aot_ap = lambda a, b: hfm[:, AOT + a: AOT + b]

            pRow = [psA(f"pRow{i}", [128, 1024], F32) for i in range(3)]
            pT = psA("pT_A", [128, 1024], BF16)
            pM = psA("pM_A", [128, 512], F32)

            for t in range(NT if not skip_early else 0):
                xt = xt2[t % 2]
                xr_ = f"xt{t % 2}"
                P.dma("sync", "xin", [], [xr_],
                      lambda e, t=t, xt=xt: [e.dma_start(
                          xt[:, :], x_in[t * 128:(t + 1) * 128, :])],
                      note=f"x{t}")
                P.g("vector", [xr_], [f"sx{t}"],
                    lambda e, t=t, xt=xt: e.reduce_sum(stat[:, t:t + 1], xt[:, :],
                                                       AX.X), f"sx{t}")
                P.g("scalar", [xr_], [f"ssq{t}", "sqscr"],
                    lambda e, t=t, xt=xt: e.activation(
                        amscr[:, :], xt[:, :], AF.Square,
                        accum_out=stat[:, 16 + t:17 + t]), f"ssq{t}")
                P.g("vector", [f"sx{t}"], [f"mu{t}"],
                    lambda e, t=t: e.tensor_scalar(stat[:, 32 + t:33 + t],
                                                   stat[:, t:t + 1], 1.0 / D, None,
                                                   ALU.mult), f"mu{t}")
                P.g("vector", [f"mu{t}"], [f"musq{t}"],
                    lambda e, t=t: e.tensor_mul(stat2[:, 32 + t:33 + t],
                                                stat[:, 32 + t:33 + t],
                                                stat[:, 32 + t:33 + t]), f"musq{t}")
                P.g("vector", [f"ssq{t}", f"musq{t}"], [f"veps{t}"],
                    lambda e, t=t: e.tensor_scalar(stat3[:, 32 + t:33 + t],
                                                   stat[:, 16 + t:17 + t], 1.0 / D,
                                                   EPS_OUT, ALU.mult, ALU.add),
                    f"v1_{t}")
                P.g("vector", [f"veps{t}", f"musq{t}"], [f"veps{t}"],
                    lambda e, t=t: e.tensor_sub(stat3[:, 32 + t:33 + t],
                                                stat3[:, 32 + t:33 + t],
                                                stat2[:, 32 + t:33 + t]), f"v2_{t}")
                P.g("scalar", [f"veps{t}"], [f"lnv{t}"],
                    lambda e, t=t: e.activation(stat2[:, 48 + t:49 + t],
                                                stat3[:, 32 + t:33 + t], AF.Ln),
                    f"lnv{t}")
                P.g("scalar", [f"lnv{t}"], [f"rstd{t}"],
                    lambda e, t=t: e.activation(stat4[:, t:t + 1],
                                                stat2[:, 48 + t:49 + t], AF.Exp,
                                                scale=-0.5), f"rstd{t}")
                P.g("vector", [xr_, f"mu{t}", f"rstd{t}"], ["htm"],
                    lambda e, t=t, xt=xt: e.tensor_scalar(htm[:, :], xt[:, :],
                                                          stat[:, 32 + t:33 + t],
                                                          stat4[:, t:t + 1],
                                                          ALU.subtract, ALU.mult),
                    f"ln{t}")
                def tr(e, t=t):
                    last = None
                    for c in range(NCH):
                        last = e.transpose(pT[:, c * 128:(c + 1) * 128],
                                           htm[:, c * 128:(c + 1) * 128], id_bf[:, :])
                    return last
                P.g("tensor", ["htm", "consts"], ["pT_A"], tr, f"trh{t}")
                def ev(e, t=t):
                    last = None
                    for c in range(NCH):
                        last = e.tensor_copy(hfm[:, c * T + t * 128: c * T + (t + 1) * 128],
                                             pT[:, c * 128:(c + 1) * 128])
                    return last
                P.g("vector", ["pT_A"], ["hfm"], ev, f"evh{t}")

            for s in range(8 if not skip_early else 0):
                wb = wbuf[s % 2]
                if s < 6:
                    src = wqkvT[:, s * 512:(s + 1) * 512]
                else:
                    src = wkyT[:, (s - 6) * 512:(s - 5) * 512]
                P.dma("gpsimd", "wa", [], [f"wbuf{s % 2}"],
                      lambda e, src=src, wb=wb: [e.dma_start(
                          wb[:, :].rearrange("p (c f) -> p c f", c=NCH),
                          src.rearrange("(c p) f -> p c f", p=128))],
                      note=f"wslab{s}")
                is_q = s < 2
                is_v = 4 <= s < 6
                if s < 2:
                    dst, dname = qfm, "qfm"
                elif s < 4:
                    dst, dname = kfm, "kfm"
                elif s < 6:
                    dst, dname = vtm, "vtm"
                else:
                    dst, dname = kyfm, "kyfm"
                for fc in range(4):
                    oc = (s % 2) * 4 + fc if s < 6 else (s - 6) * 4 + fc
                    for tg in range(4):
                        pp = pRow[(fc * 4 + tg) % 3]
                        pcols = (fc % 2) * 512
                        prs = bank_resources(pp.name, pcols, pcols + 512)
                        def mms(e, wb=wb, fc=fc, tg=tg, pp=pp, pcols=pcols):
                            last = None
                            for c in range(NCH):
                                last = e.matmul(
                                    pp[:, pcols:pcols + 512],
                                    wb[:, c * 512 + fc * 128: c * 512 + (fc + 1) * 128],
                                    hfm[:, c * T + tg * 512: c * T + (tg + 1) * 512],
                                    start=(c == 0), stop=(c == NCH - 1))
                            return last
                        P.g("tensor", [f"wbuf{s % 2}", "hfm"], prs, mms,
                            f"qkv{s}.{fc}.{tg}")
                        if not is_v:
                            def ev(e, dst=dst, oc=oc, tg=tg, pp=pp, pcols=pcols,
                                   is_q=is_q):
                                dcol = oc * T + tg * 512
                                if is_q:
                                    return e.tensor_scalar(
                                        dst[:, dcol:dcol + 512], pp[:, pcols:pcols + 512],
                                        SCALE_A, None, ALU.mult)
                                return e.tensor_copy(dst[:, dcol:dcol + 512],
                                                     pp[:, pcols:pcols + 512])
                            P.g("vector", prs, [dname], ev, f"ev{s}.{fc}.{tg}")
                        else:
                            P.g("vector", prs, ["vscr"],
                                lambda e, pp=pp, pcols=pcols: e.tensor_copy(
                                    vscr[:, :], pp[:, pcols:pcols + 512]),
                                f"evv{s}.{fc}.{tg}")
                            def trv(e):
                                last = None
                                for tau in range(4):
                                    last = e.transpose(pT[:, tau * 128:(tau + 1) * 128],
                                                       vscr[:, tau * 128:(tau + 1) * 128],
                                                       id_bf[:, :])
                                return last
                            P.g("tensor", ["vscr", "consts"], ["pT_A"], trv,
                                f"trv{s}.{fc}.{tg}")
                            def evv(e, oc=oc, tg=tg):
                                last = None
                                for tau in range(4):
                                    kt = tg * 4 + tau
                                    last = e.tensor_copy(
                                        vtm[:, kt * D + oc * 128: kt * D + (oc + 1) * 128],
                                        pT[:, tau * 128:(tau + 1) * 128])
                                return last
                            P.g("vector", ["pT_A"], ["vtm"], evv, f"evvt{s}.{fc}.{tg}")

            for fh in range(2 if not skip_early else 0):
                P.dma("gpsimd", "wa", [], [f"wbuf{fh}"],
                      lambda e, fh=fh: [e.dma_start(
                          wbuf[fh][:, :].rearrange("p (c f) -> p c f", c=NCH),
                          wprojT[:, fh * 512:(fh + 1) * 512].rearrange(
                              "(c p) f -> p c f", p=128))],
                      note=f"wproj{fh}")

            if debug_stop in ("hfm", "qfm", "kfm", "kyfm", "vtm"):
                srcs = {"hfm": hfm, "qfm": qfm, "kfm": kfm, "kyfm": kyfm,
                        "vtm": vtm}
                st = srcs[debug_stop]
                P.dma("gpsimd", "dbg", [debug_stop], ["outd"],
                      lambda e, st=st: [e.dma_start(
                          out.rearrange("(a p) b -> p a b", p=128),
                          st[:, :].rearrange("p (a b) -> p a b", a=16))],
                      note="dump")
                P.fence()
                with nc.Block() as block:
                    P.flush(block)
                return nc

            alias("hfm", ["wrow0", "wrow1", "wTsb", "escr", "attnm_t", "aoutt"])

            rot = 0
            for j in range(NT if not skip_early else 0):
                nkc = j + 1
                slabs = [(0, min(nkc, 8))]
                if nkc > 8:
                    slabs.append((8, nkc))
                def emit_pass3(h, j, nkc):
                    hi = h % 2
                    for g0 in range(0, nkc, 8):
                        g1 = min(g0 + 8, nkc)
                        def trw(e, hi=hi, g0=g0, g1=g1):
                            last = None
                            for kc in range(g0, g1):
                                last = e.transpose(
                                    pT[:, (kc - g0) * 128:(kc - g0 + 1) * 128],
                                    wrow_ap(hi, kc * 128, (kc + 1) * 128),
                                    id_bf[:, :])
                            return last
                        P.g("tensor", [f"wrow{hi}", "consts"], ["pT_A"], trw,
                            f"trw{j}.{h}.{g0}")
                        P.g("vector", ["pT_A"], ["wTsb"],
                            lambda e, g0=g0, g1=g1: e.tensor_copy(
                                wT_ap(g0 * 128, g1 * 128),
                                pT[:, 0:(g1 - g0) * 128]), f"evw{j}.{h}.{g0}")
                    def av(e, h=h, nkc=nkc):
                        last = None
                        for kc in range(nkc):
                            last = e.matmul(
                                pM[:, 0:64],
                                wT_ap(kc * 128, (kc + 1) * 128),
                                vtm[:, kc * D + h * 64: kc * D + h * 64 + 64],
                                start=(kc == 0), stop=(kc == nkc - 1))
                        return last
                    P.g("tensor", ["wTsb", "vtm"], ["pM_A"], av, f"av{j}.{h}")
                    P.g("vector", ["pM_A", f"rsw{hi}"], ["attnm_t"],
                        lambda e, j=j, h=h, hi=hi: e.tensor_scalar(
                            amt_ap((j % 2) * D + h * 64, (j % 2) * D + h * 64 + 64),
                            pM[:, 0:64], stat4[:, 20 + hi:21 + hi], None,
                            ALU.mult), f"evav{j}.{h}")

                def emit_pe_stage(h):
                    hi = h % 2
                    base = 64 * (h % 2)
                    ch = h // 2
                    st = {"ly": [], "l": []}
                    for si, (c0, c1) in enumerate(slabs):
                        if deep:
                            pp = pRow[0]
                        else:
                            nonlocal_rot = None
                            pp = pRow[rotbox[0] % 3]; rotbox[0] += 1
                        ncol = (c1 - c0) * 128
                        prs = bank_resources(pp.name, 0, ncol)
                        def mms(e, pp=pp, c0=c0, c1=c1, base=base, ch=ch, j=j):
                            last = None
                            ncol = (c1 - c0) * 128
                            qap = qfm[base:base + 64,
                                      ch * T + j * 128: ch * T + (j + 1) * 128]
                            done = 0
                            while done < ncol:
                                w = min(512, ncol - done)
                                last = e.matmul(
                                    pp[:, done:done + w], qap,
                                    kyfm[base:base + 64,
                                         ch * T + c0 * 128 + done:
                                         ch * T + c0 * 128 + done + w],
                                    start=True, stop=True)
                                done += w
                            return last
                        P.g("tensor", ["qfm", "kyfm"], prs, mms, f"ly{j}.{h}.{si}")
                        if c1 == nkc:
                            dc = (nkc - 1 - c0) * 128
                            P.g("vector", prs + ["consts"], prs,
                                lambda e, pp=pp, dc=dc: e.tensor_add(
                                    pp[:, dc:dc + 128], pp[:, dc:dc + 128],
                                    maskneg[:, :]), f"lym{j}.{h}")
                        st["ly"].append((pp, ncol, prs))
                    for si, (c0, c1) in enumerate(slabs):
                        if deep:
                            pp = pRow[1 + (h % 2)]
                        else:
                            pp = pRow[rotbox[0] % 3]; rotbox[0] += 1
                        ncol = (c1 - c0) * 128
                        prs = bank_resources(pp.name, 0, ncol)
                        def mms2(e, pp=pp, c0=c0, c1=c1, base=base, ch=ch, j=j):
                            last = None
                            ncol = (c1 - c0) * 128
                            qap = qfm[base:base + 64,
                                      ch * T + j * 128: ch * T + (j + 1) * 128]
                            done = 0
                            while done < ncol:
                                w = min(512, ncol - done)
                                last = e.matmul(
                                    pp[:, done:done + w], qap,
                                    kfm[base:base + 64,
                                        ch * T + c0 * 128 + done:
                                        ch * T + c0 * 128 + done + w],
                                    start=True, stop=True)
                                done += w
                            return last
                        P.g("tensor", ["qfm", "kfm"], prs, mms2, f"l{j}.{h}.{si}")
                        st["l"].append((pp, c0, c1, ncol, prs))
                    return st

                def emit_exp_stage(h, st):
                    hi = h % 2
                    aggcols = []
                    for si, (pp, ncol, prs) in enumerate(st["ly"]):
                        acol = 48 + 2 * hi + si
                        aggcols.append(acol)
                        P.g("scalar", prs, [f"aggp{2 * hi + si}", "escr"],
                            lambda e, pp=pp, ncol=ncol, acol=acol: e.activation(
                                escr_ap(ncol), pp[:, 0:ncol], AF.Exp,
                                accum_out=stat[:, acol:acol + 1]),
                            f"exp{j}.{h}.{si}")
                    a0 = aggcols[0]
                    a1 = aggcols[1] if len(aggcols) == 2 else None
                    if a1 is not None:
                        P.g("vector",
                            [f"aggp{a0 - 48}", f"aggp{a1 - 48}"], [f"agg{hi}"],
                            lambda e, a0=a0, a1=a1, hi=hi: e.tensor_add(
                                stat2[:, hi:hi + 1], stat[:, a0:a0 + 1],
                                stat[:, a1:a1 + 1]), f"agg{j}.{h}")
                        P.g("vector", [f"agg{hi}"], [f"agg{hi}"],
                            lambda e, hi=hi: e.tensor_scalar(
                                stat2[:, hi:hi + 1], stat2[:, hi:hi + 1], 1.0,
                                EPS_A, ALU.mult, ALU.add), f"agge{j}.{h}")
                    else:
                        P.g("vector", [f"aggp{a0 - 48}"], [f"agg{hi}"],
                            lambda e, a0=a0, hi=hi: e.tensor_scalar(
                                stat2[:, hi:hi + 1], stat[:, a0:a0 + 1], 1.0,
                                EPS_A, ALU.mult, ALU.add), f"agg{j}.{h}")
                    P.g("vector", [f"agg{hi}"], [f"ragg{hi}"],
                        lambda e, hi=hi: e.reciprocal(stat4[:, 16 + hi:17 + hi],
                                                      stat2[:, hi:hi + 1]),
                        f"ragg{j}.{h}")
                    P.g("vector", [f"ragg{hi}"], [f"lnb{hi}"],
                        lambda e, hi=hi: e.tensor_scalar(
                            stat4[:, 18 + hi:19 + hi], stat4[:, 16 + hi:17 + hi],
                            EPS_A, 1.0, ALU.mult, ALU.add), f"lnb{j}.{h}")

                def emit_pass2(h, st):
                    hi = h % 2
                    sumcols = []
                    for (pp, c0, c1, ncol, prs) in st["l"]:
                        P.g("scalar", prs + [f"ragg{hi}"], ["amscr"],
                            lambda e, pp=pp, ncol=ncol, hi=hi: e.activation(
                                amscr[:, 0:ncol], pp[:, 0:ncol], AF.Abs,
                                scale=stat4[:, 16 + hi:17 + hi]),
                            f"abs{j}.{h}")
                        has_diag = (c1 == nkc)
                        full_cols = ncol - 128 if has_diag else ncol
                        if full_cols > 0:
                            acol = 52 + 2 * hi + (0 if c0 == 0 else 1)
                            sumcols.append(acol)
                            P.g("scalar", ["amscr", f"lnb{hi}"],
                                [f"wrow{hi}", f"us{acol}"],
                                lambda e, hi=hi, c0=c0, fc=full_cols,
                                       acol=acol: e.activation(
                                    wrow_ap(hi, c0 * 128, c0 * 128 + fc),
                                    amscr[:, 0:fc], AF.Ln,
                                    bias=stat4[:, 18 + hi:19 + hi],
                                    accum_out=stat[:, acol:acol + 1]),
                                f"lnf{j}.{h}")
                        if has_diag:
                            P.g("scalar", ["amscr", f"lnb{hi}"], ["wdscr"],
                                lambda e, fc=full_cols, hi=hi: e.activation(
                                    wdscr[:, :], amscr[:, fc:fc + 128], AF.Ln,
                                    bias=stat4[:, 18 + hi:19 + hi]),
                                f"lnd{j}.{h}")
                            acol = 56 + hi
                            sumcols.append(acol)
                            P.g("vector", ["wdscr", "consts"],
                                [f"wrow{hi}", f"us{acol}"],
                                lambda e, hi=hi, j=j, acol=acol:
                                e.scalar_tensor_tensor(
                                    wrow_ap(hi, j * 128, (j + 1) * 128),
                                    wdscr[:, :], 1.0, tril01[:, :],
                                    ALU.mult, ALU.mult,
                                    accum_out=stat[:, acol:acol + 1]),
                                f"wd{j}.{h}")
                    rs = [f"us{c}" for c in sumcols]
                    if len(sumcols) == 1:
                        P.g("vector", rs, [f"sw{hi}"],
                            lambda e, sc=sumcols, hi=hi: e.tensor_scalar(
                                stat2[:, 8 + hi:9 + hi], stat[:, sc[0]:sc[0] + 1],
                                1.0, EPS_A, ALU.mult, ALU.add), f"sw{j}.{h}")
                    else:
                        P.g("vector", rs, [f"sw{hi}"],
                            lambda e, sc=sumcols, hi=hi: e.tensor_add(
                                stat2[:, 8 + hi:9 + hi], stat[:, sc[0]:sc[0] + 1],
                                stat[:, sc[1]:sc[1] + 1]), f"swa{j}.{h}")
                        if len(sumcols) == 3:
                            P.g("vector", rs + [f"sw{hi}"], [f"sw{hi}"],
                                lambda e, sc=sumcols, hi=hi: e.tensor_add(
                                    stat2[:, 8 + hi:9 + hi],
                                    stat2[:, 8 + hi:9 + hi],
                                    stat[:, sc[2]:sc[2] + 1]), f"swb{j}.{h}")
                        P.g("vector", [f"sw{hi}"], [f"sw{hi}"],
                            lambda e, hi=hi: e.tensor_scalar(
                                stat2[:, 8 + hi:9 + hi], stat2[:, 8 + hi:9 + hi],
                                1.0, EPS_A, ALU.mult, ALU.add), f"swe{j}.{h}")
                    P.g("vector", [f"sw{hi}"], [f"rsw{hi}"],
                        lambda e, hi=hi: e.reciprocal(stat4[:, 20 + hi:21 + hi],
                                                      stat2[:, 8 + hi:9 + hi]),
                        f"rsw{j}.{h}")

                deep = (nkc <= 8)
                rotbox = [rot]
                if deep:
                    sts = {}
                    for h in range(NH):
                        sts[h] = emit_pe_stage(h)
                        if h >= 2:
                            emit_pass3(h - 2, j, nkc)
                        emit_exp_stage(h, sts[h])
                        if h >= 1:
                            emit_pass2(h - 1, sts[h - 1])
                    emit_pass2(NH - 1, sts[NH - 1])
                    emit_pass3(NH - 2, j, nkc)
                else:
                    for h in range(NH):
                        st = emit_pe_stage(h)
                        if h > 0:
                            emit_pass3(h - 1, j, nkc)
                        emit_exp_stage(h, st)
                        emit_pass2(h, st)
                rot = rotbox[0]
                emit_pass3(NH - 1, j, nkc)
                def tra(e, j=j):
                    last = None
                    for c in range(NCH):
                        last = e.transpose(pT[:, c * 128:(c + 1) * 128],
                                           amt_ap((j % 2) * D + c * 128,
                                                  (j % 2) * D + (c + 1) * 128),
                                           id_bf[:, :])
                    return last
                P.g("tensor", ["attnm_t", "consts"], ["pT_A"], tra, f"tra{j}")
                P.g("vector", ["pT_A"], ["wrow0"],
                    lambda e: e.tensor_copy(wrow_ap(0, 0, 1024), pT[:, :]), f"evam{j}")
                for fh in range(2):
                    pp = pRow[fh]
                    prs = bank_resources(pp.name, 0, 512)
                    def pmm(e, fh=fh, pp=pp):
                        last = None
                        for c in range(NCH):
                            last = e.matmul(
                                pp[:, 0:512],
                                wrow_ap(0, c * 128, (c + 1) * 128),
                                wbuf[fh][:, c * 512:(c + 1) * 512],
                                start=(c == 0), stop=(c == NCH - 1))
                        return last
                    P.g("tensor", ["wrow0", "wbuf0", "wbuf1"], prs, pmm, f"proj{j}.{fh}")
                    P.g("vector", prs, ["aoutt"],
                        lambda e, j=j, fh=fh, pp=pp: e.tensor_copy(
                            aot_ap((j % 2) * D + fh * 512, (j % 2) * D + (fh + 1) * 512),
                            pp[:, 0:512]), f"evp{j}.{fh}")
                P.dma("sync", "ao", ["aoutt"], [f"aod{j}"],
                      lambda e, j=j: [e.dma_start(
                          aout_d.ap()[j * 128:(j + 1) * 128, :],
                          aot_ap((j % 2) * D, (j % 2 + 1) * D))], note=f"ao{j}")

            P.fence()
            with nc.Block() as block:
                P.flush(block)

        with ExitStack() as sR:
            sbR = lambda n, shape, dt: sR.enter_context(nc.sbuf_tensor(n, shape, dt))
            psR = lambda n, shape, dt: sR.enter_context(nc.psum_tensor(n, shape, dt))

            z = sbR("z", [128, NT * D], F32)
            pY0 = psR("pY0", [128, 2048], F32)
            pY1 = psR("pY1", [128, 1024], F32)
            pM2 = psR("pM2", [128, 512], F32)
            pT2 = psR("pT2", [128, 1024], BF16)

            def pv(pt, c0, c1):
                return pt[:, c0:c1], bank_resources(pt.name, c0, c1)

            zt = lambda t: z[:, t * D:(t + 1) * D]
            zr = lambda t: f"z{t}"




            htm2 = sbR("htm2", [128, D], BF16)
            cscr = sbR("cscr", [128, D], F32)
            rowscr = sbR("rowscr", [1, D], F32)
            rowbf = sbR("rowbf", [1, D], BF16)

            def load_cols(tag, col_src, n, dst_ap, dst_res):
                P.dma("sync", "sm", [], [dst_res],
                      lambda e, col_src=col_src, dst_ap=dst_ap: [
                          e.dma_start(dst_ap, col_src)], note=f"{tag}ld")

            def ln_ln(tag, tiles, src_fn, eps, apply_fn):
                for qi in range(0, len(tiles), 4):
                    quad = tiles[qi:qi + 4]
                    for t in quad:
                        ap, res = src_fn(t)
                        P.g("vector", [res], [f"sx{t}"],
                            lambda e, ap=ap, t=t: e.reduce_sum(
                                stat[:, t:t + 1], ap, AX.X), f"{tag}sx{t}")
                        P.g("scalar", [res], [f"ssq{t}", "cscr"],
                            lambda e, ap=ap, t=t: e.activation(
                                cscr[:, :], ap, AF.Square,
                                accum_out=stat[:, 16 + t:17 + t]), f"{tag}ssq{t}")
                        P.g("vector", [f"sx{t}"], [f"mu{t}"],
                            lambda e, t=t: e.tensor_scalar(
                                stat[:, 32 + t:33 + t], stat[:, t:t + 1],
                                1.0 / D, None, ALU.mult), f"{tag}mu{t}")
                        P.g("vector", [f"mu{t}"], [f"musq{t}"],
                            lambda e, t=t: e.tensor_mul(
                                stat2[:, 32 + t:33 + t], stat[:, 32 + t:33 + t],
                                stat[:, 32 + t:33 + t]), f"{tag}musq{t}")
                        P.g("vector", [f"ssq{t}", f"musq{t}"], [f"veps{t}"],
                            lambda e, t=t, eps=eps: e.tensor_scalar(
                                stat3[:, 32 + t:33 + t], stat[:, 16 + t:17 + t],
                                1.0 / D, eps, ALU.mult, ALU.add), f"{tag}v1{t}")
                        P.g("vector", [f"veps{t}", f"musq{t}"], [f"veps{t}"],
                            lambda e, t=t: e.tensor_sub(
                                stat3[:, 32 + t:33 + t], stat3[:, 32 + t:33 + t],
                                stat2[:, 32 + t:33 + t]), f"{tag}v2{t}")
                    c0 = 32 + (quad[0] % 16)
                    nr_rsqrt(stat3[:, c0:c0 + len(quad)], len(quad),
                             [f"veps{t}" for t in quad])
                    for t in quad:
                        apply_fn(t)

            def transpose_tile(tag, src_ap, src_res, dst_fn, gcol=None, bcol=None,
                               gres=[]):
                def tr(e, src_ap=src_ap):
                    last = None
                    for c in range(NCH):
                        last = e.transpose(pT2[:, c * 128:(c + 1) * 128],
                                           src_ap[:, c * 128:(c + 1) * 128],
                                           id_bf[:, :])
                    return last
                P.g("tensor", [src_res, "consts"], ["pT2"], tr, f"{tag}tr")
                if gcol is None:
                    def ev(e):
                        dst_ap, _ = dst_fn(None)
                        return e.tensor_copy(
                            dst_ap, pT2[:, :].rearrange("p (c x) -> p c x", x=128))
                    _, dres = dst_fn(None)
                    P.g("vector", ["pT2"], dres, ev, f"{tag}ev")
                else:
                    for c in range(NCH):
                        dst_ap, dres = dst_fn(c)
                        P.g("vector", ["pT2"] + gres, dres,
                            lambda e, c=c, dst_ap=dst_ap: e.tensor_scalar(
                                dst_ap, pT2[:, c * 128:(c + 1) * 128],
                                gcol[:, c:c + 1], bcol[:, c:c + 1],
                                ALU.mult, ALU.add), f"{tag}ev{c}")

            def emit_ffn(tag):
                sF = sR if True else None
                for wname, wsrc in (("wr_res", wrT), ("wk_res", wk2T),
                                    ("wv_res", wvT)):
                    P.dma("gpsimd", "wa", [], [wname],
                          lambda e, wsrc=wsrc, wname=wname: [e.dma_start(
                              ffn_w[wname][:, :].rearrange("p (c f) -> p c f", c=NCH),
                              wsrc.rearrange("(c p) f -> p c f", p=128))],
                          note=f"{tag}{wname}")
                for mname, msrc in (("murc", mu_r_c), ("mukc", mu_k_c)):
                    load_cols(f"{tag}{mname}", msrc, 8,
                              mucols[mname][:, :], mname)
                    P.g("vector", [mname], [mname + "o"],
                        lambda e, mname=mname: e.tensor_scalar(
                            mucols[mname + "o"][:, :], mucols[mname][:, :],
                            -1.0, 1.0, ALU.mult, ALU.add), f"{tag}{mname}o")
                def zf(e):
                    last = None
                    for c in range(NCH):
                        last = e.memset(yfm[:, c * 2049:c * 2049 + 1], 0.0)
                    return last
                P.g("gpsimd", [], ["yfm0"], zf, f"{tag}zf")

                yv = yfm.rearrange("p (c x) -> p c x", x=2049)

                def apply_y(t):
                    P.g("vector", [zr(t), f"mu{t}", f"veps{t}"], ["htm2"],
                        lambda e, t=t: e.tensor_scalar(
                            htm2[:, :], zt(t), stat[:, 32 + t:33 + t],
                            stat3[:, 32 + t:33 + t], ALU.subtract, ALU.mult),
                        f"{tag}lna{t}")
                    transpose_tile(
                        f"{tag}y{t}", htm2, "htm2",
                        lambda c, t=t: (yv[:, :, 1 + t * 128: 1 + (t + 1) * 128],
                                        ["yfm"]))
                ln_ln(f"{tag}ln", list(range(NT)), lambda t: (zt(t), zr(t)),
                      EPS_OUT, apply_y)

                for g in range(4):
                    for c in range(NCH):
                        ya = yv[:, c, 1 + g * 512: 1 + (g + 1) * 512]
                        yp = yv[:, c, g * 512: (g + 1) * 512]
                        P.g("vector", ["yfm", "yfm0"], [f"dl{c}"],
                            lambda e, ya=ya, yp=yp, c=c: e.tensor_sub(
                                dscr[:, c * 512:(c + 1) * 512], yp, ya),
                            f"{tag}dl{g}.{c}")
                        P.g("vector", [f"dl{c}", "murco", "yfm"], [f"xr{c}"],
                            lambda e, ya=ya, c=c: e.scalar_tensor_tensor(
                                xrb[:, c * 512:(c + 1) * 512],
                                dscr[:, c * 512:(c + 1) * 512],
                                mucols["murco"][:, c:c + 1], ya,
                                ALU.mult, ALU.add), f"{tag}xr{g}.{c}")
                        P.g("vector", [f"dl{c}", "mukco", "yfm"], [f"xk{c}"],
                            lambda e, ya=ya, c=c: e.scalar_tensor_tensor(
                                xkb[:, c * 512:(c + 1) * 512],
                                dscr[:, c * 512:(c + 1) * 512],
                                mucols["mukco"][:, c:c + 1], ya,
                                ALU.mult, ALU.add), f"{tag}xk{g}.{c}")
                    for fc in range(NCH):
                        ppa, prs = pv(pY0, (fc % 4) * 512, (fc % 4) * 512 + 512)
                        def kmm(e, fc=fc, ppa=ppa):
                            last = None
                            for c in range(NCH):
                                last = e.matmul(
                                    ppa, ffn_w["wk_res"][:, c * 1024 + fc * 128:
                                                         c * 1024 + (fc + 1) * 128],
                                    xkb[:, c * 512:(c + 1) * 512],
                                    start=(c == 0), stop=(c == NCH - 1))
                            return last
                        P.g("tensor", ["wk_res"] + [f"xk{c}" for c in range(NCH)],
                            prs, kmm, f"{tag}kp{g}.{fc}")
                        P.g("vector", prs, ["cscr"],
                            lambda e, ppa=ppa: e.tensor_scalar_max(
                                cscr[:, 0:512], ppa, 0.0), f"{tag}krl{g}.{fc}")
                        P.g("scalar", ["cscr"], [f"ka{fc}"],
                            lambda e, fc=fc: e.activation(
                                kab[:, fc * 512:(fc + 1) * 512], cscr[:, 0:512],
                                AF.Square), f"{tag}ksq{g}.{fc}")
                    for tr_ in range(4):
                        t = g * 4 + tr_
                        ppr, rrs = pv(pY1, 0, 1024)
                        def rmm(e, tr_=tr_, ppr=ppr):
                            last = None
                            for n in range(2):
                                for c in range(NCH):
                                    last = e.matmul(
                                        ppr[:, n * 512:(n + 1) * 512],
                                        xrb[:, c * 512 + tr_ * 128:
                                            c * 512 + tr_ * 128 + 128],
                                        ffn_w["wr_res"][:, c * 1024 + n * 512:
                                                        c * 1024 + (n + 1) * 512],
                                        start=(c == 0), stop=(c == NCH - 1))
                            return last
                        P.g("tensor", ["wr_res"] + [f"xr{c}" for c in range(NCH)],
                            rrs, rmm, f"{tag}rp{t}")
                        P.g("scalar", rrs, ["rtm"],
                            lambda e, ppr=ppr: e.activation(
                                rtm[:, :], ppr, AF.Sigmoid), f"{tag}sig{t}")
                        ppv, vrs = pv(pY0, 0, 1024)
                        def vmm(e, tr_=tr_, ppv=ppv):
                            last = None
                            for n in range(2):
                                for c in range(NCH):
                                    last = e.matmul(
                                        ppv[:, n * 512:(n + 1) * 512],
                                        kab[:, c * 512 + tr_ * 128:
                                            c * 512 + tr_ * 128 + 128],
                                        ffn_w["wv_res"][:, c * 1024 + n * 512:
                                                        c * 1024 + (n + 1) * 512],
                                        start=(c == 0), stop=(c == NCH - 1))
                            return last
                        P.g("tensor", ["wv_res"] + [f"ka{c}" for c in range(NCH)],
                            vrs, vmm, f"{tag}vp{t}")
                        P.g("vector", vrs + ["rtm"], ["cscr"],
                            lambda e, ppv=ppv: e.scalar_tensor_tensor(
                                cscr[:, :], ppv, 1.0, rtm[:, :],
                                ALU.mult, ALU.mult), f"{tag}cmb{t}")
                        P.g("gpsimd", ["cscr", zr(t)], [zr(t)],
                            lambda e, t=t: e.tensor_add(zt(t), zt(t), cscr[:, :]),
                            f"{tag}zadd{t}")

            def alloc_ffn_bufs(sx, sfx):
                sbX = lambda n, shape, dt: sx.enter_context(
                    nc.sbuf_tensor(n + sfx, shape, dt))
                fw = {n: sbX(n, [128, NCH * 1024], BF16)
                      for n in ("wr_res", "wk_res", "wv_res")}
                mc = {n: sbX(n, [128, 8], F32)
                      for n in ("murc", "mukc", "murco", "mukco")}
                return (fw, mc, sbX("yfm", [128, NCH * 2049], BF16),
                        sbX("dscr", [128, NCH * 512], BF16),
                        sbX("xrb", [128, NCH * 512], BF16),
                        sbX("xkb", [128, NCH * 512], BF16),
                        sbX("kab", [128, NCH * 512], BF16),
                        sbX("rtm", [128, D], F32))

            sF1 = ExitStack()
            sF1.__enter__()
            (ffn_w, mucols, yfm, dscr, xrb, xkb, kab, rtm) = alloc_ffn_bufs(sF1, "")

            if skip_f1:
                for t in range(NT):
                    P.g("gpsimd", [], [zr(t)],
                        lambda e, t=t: e.memset(zt(t), 0.01 * (t + 1)), f"zm{t}")
            if skip_early and not skip_f1:
                for t in range(NT):
                    P.g("gpsimd", [], [zr(t)],
                        lambda e, t=t: e.memset(zt(t), 0.01 * (t + 1)), f"zm{t}")
            for t in range(NT if not skip_early else 0):
                P.dma("sync", "xin", [], ["cscr"],
                      lambda e, t=t: [e.dma_start(cscr[:, :],
                                                  x_in[t * 128:(t + 1) * 128, :])],
                      note=f"zx{t}")
                P.dma("sync", "ao", [f"aod{t}"], ["htm2"],
                      lambda e, t=t: [e.dma_start(
                          htm2[:, :], aout_d.ap()[t * 128:(t + 1) * 128, :])],
                      note=f"za{t}")
                P.g("vector", ["cscr", "htm2"], [zr(t)],
                    lambda e, t=t: e.tensor_add(zt(t), cscr[:, :], htm2[:, :]),
                    f"zinit{t}")

            if not skip_f1:
                emit_ffn("f1")

            for t in range(NT if not skip_early else 0):
                P.dma("sync", "x2", [zr(t)], [f"x2d{t}"],
                      lambda e, t=t: [e.dma_start(x2_d.ap()[t * 128:(t + 1) * 128, :],
                                                  zt(t))], note=f"x2w{t}")

            if debug_stop == "x2":
                for t in range(NT):
                    P.dma("sync", "out", [zr(t)], [f"out{t}"],
                          lambda e, t=t: [e.dma_start(out[t * 128:(t + 1) * 128, :],
                                                      zt(t))])
                P.fence()
                with nc.Block() as block:
                    P.flush(block)
                sF1.__exit__(None, None, None)
                return nc

            def apply_lnb(t):
                P.g("vector", [zr(t), f"mu{t}", f"veps{t}"], [zr(t)],
                    lambda e, t=t: e.tensor_scalar(
                        zt(t), zt(t), stat[:, 32 + t:33 + t],
                        stat3[:, 32 + t:33 + t], ALU.subtract, ALU.mult),
                    f"lnb{t}")
            if not skip_f1:
                ln_ln("lnb", list(range(NT)), lambda t: (zt(t), zr(t)), EPS_OUT,
                      apply_lnb)

            P.fence()
            with nc.Block() as block:
                P.flush(block)
            sF1.__exit__(None, None, None)

            sB2 = ExitStack()
            sB2.__enter__()
            sbB = lambda n, shape, dt: sB2.enter_context(
                nc.sbuf_tensor(n, shape, dt))
            hfmt = [sbB(f"hfmt{i}", [128, NCH * 128], BF16) for i in range(2)]
            qs_scr = sbB("qs_scr", [128, 4 * ADIM], F32)
            ks_scr = sbB("ks_scr", [128, 4 * ADIM], F32)
            ds_scr = sbB("ds_scr", [128, ADIM], F32)
            A_bf = sbB("A_bf", [128, NT], BF16)
            qn = sbB("qn", [128, NT * ADIM], BF16)
            kn = sbB("kn", [128, NT * ADIM], BF16)
            kfmt = [sbB(f"kfmt{i}", [128, ADIM], BF16) for i in range(2)]
            otb = [sbB(f"otb{i}", [128, ADIM], BF16) for i in range(2)]
            ofmt = [sbB(f"ofmt{i}", [128, ADIM], BF16) for i in range(2)]
            wq_l = [sbB("wq_l0", [128, NCH * ADIM], BF16)]
            wk_l = [sbB("wk_l0", [128, NCH * ADIM], BF16)]
            wp_l = [sbB("wp_l0", [128, 2 * ADIM], BF16)]
            wf_l = [sbB("wf_l0", [128, 2 * D], BF16)]
            brow = [sbB("brow0", [1, 5 * ADIM + 2 * D], BF16)]
            b1row = [sbB("b1row0", [1, HID], BF16)]
            gbcol = [sbB("gbcol0", [128, 32], F32)]
            wgb = [sbB("wgb0", [128, ADIM], F32)]
            wp_s = sbB("wp_s", [128, 2 * ADIM], BF16)
            gcols = sbB("gcols", [128, 2], F32)
            h2fm = sbB("h2fm", [128, NCH * 512], BF16)
            gfm = sbB("gfm", [128, 32 * 512], BF16)
            w1buf = [sbB(f"w1buf{i}", [128, NCH * 512], BF16) for i in range(2)]
            w2buf = [sbB(f"w2buf{i}", [128, 4 * 512], BF16) for i in range(2)]

            def load_layer_weights(l):
                p = 0
                P.dma("gpsimd", "wb", [], [f"wq{p}"],
                      lambda e, l=l, p=p: [e.dma_start(
                          wq_l[p][:, :].rearrange("p (c f) -> p c f", c=NCH),
                          Wq[l].rearrange("(c p) f -> p c f", p=128))],
                      note=f"Lw q{l}")
                P.dma("gpsimd", "wb", [], [f"wk{p}"],
                      lambda e, l=l, p=p: [e.dma_start(
                          wk_l[p][:, :].rearrange("p (c f) -> p c f", c=NCH),
                          Wk[l].rearrange("(c p) f -> p c f", p=128))],
                      note=f"Lw k{l}")
                P.dma("gpsimd", "wb", [], [f"wpb{p}"],
                      lambda e, l=l, p=p: [e.dma_start(
                          wp_l[p][:, :].rearrange("p (c f) -> p c f", c=2),
                          Wp[l].rearrange("(c p) f -> p c f", p=128))],
                      note=f"Lw p{l}")
                P.dma("gpsimd", "wb", [], [f"wfb{p}"],
                      lambda e, l=l, p=p: [e.dma_start(
                          wf_l[p][:, :].rearrange("p (c f) -> p c f", c=2),
                          Wf[l].rearrange("(c p) f -> p c f", p=128))],
                      note=f"Lw f{l}")
                P.dma("gpsimd", "wb", [], [f"brow{p}"],
                      lambda e, l=l, p=p: [
                          e.dma_start(brow[p][0:1, 0:ADIM], bq[l:l + 1, :]),
                          e.dma_start(brow[p][0:1, ADIM:2 * ADIM], bk[l:l + 1, :]),
                          e.dma_start(brow[p][0:1, 2 * ADIM:3 * ADIM], bp[l:l + 1, :]),
                          e.dma_start(brow[p][0:1, 3 * ADIM:3 * ADIM + D],
                                      bf_[l:l + 1, :]),
                          e.dma_start(brow[p][0:1, 3 * ADIM + D:3 * ADIM + 2 * D],
                                      b2[l:l + 1, :])],
                      n=5, note=f"Lw b{l}")
                P.dma("gpsimd", "wb", [], [f"b1row{p}"],
                      lambda e, l=l, p=p: [e.dma_start(b1row[p][0:1, :],
                                                       b1[l:l + 1, :])],
                      note=f"Lw b1{l}")
                P.dma("gpsimd", "wb", [], ["rowscr"],
                      lambda e, l=l, p=p: [e.dma_start(rowscr[0:1, 0:ADIM],
                                                       w_g[l:l + 1, :])],
                      note=f"Lw wg{l}")
                P.g("vector", ["rowscr"], ["rowbf"],
                    lambda e: e.tensor_copy(rowbf[0:1, 0:ADIM], rowscr[0:1, 0:ADIM]),
                    f"wgbf{l}")
                P.g("tensor", ["rowbf", "consts"], ["pM2:b0"],
                    lambda e: e.matmul(pM2[:, 0:ADIM], ones_row[0:1, 0:128],
                                       rowbf[0:1, 0:ADIM], start=True, stop=True),
                    f"wgmm{l}")
                P.g("vector", ["pM2:b0"], [f"wgb{p}"],
                    lambda e, p=p: e.tensor_copy(wgb[p][:, :], pM2[:, 0:ADIM]),
                    f"wgev{l}")
                for i, (vec, cc) in enumerate(((ln1_g_c, 0), (ln1_b_c, 8),
                                               (ln2_g_c, 16), (ln2_b_c, 24))):
                    load_cols(f"Lcol{l}.{i}", vec[l], 8,
                              gbcol[p][:, cc:cc + 8], f"gb{p}_{cc}")

            load_layer_weights(0)

            def emit_attnb_layer(l):
                p = 0
                tag = f"B{l}"
                g1c = gbcol[p][:, 0:8]
                b1c = gbcol[p][:, 8:16]
                g2c = gbcol[p][:, 16:24]
                b2c = gbcol[p][:, 24:32]
                gres = [f"gb{p}_0", f"gb{p}_8", f"gb{p}_16", f"gb{p}_24"]

                if debug_stop in ("gbonly", "gbf1", "gbfull"):
                    P.dma("gpsimd", "dbg",
                          [f"gb{p}_0", f"gb{p}_8", f"gb{p}_16", f"gb{p}_24"],
                          ["outd"],
                          lambda e: [e.dma_start(out[768:896, 0:32],
                                                 gbcol[p][:, :])], n=1)
                    P.fence()
                    return True

                qk_slots = [(pY0, 0), (pY0, 512), (pY0, 1024), (pY0, 1536),
                            (pY1, 0), (pY1, 512)]
                def apply_h(t):
                    P.g("vector", [zr(t), f"mu{t}", f"veps{t}"], ["htm2"],
                        lambda e, t=t: e.tensor_scalar(
                            htm2[:, :], zt(t), stat[:, 32 + t:33 + t],
                            stat3[:, 32 + t:33 + t], ALU.subtract, ALU.mult),
                        f"{tag}h{t}")
                    hb = hfmt[t % 2]
                    transpose_tile(f"{tag}ht{t}", htm2, "htm2",
                                   lambda c, t=t: (
                                       hfmt[t % 2][:, c * 128:(c + 1) * 128],
                                       [f"hfmt{t % 2}"]),
                                   gcol=g1c, bcol=b1c, gres=gres)
                    sq = qk_slots[(2 * t) % 6]
                    sk = qk_slots[(2 * t + 1) % 6]
                    qpa, qrs = pv(sq[0], sq[1], sq[1] + ADIM)
                    kpa, krs2 = pv(sk[0], sk[1], sk[1] + ADIM)
                    for nm, wl, ppx, rsx, wres in (
                            ("q", wq_l[p], qpa, qrs, f"wq{p}"),
                            ("k", wk_l[p], kpa, krs2, f"wk{p}")):
                        def qmm(e, wl=wl, ppx=ppx, t=t, nm=nm, p=p):
                            for c in range(NCH):
                                e.matmul(ppx, hfmt[t % 2][:, c * 128:(c + 1) * 128],
                                         wl[:, c * ADIM:(c + 1) * ADIM],
                                         start=(c == 0), stop=False)
                            boff = 0 if nm == "q" else ADIM
                            return e.matmul(ppx, ones_row[0:1, 0:128],
                                            brow[p][0:1, boff:boff + ADIM]
                                            if nm == "q" else
                                            brow[p][0:1, ADIM:2 * ADIM],
                                            start=False, stop=True)
                        P.g("tensor", [f"hfmt{t % 2}", wres, f"brow{p}", "consts"],
                            rsx, qmm, f"{tag}{nm}mm{t}")
                        scr = qs_scr if nm == "q" else ks_scr
                        scol = (t % 4) * ADIM
                        P.g("scalar", rsx, [f"{nm}s{t % 4}"],
                            lambda e, ppx=ppx, scr=scr, scol=scol: e.activation(
                                scr[:, scol:scol + ADIM], ppx, AF.Identity),
                            f"{tag}{nm}ev{t}")
                        ssqc = 48 + (t % 4) if nm == "q" else 52 + (t % 4)
                        P.g("vector", [f"{nm}s{t % 4}"], [f"{nm}ssq{t % 4}"],
                            lambda e, scr=scr, scol=scol, ssqc=ssqc:
                            e.scalar_tensor_tensor(
                                ds_scr[:, :], scr[:, scol:scol + ADIM], 1.0,
                                scr[:, scol:scol + ADIM], ALU.mult, ALU.mult,
                                accum_out=stat[:, ssqc:ssqc + 1]),
                            f"{tag}{nm}sq{t}")

                def finish_quad(qd):
                    P.g("vector", [f"qssq{i}" for i in range(4)] +
                        [f"kssq{i}" for i in range(4)], ["qkss"],
                        lambda e: e.tensor_scalar_max(stat[:, 48:56],
                                                      stat[:, 48:56], 1e-24),
                        f"{tag}mx{qd}")
                    nr_rsqrt(stat[:, 48:56], 8, ["qkss"])
                    for i in range(4):
                        t = qd * 4 + i
                        for nm, scr, store, sres, ssqc in (
                                ("q", qs_scr, qn, "qn", 48 + i),
                                ("k", ks_scr, kn, "kn", 52 + i)):
                            P.g("vector", ["qkss", f"{nm}s{i}"], [sres],
                                lambda e, scr=scr, i=i, t=t, store=store,
                                       ssqc=ssqc: e.tensor_scalar(
                                    store[:, t * ADIM:(t + 1) * ADIM],
                                    scr[:, i * ADIM:(i + 1) * ADIM],
                                    stat[:, ssqc:ssqc + 1], None, ALU.mult),
                                f"{tag}{nm}n{t}")
                        P.g("vector", ["qn", f"wgb{p}"], [f"Acol{t}"],
                            lambda e, t=t, p=p: e.scalar_tensor_tensor(
                                ds_scr[:, :], qn[:, t * ADIM:(t + 1) * ADIM], 1.0,
                                wgb[p][:, :], ALU.mult, ALU.mult,
                                accum_out=stat2[:, t:t + 1]), f"{tag}A{t}")

                for qd in range(4):
                    ln_ln(f"{tag}l1q{qd}", list(range(qd * 4, qd * 4 + 4)),
                          lambda t: (zt(t), zr(t)), EPS_LN1, apply_h)
                    finish_quad(qd)
                    if debug_stop in (f"s1r_{l}", "gbs1") and qd == 0:
                        P.dma("gpsimd", "dbg",
                              ["qs0", "qs1", "qs2", "qs3", "qkss", "qn", "hfmt0",
                               "gb0_0", "gb0_8", "gb0_16", "gb0_24"],
                              ["outd"],
                              lambda e: [
                                  e.dma_start(out[0:128, :], qs_scr[:, :]),
                                  e.dma_start(out[512:640, 0:8], stat[:, 48:56]),
                                  e.dma_start(out[640:768, :],
                                              hfmt[0][:, :]),
                                  e.dma_start(out[768:896, 0:32],
                                              gbcol[0][:, :])], n=4)
                        P.fence()
                        return True

                if debug_stop == f"s1_{l}":
                    P.dma("gpsimd", "dbg", ["qn", "kn"], ["outd"],
                          lambda e: [
                              e.dma_start(out[0:512, :].rearrange(
                                  "(a p) b -> p a b", p=128),
                                  qn[:, :].rearrange("p (a b) -> p a b", a=4)),
                              e.dma_start(out[512:1024, :].rearrange(
                                  "(a p) b -> p a b", p=128),
                                  kn[:, :].rearrange("p (a b) -> p a b", a=4))],
                          n=2)
                    P.fence()
                    return True

                P.g("vector", [f"Acol{t}" for t in range(NT)], ["A_bf"],
                    lambda e: e.tensor_copy(A_bf[:, :], stat2[:, 0:NT]),
                    f"{tag}Abf")
                def ssA(e):
                    last = None
                    for t in range(NT):
                        last = e.matmul(pM2[0:1, 0:1], A_bf[:, t:t + 1],
                                        A_bf[:, t:t + 1], start=(t == 0),
                                        stop=(t == NT - 1))
                    return last
                P.g("tensor", ["A_bf"], ["pM2:b0"], ssA, f"{tag}ssA")
                P.g("vector", ["pM2:b0"], ["ssa"],
                    lambda e: e.tensor_scalar_max(stat2[0:1, 20:21], pM2[0:1, 0:1],
                                                  1e-24), f"{tag}ssAe")
                nr_rsqrt(stat2[0:1, 20:21], 1, ["ssa"], parts=1)
                for fc in range(2):
                    ppg, grs = pv(pY1, fc * 512, fc * 512 + 1)
                    def gmm(e, fc=fc, ppg=ppg):
                        last = None
                        for t in range(NT):
                            last = e.matmul(
                                ppg, qn[:, t * ADIM + fc * 128:
                                        t * ADIM + (fc + 1) * 128],
                                A_bf[:, t:t + 1], start=(t == 0),
                                stop=(t == NT - 1))
                        return last
                    P.g("tensor", ["qn", "A_bf"], grs, gmm, f"{tag}G{fc}")
                    P.g("vector", grs, [f"Gc{fc}"],
                        lambda e, fc=fc, ppg=ppg: e.tensor_copy(
                            gcols[:, fc:fc + 1], ppg), f"{tag}Gev{fc}")
                P.g("vector", ["ssa"], ["rowbf"],
                    lambda e: e.tensor_copy(rowbf[0:1, 0:1], stat2[0:1, 20:21]),
                    f"{tag}rnbf")
                P.g("tensor", ["rowbf", "consts"], ["pM2:b0"],
                    lambda e: e.matmul(pM2[:, 0:1], ones_row[0:1, 0:128],
                                       rowbf[0:1, 0:1], start=True, stop=True),
                    f"{tag}rnmm")
                P.g("vector", ["pM2:b0"], ["rncol"],
                    lambda e: e.tensor_copy(stat2[:, 21:22], pM2[:, 0:1]),
                    f"{tag}rncol")
                P.g("vector", ["rncol", "Gc0", "Gc1"], ["Gs"],
                    lambda e: e.tensor_scalar(gcols[:, 0:2], gcols[:, 0:2],
                                              stat2[:, 21:22], None, ALU.mult),
                    f"{tag}Gs")
                for c in range(2):
                    P.g("gpsimd", ["Gs", f"wpb{p}"], ["wp_s"],
                        lambda e, c=c: e.tensor_scalar(
                            wp_s[:, c * ADIM:(c + 1) * ADIM],
                            wp_l[p][:, c * ADIM:(c + 1) * ADIM],
                            gcols[:, c:c + 1], None, ALU.mult), f"{tag}wps{c}")

                def s3_stageA(t):
                    sl = t % 2
                    def trk(e, t=t):
                        last = None
                        for c in range(2):
                            last = e.transpose(
                                pT2[:, c * 128:(c + 1) * 128],
                                kn[:, t * ADIM + c * 128: t * ADIM + (c + 1) * 128],
                                id_bf[:, :])
                        return last
                    P.g("tensor", ["kn", "consts"], ["pT2"], trk, f"{tag}ktr{t}")
                    P.g("vector", ["pT2"], [f"kfmt{sl}"],
                        lambda e, sl=sl: e.tensor_copy(kfmt[sl][:, :],
                                                       pT2[:, 0:ADIM]),
                        f"{tag}kev{t}")
                    so = qk_slots[t % 6]
                    opa, ors = pv(so[0], so[1], so[1] + ADIM)
                    def omm(e, t=t, opa=opa, p=p, sl=sl):
                        for c in range(2):
                            e.matmul(opa, kfmt[sl][:, c * 128:(c + 1) * 128],
                                     wp_s[:, c * ADIM:(c + 1) * ADIM],
                                     start=(c == 0), stop=False)
                        e.matmul(opa, ones_row[0:1, 0:128],
                                 brow[p][0:1, 2 * ADIM:3 * ADIM], start=False,
                                 stop=False)
                        return e.matmul(opa, id_bf[:, :],
                                        qn[:, t * ADIM:(t + 1) * ADIM],
                                        start=False, stop=True)
                    P.g("tensor", [f"kfmt{sl}", "wp_s", f"brow{p}", "qn", "consts"],
                        ors, omm, f"{tag}omm{t}")
                    P.g("vector", ors, [f"otb{sl}"],
                        lambda e, opa=opa, sl=sl: e.tensor_copy(otb[sl][:, :], opa),
                        f"{tag}oev{t}")

                def s3_stageB(t):
                    sl = t % 2
                    def tro(e, sl=sl):
                        last = None
                        for c in range(2):
                            last = e.transpose(pT2[:, c * 128:(c + 1) * 128],
                                               otb[sl][:, c * 128:(c + 1) * 128],
                                               id_bf[:, :])
                        return last
                    P.g("tensor", [f"otb{sl}", "consts"], ["pT2"], tro,
                        f"{tag}otr{t}")
                    P.g("vector", ["pT2"], [f"ofmt{sl}"],
                        lambda e, sl=sl: e.tensor_copy(ofmt[sl][:, :],
                                                       pT2[:, 0:ADIM]),
                        f"{tag}oevt{t}")
                    apa, ars = pv(pY1, 0, 1024) if t % 2 == 0 else pv(pY0, 1024, 2048)
                    def amm(e, apa=apa, p=p, sl=sl):
                        last = None
                        for n in range(2):
                            for c in range(2):
                                e.matmul(apa[:, n * 512:(n + 1) * 512],
                                         ofmt[sl][:, c * 128:(c + 1) * 128],
                                         wf_l[p][:, c * D + n * 512:
                                                 c * D + (n + 1) * 512],
                                         start=(c == 0), stop=False)
                            last = e.matmul(
                                apa[:, n * 512:(n + 1) * 512],
                                ones_row[0:1, 0:128],
                                brow[p][0:1, 3 * ADIM + n * 512:
                                        3 * ADIM + (n + 1) * 512],
                                start=False, stop=True)
                        return last
                    P.g("tensor", [f"ofmt{sl}", f"wfb{p}", f"brow{p}", "consts"],
                        ars, amm, f"{tag}fmm{t}")
                    P.g("vector", ars + [zr(t)], [zr(t)],
                        lambda e, t=t, apa=apa: e.tensor_add(zt(t), zt(t), apa),
                        f"{tag}zadd{t}")

                for t in range(NT):
                    s3_stageA(t)
                    if t > 0:
                        s3_stageB(t - 1)
                s3_stageB(NT - 1)

                for g in range(4):
                    def apply_h2(t):
                        P.g("vector", [zr(t), f"mu{t}", f"veps{t}"], ["htm2"],
                            lambda e, t=t: e.tensor_scalar(
                                htm2[:, :], zt(t), stat[:, 32 + t:33 + t],
                                stat3[:, 32 + t:33 + t], ALU.subtract, ALU.mult),
                            f"{tag}h2{t}")
                        tr_ = t % 4
                        transpose_tile(f"{tag}h2t{t}", htm2, "htm2",
                                       lambda c, tr_=tr_: (
                                           h2fm[:, c * 512 + tr_ * 128:
                                                c * 512 + tr_ * 128 + 128],
                                           ["h2fm"]),
                                       gcol=g2c, bcol=b2c, gres=gres)
                    ln_ln(f"{tag}l2g{g}", list(range(g * 4, g * 4 + 4)),
                          lambda t: (zt(t), zr(t)), EPS_LN1, apply_h2)

                    rounds = []
                    fc0 = 0
                    tog = 0
                    while fc0 < 32:
                        sz = 4 if tog == 0 else 2
                        sz = min(sz, 32 - fc0)
                        rounds.append((fc0, sz, tog))
                        fc0 += sz
                        tog ^= 1
                    for (fc0, sz, tog) in rounds:
                        for fc in range(fc0, fc0 + sz):
                            sl = fc // 4
                            if fc % 4 == 0:
                                P.dma("gpsimd", f"w1{sl % 2}", [f"w1bf{l}"],
                                      [f"w1s{sl % 2}"],
                                      lambda e, l=l, sl=sl: [e.dma_start(
                                          w1buf[sl % 2][:, :].rearrange(
                                              "p (c f) -> p c f", c=NCH),
                                          w1bf_d.ap()[l][:, sl * 512:(sl + 1) * 512]
                                          .rearrange("(c p) f -> p c f", p=128))],
                                      note=f"{tag}w1s{g}.{sl}")
                        ppt = pY0 if tog == 0 else pY1
                        for i, fc in enumerate(range(fc0, fc0 + sz)):
                            sl = fc // 4
                            ppa, prs = pv(ppt, i * 512, (i + 1) * 512)
                            def ymm(e, fc=fc, sl=sl, ppa=ppa, p=p):
                                for c in range(NCH):
                                    e.matmul(
                                        ppa,
                                        w1buf[sl % 2][:, c * 512 + (fc % 4) * 128:
                                                      c * 512 + (fc % 4) * 128 + 128],
                                        h2fm[:, c * 512:(c + 1) * 512],
                                        start=(c == 0), stop=False)
                                return e.matmul(
                                    ppa, b1row[p][0:1, fc * 128:(fc + 1) * 128],
                                    ones_row[0:1, 0:512], start=False, stop=True)
                            P.g("tensor", [f"w1s{sl % 2}", "h2fm", f"b1row{p}",
                                           "consts"], prs, ymm, f"{tag}y1.{g}.{fc}")
                        allrs = bank_resources(ppt.name, 0, sz * 512)
                        P.g("scalar", allrs, [f"gfm{fc0}"],
                            lambda e, fc0=fc0, sz=sz, ppt=ppt: e.activation(
                                gfm[:, fc0 * 512:(fc0 + sz) * 512],
                                ppt[:, 0:sz * 512], AF.Gelu), f"{tag}gel{g}.{fc0}")

                    if debug_stop == f"s4g_{l}" and g == 0:
                        P.dma("gpsimd", "dbg",
                              [f"gfm{fc0}" for (fc0, _, _) in rounds] + ["h2fm"],
                              ["outd"],
                              lambda e: [
                                  e.dma_start(out[0:2048, 0:512].rearrange(
                                      "(a p) b -> p a b", p=128),
                                      gfm[:, 0:16 * 512].rearrange(
                                          "p (a b) -> p a b", b=512)),
                                  e.dma_start(out[0:128, 512:1024],
                                              h2fm[:, 0:512])], n=2)
                        P.fence()
                        return True

                    gres_all = [f"gfm{fc0}" for (fc0, _, _) in rounds]
                    for n in range(2):
                        for qtr in range(8):
                            P.dma("gpsimd", f"w2{qtr % 2}", [f"w2bf{l}"],
                                  [f"w2s{qtr % 2}"],
                                  lambda e, l=l, n=n, qtr=qtr: [e.dma_start(
                                      w2buf[qtr % 2][:, :].rearrange(
                                          "p (c f) -> p c f", c=4),
                                      w2bf_d.ap()[l][qtr * 512:(qtr + 1) * 512,
                                                     n * 512:(n + 1) * 512]
                                      .rearrange("(c p) f -> p c f", p=128))],
                                  note=f"{tag}w2s{g}.{n}.{qtr}")
                            for tr_ in range(4):
                                t = g * 4 + tr_
                                ppa, prs = pv(pY0, tr_ * 512, (tr_ + 1) * 512)
                                def y2mm(e, n=n, qtr=qtr, tr_=tr_, ppa=ppa, p=p):
                                    last = None
                                    for kq in range(4):
                                        kc = qtr * 4 + kq
                                        last = e.matmul(
                                            ppa,
                                            gfm[:, kc * 512 + tr_ * 128:
                                                kc * 512 + tr_ * 128 + 128],
                                            w2buf[qtr % 2][:, kq * 512:
                                                           (kq + 1) * 512],
                                            start=(qtr == 0 and kq == 0),
                                            stop=False)
                                    if qtr == 7:
                                        last = e.matmul(
                                            ppa, ones_row[0:1, 0:128],
                                            brow[p][0:1, 3 * ADIM + D + n * 512:
                                                    3 * ADIM + D + (n + 1) * 512],
                                            start=False, stop=True)
                                    return last
                                P.g("tensor",
                                    gres_all + [f"w2s{qtr % 2}", f"brow{p}",
                                                "consts"],
                                    prs, y2mm, f"{tag}y2.{g}.{n}.{qtr}.{tr_}")
                        for tr_ in range(4):
                            t = g * 4 + tr_
                            ppa, prs = pv(pY0, tr_ * 512, (tr_ + 1) * 512)
                            P.g("vector", prs + [zr(t)], [zr(t)],
                                lambda e, t=t, n=n, ppa=ppa: e.tensor_add(
                                    zt(t)[:, n * 512:(n + 1) * 512],
                                    zt(t)[:, n * 512:(n + 1) * 512], ppa),
                                f"{tag}z2{t}.{n}")

                if l + 1 < DEPTH_B:
                    load_layer_weights(l + 1)

                if debug_stop == f"zb{l}":
                    for t in range(NT):
                        P.dma("sync", "out", [zr(t)], [f"out{t}"],
                              lambda e, t=t: [e.dma_start(
                                  out[t * 128:(t + 1) * 128, :], zt(t))])
                    P.fence()
                    return True
                return False

            stopped = False
            for l in range(DEPTH_B):
                if emit_attnb_layer(l):
                    stopped = True
                    break
            with nc.Block() as block:
                P.flush(block)
            sB2.__exit__(None, None, None)
            if stopped:
                return nc

            for t in range(NT):
                P.dma("sync", "xin", [f"x2d{t}"], ["cscr"],
                      lambda e, t=t: [e.dma_start(
                          cscr[:, :], x2_d.ap()[t * 128:(t + 1) * 128, :])],
                      note=f"x3r{t}")
                P.g("vector", ["cscr", zr(t)], [zr(t)],
                    lambda e, t=t: e.tensor_add(zt(t), zt(t), cscr[:, :]),
                    f"x3{t}")

            if debug_stop == "x3":
                for t in range(NT):
                    P.dma("sync", "out", [zr(t)], [f"out{t}"],
                          lambda e, t=t: [e.dma_start(out[t * 128:(t + 1) * 128, :],
                                                      zt(t))])
                P.fence()
                with nc.Block() as block:
                    P.flush(block)
                return nc

            sF2 = ExitStack()
            sF2.__enter__()
            (ffn_w, mucols, yfm, dscr, xrb, xkb, kab, rtm) = alloc_ffn_bufs(sF2, "_2")
            emit_ffn("f2")

            for t in range(NT):
                P.dma("sync", "out", [zr(t)], [f"out{t}"],
                      lambda e, t=t: [e.dma_start(out[t * 128:(t + 1) * 128, :],
                                                  zt(t))], note=f"fin{t}")
            P.fence()
            with nc.Block() as block:
                P.flush(block)
            sF2.__exit__(None, None, None)

    return nc



def kernel(**inputs):
    n = 8
    x = np.ascontiguousarray(inputs["L0_attn_a_x"])
    in_map_common = {
        "wqkvT": np.ascontiguousarray(inputs["L0_attn_a_w_qkv"].T),
        "wkyT": np.ascontiguousarray(inputs["L0_attn_a_w_ky"].T),
        "wprojT": np.ascontiguousarray(inputs["L0_attn_a_w_proj"].T),
        "wrT": np.ascontiguousarray(inputs["L0_ffn_W_r"].T),
        "wk2T": np.ascontiguousarray(inputs["L0_ffn_W_k"].T),
        "wvT": np.ascontiguousarray(inputs["L0_ffn_W_v"].T),
        "mu_r": np.ascontiguousarray(inputs["L0_ffn_mu_r"]),
        "mu_k": np.ascontiguousarray(inputs["L0_ffn_mu_k"]),
        "mu_r_c": np.ascontiguousarray(
            inputs["L0_ffn_mu_r"].reshape(8, 128).T),
        "mu_k_c": np.ascontiguousarray(
            inputs["L0_ffn_mu_k"].reshape(8, 128).T),
        "ln1_g_c": np.ascontiguousarray(
            inputs["L0_attn_b_ln1_g"].reshape(4, 8, 128).transpose(0, 2, 1)),
        "ln1_b_c": np.ascontiguousarray(
            inputs["L0_attn_b_ln1_b"].reshape(4, 8, 128).transpose(0, 2, 1)),
        "ln2_g_c": np.ascontiguousarray(
            inputs["L0_attn_b_ln2_g"].reshape(4, 8, 128).transpose(0, 2, 1)),
        "ln2_b_c": np.ascontiguousarray(
            inputs["L0_attn_b_ln2_b"].reshape(4, 8, 128).transpose(0, 2, 1)),
        "ln1_g": np.ascontiguousarray(inputs["L0_attn_b_ln1_g"]),
        "ln1_b": np.ascontiguousarray(inputs["L0_attn_b_ln1_b"]),
        "Wq": np.ascontiguousarray(inputs["L0_attn_b_Wq"]),
        "bq": np.ascontiguousarray(inputs["L0_attn_b_bq"]),
        "Wk": np.ascontiguousarray(inputs["L0_attn_b_Wk"]),
        "bk": np.ascontiguousarray(inputs["L0_attn_b_bk"]),
        "w_g": np.ascontiguousarray(inputs["L0_attn_b_w_g"][:, :, 0]),
        "Wp": np.ascontiguousarray(inputs["L0_attn_b_Wp"]),
        "bp": np.ascontiguousarray(inputs["L0_attn_b_bp"]),
        "Wf": np.ascontiguousarray(inputs["L0_attn_b_Wf"]),
        "bf": np.ascontiguousarray(inputs["L0_attn_b_bf"]),
        "ln2_g": np.ascontiguousarray(inputs["L0_attn_b_ln2_g"]),
        "ln2_b": np.ascontiguousarray(inputs["L0_attn_b_ln2_b"]),
        "W1": np.ascontiguousarray(inputs["L0_attn_b_W1"]),
        "b1": np.ascontiguousarray(inputs["L0_attn_b_b1"]),
        "W2": np.ascontiguousarray(inputs["L0_attn_b_W2"]),
        "b2": np.ascontiguousarray(inputs["L0_attn_b_b2"]),
    }
    nc = build_program(debug_stop=DEBUG_STOP)
    in_maps = []
    for i in range(n):
        m = dict(in_map_common)
        m["x"] = np.ascontiguousarray(x[i])
        in_maps.append(m)
    res = run_bass_kernel_spmd(nc, in_maps, core_ids=list(range(n)))
    return np.stack([r["out"] for r in res.results], axis=0)


# version 08
# speedup vs baseline: 1.0467x; 1.0137x over previous
import numpy as np
import ml_dtypes
from contextlib import ExitStack

import concourse.bass as bass
import concourse.mybir as mybir
from concourse.bass_utils import run_bass_kernel_spmd

F32 = mybir.dt.float32
BF16 = mybir.dt.bfloat16
I32 = mybir.dt.int32
AF = mybir.ActivationFunctionType
ALU = mybir.AluOpType
AX = mybir.AxisListType

T = 2048
D = 1024
NT = 16
NCH = 8
NH = 16
HD = 64
ADIM = 256
HID = 4096
DEPTH_B = 4
MAGIC = 0x5F3759DF
EPS_OUT = 1e-5
EPS_LN1 = 1e-6
EPS_A = 1e-6
SCALE_A = 0.125

ENGINES = ("tensor", "vector", "scalar", "gpsimd", "sync")

DEBUG_STOP = None


class Prog:

    def __init__(self, nc, ctx):
        self.nc = nc
        self.q = {e: [] for e in ENGINES}
        self.psem = {e: ctx.enter_context(nc.semaphore(f"pg_{e}")) for e in ENGINES}
        self.pcnt = {e: 0 for e in ENGINES}
        self.dsem = {}
        self.dcnt = {}
        self.ctx = ctx
        self.marks = {}
        self.lastwait = {e: {} for e in ENGINES}
        self.ngroups = 0
        self.log = []

    def chan(self, name):
        if name not in self.dsem:
            self.dsem[name] = self.ctx.enter_context(self.nc.semaphore(f"dc_{name}"))
            self.dcnt[name] = 0
        return self.dsem[name]

    def _needs(self, eng, reads, writes):
        need = {}
        def add(mk):
            if mk is None:
                return
            key, hdl, val = mk
            if key in need:
                if val > need[key][1]:
                    need[key] = (hdl, val)
            else:
                need[key] = (hdl, val)
        for r in reads:
            m = self.marks.get(r)
            if m:
                add(m["w"])
        for w in writes:
            m = self.marks.get(w)
            if m:
                add(m["w"])
                for k, (hdl, val) in m["r"].items():
                    add((k, hdl, val))
        out = []
        for key, (hdl, val) in need.items():
            if key == ("e", eng):
                pass
            lw = self.lastwait[eng].get(key, -1)
            if val > lw:
                out.append((key, hdl, val))
        return out

    def _after(self, eng, reads, writes, key, hdl, val):
        for w in writes:
            self.marks[w] = {"w": (key, hdl, val), "r": {}}
        for r in reads:
            m = self.marks.setdefault(r, {"w": None, "r": {}})
            old = m["r"].get(key)
            if old is None or val > old[1]:
                m["r"][key] = (hdl, val)

    def g(self, eng, reads, writes, fn, note=""):
        waits = self._needs(eng, reads, writes)
        for key, hdl, val in waits:
            self.lastwait[eng][key] = val
        self.pcnt[eng] += 1
        myval = self.pcnt[eng]
        mysem = self.psem[eng]
        self._after(eng, reads, writes, ("e", eng), mysem, myval)
        self.ngroups += 1
        self.log.append((eng, note, [(k, v) for k, _, v in waits],
                         ("e", eng, myval), reads, writes))

        def emit(engine, waits=waits, fn=fn, mysem=mysem, note=note):
            for key, hdl, val in waits:
                engine.wait_ge(hdl, val)
            last = fn(engine)
            assert last is not None, f"group fn returned None ({note})"
            last.then_inc(mysem, 1)
        self.q[eng].append(emit)

    def dma(self, eng, channel, reads, writes, fn, n=1, note=""):
        hdl = self.chan(channel)
        waits = self._needs(eng, reads, writes)
        for key, hdl2, val in waits:
            self.lastwait[eng][key] = val
        self.dcnt[channel] += 16 * n
        myval = self.dcnt[channel]
        self._after(eng, reads, writes, ("d", channel), hdl, myval)
        self.ngroups += 1
        self.log.append((eng, note, [(k, v) for k, _, v in waits],
                         ("d", channel, myval), reads, writes))

        def emit(engine, waits=waits, fn=fn, hdl=hdl, note=note):
            for key, hdl2, val in waits:
                engine.wait_ge(hdl2, val)
            insts = fn(engine)
            for i in insts:
                i.then_inc(hdl, 16)
        self.q[eng].append(emit)

    def fence(self, delay=0):
        snap = dict(self.pcnt)
        dsnap = dict(self.dcnt)
        for e in ENGINES:
            items = []
            for e2 in ENGINES:
                if e2 != e and snap[e2] > self.lastwait[e].get(("e", e2), -1):
                    items.append((("e", e2), self.psem[e2], snap[e2]))
            for c, v in dsnap.items():
                if c == "pre":
                    continue
                if v > self.lastwait[e].get(("d", c), -1):
                    items.append((("d", c), self.dsem[c], v))
            for key, hdl, val in items:
                self.lastwait[e][key] = val

            def emit(engine, items=items, delay=delay):
                engine.drain()
                for key, hdl, val in items:
                    engine.wait_ge(hdl, val)
                if delay:
                    engine.nop(cycle_cnt=delay)
                engine.drain()
            self.q[e].append(emit)

    def flush(self, block):
        qs = self.q
        self.q = {e: [] for e in ENGINES}

        @block.tensor
        def _(eng):
            for f in qs["tensor"]:
                f(eng)

        @block.vector
        def _(eng):
            for f in qs["vector"]:
                f(eng)

        @block.scalar
        def _(eng):
            for f in qs["scalar"]:
                f(eng)

        @block.gpsimd
        def _(eng):
            for f in qs["gpsimd"]:
                f(eng)

        @block.sync
        def _(eng):
            for f in qs["sync"]:
                f(eng)


def bank_resources(name, col0, col1, dtype_bytes=4):
    b0 = (col0 * dtype_bytes) // 2048
    b1 = (col1 * dtype_bytes - 1) // 2048
    return [f"{name}:b{i}" for i in range(b0, b1 + 1)]



def build_program(debug_stop=None):
    nc = bass.Bass("TRN2", target_bir_lowering=False)

    x_in = nc.dram_tensor("x", [T, D], F32, kind="ExternalInput").ap()
    wqkvT = nc.dram_tensor("wqkvT", [D, 3 * D], F32, kind="ExternalInput").ap()
    wkyT = nc.dram_tensor("wkyT", [D, D], F32, kind="ExternalInput").ap()
    wprojT = nc.dram_tensor("wprojT", [D, D], F32, kind="ExternalInput").ap()
    wrT = nc.dram_tensor("wrT", [D, D], F32, kind="ExternalInput").ap()
    wk2T = nc.dram_tensor("wk2T", [D, D], F32, kind="ExternalInput").ap()
    wvT = nc.dram_tensor("wvT", [D, D], F32, kind="ExternalInput").ap()
    mu_r = nc.dram_tensor("mu_r", [D], F32, kind="ExternalInput").ap()
    mu_k = nc.dram_tensor("mu_k", [D], F32, kind="ExternalInput").ap()
    mu_r_c = nc.dram_tensor("mu_r_c", [128, 8], F32, kind="ExternalInput").ap()
    mu_k_c = nc.dram_tensor("mu_k_c", [128, 8], F32, kind="ExternalInput").ap()
    ln1_g_c = nc.dram_tensor("ln1_g_c", [DEPTH_B, 128, 8], F32, kind="ExternalInput").ap()
    ln1_b_c = nc.dram_tensor("ln1_b_c", [DEPTH_B, 128, 8], F32, kind="ExternalInput").ap()
    ln2_g_c = nc.dram_tensor("ln2_g_c", [DEPTH_B, 128, 8], F32, kind="ExternalInput").ap()
    ln2_b_c = nc.dram_tensor("ln2_b_c", [DEPTH_B, 128, 8], F32, kind="ExternalInput").ap()
    ln1_g = nc.dram_tensor("ln1_g", [DEPTH_B, D], F32, kind="ExternalInput").ap()
    ln1_b = nc.dram_tensor("ln1_b", [DEPTH_B, D], F32, kind="ExternalInput").ap()
    Wq = nc.dram_tensor("Wq", [DEPTH_B, D, ADIM], F32, kind="ExternalInput").ap()
    bq = nc.dram_tensor("bq", [DEPTH_B, ADIM], F32, kind="ExternalInput").ap()
    Wk = nc.dram_tensor("Wk", [DEPTH_B, D, ADIM], F32, kind="ExternalInput").ap()
    bk = nc.dram_tensor("bk", [DEPTH_B, ADIM], F32, kind="ExternalInput").ap()
    w_g = nc.dram_tensor("w_g", [DEPTH_B, ADIM], F32, kind="ExternalInput").ap()
    Wp = nc.dram_tensor("Wp", [DEPTH_B, ADIM, ADIM], F32, kind="ExternalInput").ap()
    bp = nc.dram_tensor("bp", [DEPTH_B, ADIM], F32, kind="ExternalInput").ap()
    Wf = nc.dram_tensor("Wf", [DEPTH_B, ADIM, D], F32, kind="ExternalInput").ap()
    bf_ = nc.dram_tensor("bf", [DEPTH_B, D], F32, kind="ExternalInput").ap()
    ln2_g = nc.dram_tensor("ln2_g", [DEPTH_B, D], F32, kind="ExternalInput").ap()
    ln2_b = nc.dram_tensor("ln2_b", [DEPTH_B, D], F32, kind="ExternalInput").ap()
    W1 = nc.dram_tensor("W1", [DEPTH_B, D, HID], F32, kind="ExternalInput").ap()
    b1 = nc.dram_tensor("b1", [DEPTH_B, HID], F32, kind="ExternalInput").ap()
    W2 = nc.dram_tensor("W2", [DEPTH_B, HID, D], F32, kind="ExternalInput").ap()
    b2 = nc.dram_tensor("b2", [DEPTH_B, D], F32, kind="ExternalInput").ap()
    out = nc.dram_tensor("out", [T, D], F32, kind="ExternalOutput").ap()

    aout_d = nc.dram_tensor("aout_d", [T, D], BF16)
    x2_d = nc.dram_tensor("x2_d", [T, D], F32)
    w1bf_d = nc.dram_tensor("w1bf_d", [DEPTH_B, D, HID], BF16)
    w2bf_d = nc.dram_tensor("w2bf_d", [DEPTH_B, HID, D], BF16)

    ctx = ExitStack()
    with ctx:
        sb = lambda n, shape, dt: ctx.enter_context(nc.sbuf_tensor(n, shape, dt))
        P = Prog(nc, ctx)

        id_bf = sb("id_bf", [128, 128], BF16)
        id_f32 = sb("id_f32", [128, 128], F32)
        maskneg = sb("maskneg", [128, 128], F32)
        tril01 = sb("tril01", [128, 128], F32)
        ones_row = sb("ones_row", [1, 512], BF16)
        stat = sb("stat", [128, 64], F32)
        stat2 = sb("stat2", [128, 64], F32)
        stat3 = sb("stat3", [128, 64], F32)
        stat4 = sb("stat4", [128, 64], F32)
        stati = sb("stati", [128, 64], I32)

        P.g("gpsimd", [], ["c_idbf"],
            lambda e: e.memset(id_bf[:, :], 1.0), "ms_idbf")
        P.g("gpsimd", ["c_idbf"], ["c_idbf"],
            lambda e: e.affine_select(id_bf[:, :], id_bf[:, :], [[-1, 128]],
                                      ALU.is_equal, 0.0, base=0,
                                      channel_multiplier=1), "as_idbf")
        P.g("gpsimd", [], ["c_idf"],
            lambda e: e.memset(id_f32[:, :], 1.0), "ms_idf")
        P.g("gpsimd", ["c_idf"], ["c_idf"],
            lambda e: e.affine_select(id_f32[:, :], id_f32[:, :], [[-1, 128]],
                                      ALU.is_equal, 0.0, base=0,
                                      channel_multiplier=1), "as_idf")
        P.g("gpsimd", [], ["c_mn"],
            lambda e: e.memset(maskneg[:, :], 0.0), "ms_mn")
        P.g("gpsimd", ["c_mn"], ["c_mn"],
            lambda e: e.affine_select(maskneg[:, :], maskneg[:, :], [[-1, 128]],
                                      ALU.is_ge, -1e30, base=0,
                                      channel_multiplier=1), "as_mn")
        P.g("gpsimd", [], ["c_tril"],
            lambda e: e.memset(tril01[:, :], 1.0), "ms_tril")
        P.g("gpsimd", ["c_tril"], ["c_tril"],
            lambda e: e.affine_select(tril01[:, :], tril01[:, :], [[-1, 128]],
                                      ALU.is_ge, 0.0, base=0,
                                      channel_multiplier=1), "as_tril")
        P.g("gpsimd", ["c_idbf", "c_idf", "c_mn", "c_tril"], ["consts"],
            lambda e: e.memset(ones_row[:, :], 1.0), "ms_ones")

        for l in range(DEPTH_B):
            P.dma("gpsimd", "pre", [], [f"w1bf{l}"],
                  lambda e, l=l: [e.dma_start(w1bf_d.ap()[l], W1[l])],
                  note=f"pre1.{l}")
            P.dma("gpsimd", "pre", [], [f"w2bf{l}"],
                  lambda e, l=l: [e.dma_start(w2bf_d.ap()[l], W2[l])],
                  note=f"pre2.{l}")

        def nr_rsqrt(cols_ap, n, res, eng="vector", parts=128):
            t1 = stat4[0:parts, 32:32 + n]
            t2 = stat3[0:parts, 48:48 + n]
            if not isinstance(res, list):
                res = [res]
            P.g(eng, res, ["nr_t1"],
                lambda e, a=cols_ap: e.tensor_scalar(
                    t1.bitcast(I32), a.bitcast(I32), -0.5, MAGIC, ALU.mult, ALU.add),
                "nr_seed")
            for it in range(4):
                P.g(eng, ["nr_t1"], ["nr_t2"],
                    lambda e: e.tensor_mul(t2, t1, t1), "nr_sq")
                P.g(eng, ["nr_t2"] + res, ["nr_t2"],
                    lambda e, a=cols_ap: e.tensor_mul(t2, t2, a), "nr_mv")
                P.g(eng, ["nr_t2"], ["nr_t2"],
                    lambda e: e.tensor_scalar(t2, t2, -0.5, 1.5,
                                              ALU.mult, ALU.add), "nr_aff")
                P.g(eng, ["nr_t1", "nr_t2"], ["nr_t1"],
                    lambda e: e.tensor_mul(t1, t1, t2), "nr_up")
            P.g(eng, ["nr_t1"], res,
                lambda e, a=cols_ap: e.tensor_copy(a, t1), "nr_out")

        def alias(srcres, dstres_list):
            import copy as _copy
            m = P.marks.get(srcres)
            if m is None:
                return
            for d in dstres_list:
                P.marks[d] = {"w": m["w"], "r": dict(m["r"])}

        skip_early = debug_stop is not None and debug_stop.startswith("gb")
        skip_f1 = skip_early and debug_stop != "gbf1"
        with ExitStack() as sA:
            sbA = lambda n, shape, dt: sA.enter_context(nc.sbuf_tensor(n, shape, dt))
            psA = lambda n, shape, dt: sA.enter_context(nc.psum_tensor(n, shape, dt))

            hfm = sbA("hfm", [128, NCH * T], BF16)
            qfm = sbA("qfm", [128, NCH * T], BF16)
            kfm = sbA("kfm", [128, NCH * T], BF16)
            kyfm = sbA("kyfm", [128, NCH * T], BF16)
            vtm = sbA("vtm", [128, NT * D], BF16)
            xt2 = [sbA(f"xt{i}", [128, D], F32) for i in range(2)]
            htm = sbA("htm", [128, D], BF16)
            wbuf = [sbA(f"wbuf{i}", [128, NCH * 512], BF16) for i in range(2)]
            vscr = sbA("vscr", [128, 512], BF16)
            amscr = sbA("amscr", [128, 1024], F32)
            wdscr = sbA("wdscr", [128, 128], F32)

            WR = (0, 2048)
            WTC = 4096
            ESC = 6144
            AMT = 7168
            AOT = 9216
            wrow_ap = lambda hi, a, b: hfm[:, WR[hi] + a: WR[hi] + b]
            wT_ap = lambda a, b: hfm[:, WTC + a: WTC + b]
            escr_ap = lambda n: hfm[:, ESC: ESC + n]
            amt_ap = lambda a, b: hfm[:, AMT + a: AMT + b]
            aot_ap = lambda a, b: hfm[:, AOT + a: AOT + b]

            pRow = [psA(f"pRow{i}", [128, 1024], F32) for i in range(3)]
            pT = psA("pT_A", [128, 1024], BF16)
            pM = psA("pM_A", [128, 512], F32)

            for t in range(NT if not skip_early else 0):
                xt = xt2[t % 2]
                xr_ = f"xt{t % 2}"
                P.dma("sync", "xin", [], [xr_],
                      lambda e, t=t, xt=xt: [e.dma_start(
                          xt[:, :], x_in[t * 128:(t + 1) * 128, :])],
                      note=f"x{t}")
                P.g("vector", [xr_], [f"sx{t}"],
                    lambda e, t=t, xt=xt: e.reduce_sum(stat[:, t:t + 1], xt[:, :],
                                                       AX.X), f"sx{t}")
                P.g("scalar", [xr_], [f"ssq{t}", "sqscr"],
                    lambda e, t=t, xt=xt: e.activation(
                        amscr[:, :], xt[:, :], AF.Square,
                        accum_out=stat[:, 16 + t:17 + t]), f"ssq{t}")
                P.g("vector", [f"sx{t}"], [f"mu{t}"],
                    lambda e, t=t: e.tensor_scalar(stat[:, 32 + t:33 + t],
                                                   stat[:, t:t + 1], 1.0 / D, None,
                                                   ALU.mult), f"mu{t}")
                P.g("vector", [f"mu{t}"], [f"musq{t}"],
                    lambda e, t=t: e.tensor_mul(stat2[:, 32 + t:33 + t],
                                                stat[:, 32 + t:33 + t],
                                                stat[:, 32 + t:33 + t]), f"musq{t}")
                P.g("vector", [f"ssq{t}", f"musq{t}"], [f"veps{t}"],
                    lambda e, t=t: e.tensor_scalar(stat3[:, 32 + t:33 + t],
                                                   stat[:, 16 + t:17 + t], 1.0 / D,
                                                   EPS_OUT, ALU.mult, ALU.add),
                    f"v1_{t}")
                P.g("vector", [f"veps{t}", f"musq{t}"], [f"veps{t}"],
                    lambda e, t=t: e.tensor_sub(stat3[:, 32 + t:33 + t],
                                                stat3[:, 32 + t:33 + t],
                                                stat2[:, 32 + t:33 + t]), f"v2_{t}")
            if not skip_early:
                P.g("scalar", [f"veps{t}" for t in range(NT)], ["lnvall"],
                    lambda e: e.activation(stat2[:, 48:64], stat3[:, 32:48],
                                           AF.Ln), "lnvall")
                P.g("scalar", ["lnvall"], ["rstdall"],
                    lambda e: e.activation(stat4[:, 0:16], stat2[:, 48:64],
                                           AF.Exp, scale=-0.5), "rstdall")
            for t in range(NT if not skip_early else 0):
                xt = xt2[t % 2]
                xr_ = f"xt{t % 2}"
                P.g("vector", [xr_, f"mu{t}", "rstdall"], ["htm"],
                    lambda e, t=t, xt=xt: e.tensor_scalar(htm[:, :], xt[:, :],
                                                          stat[:, 32 + t:33 + t],
                                                          stat4[:, t:t + 1],
                                                          ALU.subtract, ALU.mult),
                    f"ln{t}")
                def tr(e, t=t):
                    last = None
                    for c in range(NCH):
                        last = e.transpose(pT[:, c * 128:(c + 1) * 128],
                                           htm[:, c * 128:(c + 1) * 128], id_bf[:, :])
                    return last
                P.g("tensor", ["htm", "consts"], ["pT_A"], tr, f"trh{t}")
                def ev(e, t=t):
                    last = None
                    for c in range(NCH):
                        last = e.tensor_copy(hfm[:, c * T + t * 128: c * T + (t + 1) * 128],
                                             pT[:, c * 128:(c + 1) * 128])
                    return last
                P.g("vector", ["pT_A"], ["hfm"], ev, f"evh{t}")

            for s in range(8 if not skip_early else 0):
                wb = wbuf[s % 2]
                if s < 6:
                    src = wqkvT[:, s * 512:(s + 1) * 512]
                else:
                    src = wkyT[:, (s - 6) * 512:(s - 5) * 512]
                P.dma("gpsimd", "wa", [], [f"wbuf{s % 2}"],
                      lambda e, src=src, wb=wb: [e.dma_start(
                          wb[:, :].rearrange("p (c f) -> p c f", c=NCH),
                          src.rearrange("(c p) f -> p c f", p=128))],
                      note=f"wslab{s}")
                is_q = s < 2
                is_v = 4 <= s < 6
                if s < 2:
                    dst, dname = qfm, "qfm"
                elif s < 4:
                    dst, dname = kfm, "kfm"
                elif s < 6:
                    dst, dname = vtm, "vtm"
                else:
                    dst, dname = kyfm, "kyfm"
                for fc in range(4):
                    oc = (s % 2) * 4 + fc if s < 6 else (s - 6) * 4 + fc
                    for tg in range(4):
                        pp = pRow[(fc * 4 + tg) % 3]
                        pcols = (fc % 2) * 512
                        prs = bank_resources(pp.name, pcols, pcols + 512)
                        def mms(e, wb=wb, fc=fc, tg=tg, pp=pp, pcols=pcols):
                            last = None
                            for c in range(NCH):
                                last = e.matmul(
                                    pp[:, pcols:pcols + 512],
                                    wb[:, c * 512 + fc * 128: c * 512 + (fc + 1) * 128],
                                    hfm[:, c * T + tg * 512: c * T + (tg + 1) * 512],
                                    start=(c == 0), stop=(c == NCH - 1))
                            return last
                        P.g("tensor", [f"wbuf{s % 2}", "hfm"], prs, mms,
                            f"qkv{s}.{fc}.{tg}")
                        if not is_v:
                            def ev(e, dst=dst, oc=oc, tg=tg, pp=pp, pcols=pcols,
                                   is_q=is_q):
                                dcol = oc * T + tg * 512
                                if is_q:
                                    return e.tensor_scalar(
                                        dst[:, dcol:dcol + 512], pp[:, pcols:pcols + 512],
                                        SCALE_A, None, ALU.mult)
                                return e.tensor_copy(dst[:, dcol:dcol + 512],
                                                     pp[:, pcols:pcols + 512])
                            P.g("vector", prs, [dname], ev, f"ev{s}.{fc}.{tg}")
                        else:
                            P.g("vector", prs, ["vscr"],
                                lambda e, pp=pp, pcols=pcols: e.tensor_copy(
                                    vscr[:, :], pp[:, pcols:pcols + 512]),
                                f"evv{s}.{fc}.{tg}")
                            def trv(e):
                                last = None
                                for tau in range(4):
                                    last = e.transpose(pT[:, tau * 128:(tau + 1) * 128],
                                                       vscr[:, tau * 128:(tau + 1) * 128],
                                                       id_bf[:, :])
                                return last
                            P.g("tensor", ["vscr", "consts"], ["pT_A"], trv,
                                f"trv{s}.{fc}.{tg}")
                            def evv(e, oc=oc, tg=tg):
                                last = None
                                for tau in range(4):
                                    kt = tg * 4 + tau
                                    last = e.tensor_copy(
                                        vtm[:, kt * D + oc * 128: kt * D + (oc + 1) * 128],
                                        pT[:, tau * 128:(tau + 1) * 128])
                                return last
                            P.g("vector", ["pT_A"], ["vtm"], evv, f"evvt{s}.{fc}.{tg}")

            for fh in range(2 if not skip_early else 0):
                P.dma("gpsimd", "wa", [], [f"wbuf{fh}"],
                      lambda e, fh=fh: [e.dma_start(
                          wbuf[fh][:, :].rearrange("p (c f) -> p c f", c=NCH),
                          wprojT[:, fh * 512:(fh + 1) * 512].rearrange(
                              "(c p) f -> p c f", p=128))],
                      note=f"wproj{fh}")

            if debug_stop in ("hfm", "qfm", "kfm", "kyfm", "vtm"):
                srcs = {"hfm": hfm, "qfm": qfm, "kfm": kfm, "kyfm": kyfm,
                        "vtm": vtm}
                st = srcs[debug_stop]
                P.dma("gpsimd", "dbg", [debug_stop], ["outd"],
                      lambda e, st=st: [e.dma_start(
                          out.rearrange("(a p) b -> p a b", p=128),
                          st[:, :].rearrange("p (a b) -> p a b", a=16))],
                      note="dump")
                P.fence()
                with nc.Block() as block:
                    P.flush(block)
                return nc

            alias("hfm", ["wrow0", "wrow1", "wTsb", "escr", "attnm_t", "aoutt"])

            rot = 0
            for j in range(NT if not skip_early else 0):
                nkc = j + 1
                slabs = [(0, min(nkc, 8))]
                if nkc > 8:
                    slabs.append((8, nkc))
                def emit_pass3(h, j, nkc):
                    hi = h % 2
                    for g0 in range(0, nkc, 8):
                        g1 = min(g0 + 8, nkc)
                        def trw(e, hi=hi, g0=g0, g1=g1):
                            last = None
                            for kc in range(g0, g1):
                                last = e.transpose(
                                    pT[:, (kc - g0) * 128:(kc - g0 + 1) * 128],
                                    wrow_ap(hi, kc * 128, (kc + 1) * 128),
                                    id_bf[:, :])
                            return last
                        P.g("tensor", [f"wrow{hi}", "consts"], ["pT_A"], trw,
                            f"trw{j}.{h}.{g0}")
                        P.g("vector", ["pT_A"], ["wTsb"],
                            lambda e, g0=g0, g1=g1: e.tensor_copy(
                                wT_ap(g0 * 128, g1 * 128),
                                pT[:, 0:(g1 - g0) * 128]), f"evw{j}.{h}.{g0}")
                    def av(e, h=h, nkc=nkc):
                        last = None
                        for kc in range(nkc):
                            last = e.matmul(
                                pM[:, 0:64],
                                wT_ap(kc * 128, (kc + 1) * 128),
                                vtm[:, kc * D + h * 64: kc * D + h * 64 + 64],
                                start=(kc == 0), stop=(kc == nkc - 1))
                        return last
                    P.g("tensor", ["wTsb", "vtm"], ["pM_A"], av, f"av{j}.{h}")
                    P.g("vector", ["pM_A", f"rsw{hi}"], ["attnm_t"],
                        lambda e, j=j, h=h, hi=hi: e.tensor_scalar(
                            amt_ap((j % 2) * D + h * 64, (j % 2) * D + h * 64 + 64),
                            pM[:, 0:64], stat4[:, 20 + hi:21 + hi], None,
                            ALU.mult), f"evav{j}.{h}")

                def emit_pe_stage(h):
                    hi = h % 2
                    base = 64 * (h % 2)
                    ch = h // 2
                    st = {"ly": [], "l": []}
                    for si, (c0, c1) in enumerate(slabs):
                        if deep:
                            pp = pRow[0]
                        else:
                            nonlocal_rot = None
                            pp = pRow[rotbox[0] % 3]; rotbox[0] += 1
                        ncol = (c1 - c0) * 128
                        prs = bank_resources(pp.name, 0, ncol)
                        def mms(e, pp=pp, c0=c0, c1=c1, base=base, ch=ch, j=j):
                            last = None
                            ncol = (c1 - c0) * 128
                            qap = qfm[base:base + 64,
                                      ch * T + j * 128: ch * T + (j + 1) * 128]
                            done = 0
                            while done < ncol:
                                w = min(512, ncol - done)
                                last = e.matmul(
                                    pp[:, done:done + w], qap,
                                    kyfm[base:base + 64,
                                         ch * T + c0 * 128 + done:
                                         ch * T + c0 * 128 + done + w],
                                    start=True, stop=True)
                                done += w
                            return last
                        P.g("tensor", ["qfm", "kyfm"], prs, mms, f"ly{j}.{h}.{si}")
                        if c1 == nkc:
                            dc = (nkc - 1 - c0) * 128
                            P.g("vector", prs + ["consts"], prs,
                                lambda e, pp=pp, dc=dc: e.tensor_add(
                                    pp[:, dc:dc + 128], pp[:, dc:dc + 128],
                                    maskneg[:, :]), f"lym{j}.{h}")
                        st["ly"].append((pp, ncol, prs))
                    for si, (c0, c1) in enumerate(slabs):
                        if deep:
                            pp = pRow[1 + (h % 2)]
                        else:
                            pp = pRow[rotbox[0] % 3]; rotbox[0] += 1
                        ncol = (c1 - c0) * 128
                        prs = bank_resources(pp.name, 0, ncol)
                        def mms2(e, pp=pp, c0=c0, c1=c1, base=base, ch=ch, j=j):
                            last = None
                            ncol = (c1 - c0) * 128
                            qap = qfm[base:base + 64,
                                      ch * T + j * 128: ch * T + (j + 1) * 128]
                            done = 0
                            while done < ncol:
                                w = min(512, ncol - done)
                                last = e.matmul(
                                    pp[:, done:done + w], qap,
                                    kfm[base:base + 64,
                                        ch * T + c0 * 128 + done:
                                        ch * T + c0 * 128 + done + w],
                                    start=True, stop=True)
                                done += w
                            return last
                        P.g("tensor", ["qfm", "kfm"], prs, mms2, f"l{j}.{h}.{si}")
                        st["l"].append((pp, c0, c1, ncol, prs))
                    return st

                def emit_exp_stage(h, st):
                    hi = h % 2
                    aggcols = []
                    for si, (pp, ncol, prs) in enumerate(st["ly"]):
                        acol = 48 + 2 * hi + si
                        aggcols.append(acol)
                        P.g("scalar", prs, [f"aggp{2 * hi + si}", "escr"],
                            lambda e, pp=pp, ncol=ncol, acol=acol: e.activation(
                                escr_ap(ncol), pp[:, 0:ncol], AF.Exp,
                                accum_out=stat[:, acol:acol + 1]),
                            f"exp{j}.{h}.{si}")
                    a0 = aggcols[0]
                    a1 = aggcols[1] if len(aggcols) == 2 else None
                    if a1 is not None:
                        P.g("vector",
                            [f"aggp{a0 - 48}", f"aggp{a1 - 48}"], [f"agg{hi}"],
                            lambda e, a0=a0, a1=a1, hi=hi: e.tensor_add(
                                stat2[:, hi:hi + 1], stat[:, a0:a0 + 1],
                                stat[:, a1:a1 + 1]), f"agg{j}.{h}")
                        P.g("vector", [f"agg{hi}"], [f"agg{hi}"],
                            lambda e, hi=hi: e.tensor_scalar(
                                stat2[:, hi:hi + 1], stat2[:, hi:hi + 1], 1.0,
                                EPS_A, ALU.mult, ALU.add), f"agge{j}.{h}")
                    else:
                        P.g("vector", [f"aggp{a0 - 48}"], [f"agg{hi}"],
                            lambda e, a0=a0, hi=hi: e.tensor_scalar(
                                stat2[:, hi:hi + 1], stat[:, a0:a0 + 1], 1.0,
                                EPS_A, ALU.mult, ALU.add), f"agg{j}.{h}")
                    P.g("vector", [f"agg{hi}"], [f"ragg{hi}"],
                        lambda e, hi=hi: e.reciprocal(stat4[:, 16 + hi:17 + hi],
                                                      stat2[:, hi:hi + 1]),
                        f"ragg{j}.{h}")
                    P.g("vector", [f"ragg{hi}"], [f"lnb{hi}"],
                        lambda e, hi=hi: e.tensor_scalar(
                            stat4[:, 18 + hi:19 + hi], stat4[:, 16 + hi:17 + hi],
                            EPS_A, 1.0, ALU.mult, ALU.add), f"lnb{j}.{h}")

                def emit_pass2(h, st):
                    hi = h % 2
                    sumcols = []
                    for (pp, c0, c1, ncol, prs) in st["l"]:
                        P.g("scalar", prs + [f"ragg{hi}"], ["amscr"],
                            lambda e, pp=pp, ncol=ncol, hi=hi: e.activation(
                                amscr[:, 0:ncol], pp[:, 0:ncol], AF.Abs,
                                scale=stat4[:, 16 + hi:17 + hi]),
                            f"abs{j}.{h}")
                        has_diag = (c1 == nkc)
                        full_cols = ncol - 128 if has_diag else ncol
                        if full_cols > 0:
                            acol = 52 + 2 * hi + (0 if c0 == 0 else 1)
                            sumcols.append(acol)
                            P.g("scalar", ["amscr", f"lnb{hi}"],
                                [f"wrow{hi}", f"us{acol}"],
                                lambda e, hi=hi, c0=c0, fc=full_cols,
                                       acol=acol: e.activation(
                                    wrow_ap(hi, c0 * 128, c0 * 128 + fc),
                                    amscr[:, 0:fc], AF.Ln,
                                    bias=stat4[:, 18 + hi:19 + hi],
                                    accum_out=stat[:, acol:acol + 1]),
                                f"lnf{j}.{h}")
                        if has_diag:
                            P.g("scalar", ["amscr", f"lnb{hi}"], ["wdscr"],
                                lambda e, fc=full_cols, hi=hi: e.activation(
                                    wdscr[:, :], amscr[:, fc:fc + 128], AF.Ln,
                                    bias=stat4[:, 18 + hi:19 + hi]),
                                f"lnd{j}.{h}")
                            acol = 56 + hi
                            sumcols.append(acol)
                            P.g("vector", ["wdscr", "consts"],
                                [f"wrow{hi}", f"us{acol}"],
                                lambda e, hi=hi, j=j, acol=acol:
                                e.scalar_tensor_tensor(
                                    wrow_ap(hi, j * 128, (j + 1) * 128),
                                    wdscr[:, :], 1.0, tril01[:, :],
                                    ALU.mult, ALU.mult,
                                    accum_out=stat[:, acol:acol + 1]),
                                f"wd{j}.{h}")
                    rs = [f"us{c}" for c in sumcols]
                    if len(sumcols) == 1:
                        P.g("vector", rs, [f"sw{hi}"],
                            lambda e, sc=sumcols, hi=hi: e.tensor_scalar(
                                stat2[:, 8 + hi:9 + hi], stat[:, sc[0]:sc[0] + 1],
                                1.0, EPS_A, ALU.mult, ALU.add), f"sw{j}.{h}")
                    else:
                        P.g("vector", rs, [f"sw{hi}"],
                            lambda e, sc=sumcols, hi=hi: e.tensor_add(
                                stat2[:, 8 + hi:9 + hi], stat[:, sc[0]:sc[0] + 1],
                                stat[:, sc[1]:sc[1] + 1]), f"swa{j}.{h}")
                        if len(sumcols) == 3:
                            P.g("vector", rs + [f"sw{hi}"], [f"sw{hi}"],
                                lambda e, sc=sumcols, hi=hi: e.tensor_add(
                                    stat2[:, 8 + hi:9 + hi],
                                    stat2[:, 8 + hi:9 + hi],
                                    stat[:, sc[2]:sc[2] + 1]), f"swb{j}.{h}")
                        P.g("vector", [f"sw{hi}"], [f"sw{hi}"],
                            lambda e, hi=hi: e.tensor_scalar(
                                stat2[:, 8 + hi:9 + hi], stat2[:, 8 + hi:9 + hi],
                                1.0, EPS_A, ALU.mult, ALU.add), f"swe{j}.{h}")
                    P.g("vector", [f"sw{hi}"], [f"rsw{hi}"],
                        lambda e, hi=hi: e.reciprocal(stat4[:, 20 + hi:21 + hi],
                                                      stat2[:, 8 + hi:9 + hi]),
                        f"rsw{j}.{h}")

                deep = (nkc <= 8)
                rotbox = [rot]
                if deep:
                    sts = {}
                    for h in range(NH):
                        sts[h] = emit_pe_stage(h)
                        if h >= 2:
                            emit_pass3(h - 2, j, nkc)
                        emit_exp_stage(h, sts[h])
                        if h >= 1:
                            emit_pass2(h - 1, sts[h - 1])
                    emit_pass2(NH - 1, sts[NH - 1])
                    emit_pass3(NH - 2, j, nkc)
                else:
                    for h in range(NH):
                        st = emit_pe_stage(h)
                        if h > 0:
                            emit_pass3(h - 1, j, nkc)
                        emit_exp_stage(h, st)
                        emit_pass2(h, st)
                rot = rotbox[0]
                emit_pass3(NH - 1, j, nkc)
                def tra(e, j=j):
                    last = None
                    for c in range(NCH):
                        last = e.transpose(pT[:, c * 128:(c + 1) * 128],
                                           amt_ap((j % 2) * D + c * 128,
                                                  (j % 2) * D + (c + 1) * 128),
                                           id_bf[:, :])
                    return last
                P.g("tensor", ["attnm_t", "consts"], ["pT_A"], tra, f"tra{j}")
                P.g("vector", ["pT_A"], ["wrow0"],
                    lambda e: e.tensor_copy(wrow_ap(0, 0, 1024), pT[:, :]), f"evam{j}")
                for fh in range(2):
                    pp = pRow[fh]
                    prs = bank_resources(pp.name, 0, 512)
                    def pmm(e, fh=fh, pp=pp):
                        last = None
                        for c in range(NCH):
                            last = e.matmul(
                                pp[:, 0:512],
                                wrow_ap(0, c * 128, (c + 1) * 128),
                                wbuf[fh][:, c * 512:(c + 1) * 512],
                                start=(c == 0), stop=(c == NCH - 1))
                        return last
                    P.g("tensor", ["wrow0", "wbuf0", "wbuf1"], prs, pmm, f"proj{j}.{fh}")
                    P.g("vector", prs, ["aoutt"],
                        lambda e, j=j, fh=fh, pp=pp: e.tensor_copy(
                            aot_ap((j % 2) * D + fh * 512, (j % 2) * D + (fh + 1) * 512),
                            pp[:, 0:512]), f"evp{j}.{fh}")
                P.dma("sync", "ao", ["aoutt"], [f"aod{j}"],
                      lambda e, j=j: [e.dma_start(
                          aout_d.ap()[j * 128:(j + 1) * 128, :],
                          aot_ap((j % 2) * D, (j % 2 + 1) * D))], note=f"ao{j}")

            P.fence()
            with nc.Block() as block:
                P.flush(block)

        with ExitStack() as sR:
            sbR = lambda n, shape, dt: sR.enter_context(nc.sbuf_tensor(n, shape, dt))
            psR = lambda n, shape, dt: sR.enter_context(nc.psum_tensor(n, shape, dt))

            z = sbR("z", [128, NT * D], F32)
            pY0 = psR("pY0", [128, 2048], F32)
            pY1 = psR("pY1", [128, 1024], F32)
            pM2 = psR("pM2", [128, 512], F32)
            pT2 = psR("pT2", [128, 1024], BF16)

            def pv(pt, c0, c1):
                return pt[:, c0:c1], bank_resources(pt.name, c0, c1)

            zt = lambda t: z[:, t * D:(t + 1) * D]
            zr = lambda t: f"z{t}"




            htm2 = sbR("htm2", [128, D], BF16)
            cscr = sbR("cscr", [128, D], F32)
            rowscr = sbR("rowscr", [1, D], F32)
            rowbf = sbR("rowbf", [1, D], BF16)

            def load_cols(tag, col_src, n, dst_ap, dst_res):
                P.dma("sync", "sm", [], [dst_res],
                      lambda e, col_src=col_src, dst_ap=dst_ap: [
                          e.dma_start(dst_ap, col_src)], note=f"{tag}ld")

            def ln_ln(tag, tiles, src_fn, eps, apply_fn):
                for qi in range(0, len(tiles), 16):
                    quad = tiles[qi:qi + 16]
                    for t in quad:
                        ap, res = src_fn(t)
                        P.g("vector", [res], [f"sx{t}"],
                            lambda e, ap=ap, t=t: e.reduce_sum(
                                stat[:, t:t + 1], ap, AX.X), f"{tag}sx{t}")
                        P.g("scalar", [res], [f"ssq{t}", "cscr"],
                            lambda e, ap=ap, t=t: e.activation(
                                cscr[:, :], ap, AF.Square,
                                accum_out=stat[:, 16 + t:17 + t]), f"{tag}ssq{t}")
                        P.g("vector", [f"sx{t}"], [f"mu{t}"],
                            lambda e, t=t: e.tensor_scalar(
                                stat[:, 32 + t:33 + t], stat[:, t:t + 1],
                                1.0 / D, None, ALU.mult), f"{tag}mu{t}")
                        P.g("vector", [f"mu{t}"], [f"musq{t}"],
                            lambda e, t=t: e.tensor_mul(
                                stat2[:, 32 + t:33 + t], stat[:, 32 + t:33 + t],
                                stat[:, 32 + t:33 + t]), f"{tag}musq{t}")
                        P.g("vector", [f"ssq{t}", f"musq{t}"], [f"veps{t}"],
                            lambda e, t=t, eps=eps: e.tensor_scalar(
                                stat3[:, 32 + t:33 + t], stat[:, 16 + t:17 + t],
                                1.0 / D, eps, ALU.mult, ALU.add), f"{tag}v1{t}")
                        P.g("vector", [f"veps{t}", f"musq{t}"], [f"veps{t}"],
                            lambda e, t=t: e.tensor_sub(
                                stat3[:, 32 + t:33 + t], stat3[:, 32 + t:33 + t],
                                stat2[:, 32 + t:33 + t]), f"{tag}v2{t}")
                    c0 = 32 + (quad[0] % 16)
                    nr_rsqrt(stat3[:, c0:c0 + len(quad)], len(quad),
                             [f"veps{t}" for t in quad])
                    for t in quad:
                        apply_fn(t)

            def transpose_tile(tag, src_ap, src_res, dst_fn, gcol=None, bcol=None,
                               gres=[]):
                def tr(e, src_ap=src_ap):
                    last = None
                    for c in range(NCH):
                        last = e.transpose(pT2[:, c * 128:(c + 1) * 128],
                                           src_ap[:, c * 128:(c + 1) * 128],
                                           id_bf[:, :])
                    return last
                P.g("tensor", [src_res, "consts"], ["pT2"], tr, f"{tag}tr")
                if gcol is None:
                    def ev(e):
                        dst_ap, _ = dst_fn(None)
                        return e.tensor_copy(
                            dst_ap, pT2[:, :].rearrange("p (c x) -> p c x", x=128))
                    _, dres = dst_fn(None)
                    P.g("vector", ["pT2"], dres, ev, f"{tag}ev")
                else:
                    for c in range(NCH):
                        dst_ap, dres = dst_fn(c)
                        P.g("vector", ["pT2"] + gres, dres,
                            lambda e, c=c, dst_ap=dst_ap: e.tensor_scalar(
                                dst_ap, pT2[:, c * 128:(c + 1) * 128],
                                gcol[:, c:c + 1], bcol[:, c:c + 1],
                                ALU.mult, ALU.add), f"{tag}ev{c}")

            def emit_ffn(tag):
                sF = sR if True else None
                for wname, wsrc in (("wr_res", wrT), ("wk_res", wk2T),
                                    ("wv_res", wvT)):
                    P.dma("gpsimd", "wa", [], [wname],
                          lambda e, wsrc=wsrc, wname=wname: [e.dma_start(
                              ffn_w[wname][:, :].rearrange("p (c f) -> p c f", c=NCH),
                              wsrc.rearrange("(c p) f -> p c f", p=128))],
                          note=f"{tag}{wname}")
                for mname, msrc in (("murc", mu_r_c), ("mukc", mu_k_c)):
                    load_cols(f"{tag}{mname}", msrc, 8,
                              mucols[mname][:, :], mname)
                    P.g("vector", [mname], [mname + "o"],
                        lambda e, mname=mname: e.tensor_scalar(
                            mucols[mname + "o"][:, :], mucols[mname][:, :],
                            -1.0, 1.0, ALU.mult, ALU.add), f"{tag}{mname}o")
                def zf(e):
                    last = None
                    for c in range(NCH):
                        last = e.memset(yfm[:, c * 2049:c * 2049 + 1], 0.0)
                    return last
                P.g("gpsimd", [], ["yfm0"], zf, f"{tag}zf")

                yv = yfm.rearrange("p (c x) -> p c x", x=2049)

                def apply_y(t):
                    P.g("vector", [zr(t), f"mu{t}", f"veps{t}"], ["htm2"],
                        lambda e, t=t: e.tensor_scalar(
                            htm2[:, :], zt(t), stat[:, 32 + t:33 + t],
                            stat3[:, 32 + t:33 + t], ALU.subtract, ALU.mult),
                        f"{tag}lna{t}")
                    transpose_tile(
                        f"{tag}y{t}", htm2, "htm2",
                        lambda c, t=t: (yv[:, :, 1 + t * 128: 1 + (t + 1) * 128],
                                        ["yfm"]))
                ln_ln(f"{tag}ln", list(range(NT)), lambda t: (zt(t), zr(t)),
                      EPS_OUT, apply_y)

                for g in range(4):
                    for c in range(NCH):
                        ya = yv[:, c, 1 + g * 512: 1 + (g + 1) * 512]
                        yp = yv[:, c, g * 512: (g + 1) * 512]
                        P.g("vector", ["yfm", "yfm0"], [f"dl{c}"],
                            lambda e, ya=ya, yp=yp, c=c: e.tensor_sub(
                                dscr[:, c * 512:(c + 1) * 512], yp, ya),
                            f"{tag}dl{g}.{c}")
                        P.g("vector", [f"dl{c}", "murco", "yfm"], [f"xr{c}"],
                            lambda e, ya=ya, c=c: e.scalar_tensor_tensor(
                                xrb[:, c * 512:(c + 1) * 512],
                                dscr[:, c * 512:(c + 1) * 512],
                                mucols["murco"][:, c:c + 1], ya,
                                ALU.mult, ALU.add), f"{tag}xr{g}.{c}")
                        P.g("vector", [f"dl{c}", "mukco", "yfm"], [f"xk{c}"],
                            lambda e, ya=ya, c=c: e.scalar_tensor_tensor(
                                xkb[:, c * 512:(c + 1) * 512],
                                dscr[:, c * 512:(c + 1) * 512],
                                mucols["mukco"][:, c:c + 1], ya,
                                ALU.mult, ALU.add), f"{tag}xk{g}.{c}")
                    for fc in range(NCH):
                        ppa, prs = pv(pY0, (fc % 4) * 512, (fc % 4) * 512 + 512)
                        def kmm(e, fc=fc, ppa=ppa):
                            last = None
                            for c in range(NCH):
                                last = e.matmul(
                                    ppa, ffn_w["wk_res"][:, c * 1024 + fc * 128:
                                                         c * 1024 + (fc + 1) * 128],
                                    xkb[:, c * 512:(c + 1) * 512],
                                    start=(c == 0), stop=(c == NCH - 1))
                            return last
                        P.g("tensor", ["wk_res"] + [f"xk{c}" for c in range(NCH)],
                            prs, kmm, f"{tag}kp{g}.{fc}")
                        P.g("vector", prs, ["cscr"],
                            lambda e, ppa=ppa: e.tensor_scalar_max(
                                cscr[:, 0:512], ppa, 0.0), f"{tag}krl{g}.{fc}")
                        P.g("scalar", ["cscr"], [f"ka{fc}"],
                            lambda e, fc=fc: e.activation(
                                kab[:, fc * 512:(fc + 1) * 512], cscr[:, 0:512],
                                AF.Square), f"{tag}ksq{g}.{fc}")
                    for tr_ in range(4):
                        t = g * 4 + tr_
                        ppr, rrs = pv(pY1, 0, 1024)
                        def rmm(e, tr_=tr_, ppr=ppr):
                            last = None
                            for n in range(2):
                                for c in range(NCH):
                                    last = e.matmul(
                                        ppr[:, n * 512:(n + 1) * 512],
                                        xrb[:, c * 512 + tr_ * 128:
                                            c * 512 + tr_ * 128 + 128],
                                        ffn_w["wr_res"][:, c * 1024 + n * 512:
                                                        c * 1024 + (n + 1) * 512],
                                        start=(c == 0), stop=(c == NCH - 1))
                            return last
                        P.g("tensor", ["wr_res"] + [f"xr{c}" for c in range(NCH)],
                            rrs, rmm, f"{tag}rp{t}")
                        P.g("scalar", rrs, ["rtm"],
                            lambda e, ppr=ppr: e.activation(
                                rtm[:, :], ppr, AF.Sigmoid), f"{tag}sig{t}")
                        ppv, vrs = pv(pY0, 0, 1024)
                        def vmm(e, tr_=tr_, ppv=ppv):
                            last = None
                            for n in range(2):
                                for c in range(NCH):
                                    last = e.matmul(
                                        ppv[:, n * 512:(n + 1) * 512],
                                        kab[:, c * 512 + tr_ * 128:
                                            c * 512 + tr_ * 128 + 128],
                                        ffn_w["wv_res"][:, c * 1024 + n * 512:
                                                        c * 1024 + (n + 1) * 512],
                                        start=(c == 0), stop=(c == NCH - 1))
                            return last
                        P.g("tensor", ["wv_res"] + [f"ka{c}" for c in range(NCH)],
                            vrs, vmm, f"{tag}vp{t}")
                        P.g("vector", vrs + ["rtm"], ["cscr"],
                            lambda e, ppv=ppv: e.scalar_tensor_tensor(
                                cscr[:, :], ppv, 1.0, rtm[:, :],
                                ALU.mult, ALU.mult), f"{tag}cmb{t}")
                        P.g("gpsimd", ["cscr", zr(t)], [zr(t)],
                            lambda e, t=t: e.tensor_add(zt(t), zt(t), cscr[:, :]),
                            f"{tag}zadd{t}")

            def alloc_ffn_bufs(sx, sfx):
                sbX = lambda n, shape, dt: sx.enter_context(
                    nc.sbuf_tensor(n + sfx, shape, dt))
                fw = {n: sbX(n, [128, NCH * 1024], BF16)
                      for n in ("wr_res", "wk_res", "wv_res")}
                mc = {n: sbX(n, [128, 8], F32)
                      for n in ("murc", "mukc", "murco", "mukco")}
                return (fw, mc, sbX("yfm", [128, NCH * 2049], BF16),
                        sbX("dscr", [128, NCH * 512], BF16),
                        sbX("xrb", [128, NCH * 512], BF16),
                        sbX("xkb", [128, NCH * 512], BF16),
                        sbX("kab", [128, NCH * 512], BF16),
                        sbX("rtm", [128, D], F32))

            sF1 = ExitStack()
            sF1.__enter__()
            (ffn_w, mucols, yfm, dscr, xrb, xkb, kab, rtm) = alloc_ffn_bufs(sF1, "")

            if skip_f1:
                for t in range(NT):
                    P.g("gpsimd", [], [zr(t)],
                        lambda e, t=t: e.memset(zt(t), 0.01 * (t + 1)), f"zm{t}")
            if skip_early and not skip_f1:
                for t in range(NT):
                    P.g("gpsimd", [], [zr(t)],
                        lambda e, t=t: e.memset(zt(t), 0.01 * (t + 1)), f"zm{t}")
            for t in range(NT if not skip_early else 0):
                P.dma("sync", "xin", [], ["cscr"],
                      lambda e, t=t: [e.dma_start(cscr[:, :],
                                                  x_in[t * 128:(t + 1) * 128, :])],
                      note=f"zx{t}")
                P.dma("sync", "ao", [f"aod{t}"], ["htm2"],
                      lambda e, t=t: [e.dma_start(
                          htm2[:, :], aout_d.ap()[t * 128:(t + 1) * 128, :])],
                      note=f"za{t}")
                P.g("vector", ["cscr", "htm2"], [zr(t)],
                    lambda e, t=t: e.tensor_add(zt(t), cscr[:, :], htm2[:, :]),
                    f"zinit{t}")

            if not skip_f1:
                emit_ffn("f1")

            for t in range(NT if not skip_early else 0):
                P.dma("sync", "x2", [zr(t)], [f"x2d{t}"],
                      lambda e, t=t: [e.dma_start(x2_d.ap()[t * 128:(t + 1) * 128, :],
                                                  zt(t))], note=f"x2w{t}")

            if debug_stop == "x2":
                for t in range(NT):
                    P.dma("sync", "out", [zr(t)], [f"out{t}"],
                          lambda e, t=t: [e.dma_start(out[t * 128:(t + 1) * 128, :],
                                                      zt(t))])
                P.fence()
                with nc.Block() as block:
                    P.flush(block)
                sF1.__exit__(None, None, None)
                return nc

            def apply_lnb(t):
                P.g("vector", [zr(t), f"mu{t}", f"veps{t}"], [zr(t)],
                    lambda e, t=t: e.tensor_scalar(
                        zt(t), zt(t), stat[:, 32 + t:33 + t],
                        stat3[:, 32 + t:33 + t], ALU.subtract, ALU.mult),
                    f"lnb{t}")
            if not skip_f1:
                ln_ln("lnb", list(range(NT)), lambda t: (zt(t), zr(t)), EPS_OUT,
                      apply_lnb)

            P.fence()
            with nc.Block() as block:
                P.flush(block)
            sF1.__exit__(None, None, None)

            sB2 = ExitStack()
            sB2.__enter__()
            sbB = lambda n, shape, dt: sB2.enter_context(
                nc.sbuf_tensor(n, shape, dt))
            hfmt = [sbB(f"hfmt{i}", [128, NCH * 128], BF16) for i in range(2)]
            qs_scr = sbB("qs_scr", [128, 4 * ADIM], F32)
            ks_scr = sbB("ks_scr", [128, 4 * ADIM], F32)
            ds_scr = sbB("ds_scr", [128, ADIM], F32)
            A_bf = sbB("A_bf", [128, NT], BF16)
            qn = sbB("qn", [128, NT * ADIM], BF16)
            kn = sbB("kn", [128, NT * ADIM], BF16)
            kfmt = [sbB(f"kfmt{i}", [128, ADIM], BF16) for i in range(2)]
            otb = [sbB(f"otb{i}", [128, ADIM], BF16) for i in range(2)]
            ofmt = [sbB(f"ofmt{i}", [128, ADIM], BF16) for i in range(2)]
            wq_l = [sbB("wq_l0", [128, NCH * ADIM], BF16)]
            wk_l = [sbB("wk_l0", [128, NCH * ADIM], BF16)]
            wp_l = [sbB("wp_l0", [128, 2 * ADIM], BF16)]
            wf_l = [sbB("wf_l0", [128, 2 * D], BF16)]
            brow = [sbB("brow0", [1, 5 * ADIM + 2 * D], BF16)]
            b1row = [sbB("b1row0", [1, HID], BF16)]
            gbcol = [sbB("gbcol0", [128, 32], F32)]
            wgb = [sbB("wgb0", [128, ADIM], F32)]
            wp_s = sbB("wp_s", [128, 2 * ADIM], BF16)
            gcols = sbB("gcols", [128, 2], F32)
            h2fm = sbB("h2fm", [128, NCH * 512], BF16)
            gfm = sbB("gfm", [128, 32 * 512], BF16)
            w1buf = [sbB(f"w1buf{i}", [128, NCH * 512], BF16) for i in range(2)]
            w2buf = [sbB(f"w2buf{i}", [128, 4 * 512], BF16) for i in range(2)]

            def load_layer_weights(l):
                p = 0
                P.dma("gpsimd", "wb", [], [f"wq{p}"],
                      lambda e, l=l, p=p: [e.dma_start(
                          wq_l[p][:, :].rearrange("p (c f) -> p c f", c=NCH),
                          Wq[l].rearrange("(c p) f -> p c f", p=128))],
                      note=f"Lw q{l}")
                P.dma("gpsimd", "wb", [], [f"wk{p}"],
                      lambda e, l=l, p=p: [e.dma_start(
                          wk_l[p][:, :].rearrange("p (c f) -> p c f", c=NCH),
                          Wk[l].rearrange("(c p) f -> p c f", p=128))],
                      note=f"Lw k{l}")
                P.dma("gpsimd", "wb", [], [f"wpb{p}"],
                      lambda e, l=l, p=p: [e.dma_start(
                          wp_l[p][:, :].rearrange("p (c f) -> p c f", c=2),
                          Wp[l].rearrange("(c p) f -> p c f", p=128))],
                      note=f"Lw p{l}")
                P.dma("gpsimd", "wb", [], [f"wfb{p}"],
                      lambda e, l=l, p=p: [e.dma_start(
                          wf_l[p][:, :].rearrange("p (c f) -> p c f", c=2),
                          Wf[l].rearrange("(c p) f -> p c f", p=128))],
                      note=f"Lw f{l}")
                P.dma("gpsimd", "wb", [], [f"brow{p}"],
                      lambda e, l=l, p=p: [
                          e.dma_start(brow[p][0:1, 0:ADIM], bq[l:l + 1, :]),
                          e.dma_start(brow[p][0:1, ADIM:2 * ADIM], bk[l:l + 1, :]),
                          e.dma_start(brow[p][0:1, 2 * ADIM:3 * ADIM], bp[l:l + 1, :]),
                          e.dma_start(brow[p][0:1, 3 * ADIM:3 * ADIM + D],
                                      bf_[l:l + 1, :]),
                          e.dma_start(brow[p][0:1, 3 * ADIM + D:3 * ADIM + 2 * D],
                                      b2[l:l + 1, :])],
                      n=5, note=f"Lw b{l}")
                P.dma("gpsimd", "wb", [], [f"b1row{p}"],
                      lambda e, l=l, p=p: [e.dma_start(b1row[p][0:1, :],
                                                       b1[l:l + 1, :])],
                      note=f"Lw b1{l}")
                P.dma("gpsimd", "wb", [], ["rowscr"],
                      lambda e, l=l, p=p: [e.dma_start(rowscr[0:1, 0:ADIM],
                                                       w_g[l:l + 1, :])],
                      note=f"Lw wg{l}")
                P.g("vector", ["rowscr"], ["rowbf"],
                    lambda e: e.tensor_copy(rowbf[0:1, 0:ADIM], rowscr[0:1, 0:ADIM]),
                    f"wgbf{l}")
                P.g("tensor", ["rowbf", "consts"], ["pM2:b0"],
                    lambda e: e.matmul(pM2[:, 0:ADIM], ones_row[0:1, 0:128],
                                       rowbf[0:1, 0:ADIM], start=True, stop=True),
                    f"wgmm{l}")
                P.g("vector", ["pM2:b0"], [f"wgb{p}"],
                    lambda e, p=p: e.tensor_copy(wgb[p][:, :], pM2[:, 0:ADIM]),
                    f"wgev{l}")
                for i, (vec, cc) in enumerate(((ln1_g_c, 0), (ln1_b_c, 8),
                                               (ln2_g_c, 16), (ln2_b_c, 24))):
                    load_cols(f"Lcol{l}.{i}", vec[l], 8,
                              gbcol[p][:, cc:cc + 8], f"gb{p}_{cc}")

            load_layer_weights(0)

            def emit_attnb_layer(l):
                p = 0
                tag = f"B{l}"
                g1c = gbcol[p][:, 0:8]
                b1c = gbcol[p][:, 8:16]
                g2c = gbcol[p][:, 16:24]
                b2c = gbcol[p][:, 24:32]
                gres = [f"gb{p}_0", f"gb{p}_8", f"gb{p}_16", f"gb{p}_24"]

                if debug_stop in ("gbonly", "gbf1", "gbfull"):
                    P.dma("gpsimd", "dbg",
                          [f"gb{p}_0", f"gb{p}_8", f"gb{p}_16", f"gb{p}_24"],
                          ["outd"],
                          lambda e: [e.dma_start(out[768:896, 0:32],
                                                 gbcol[p][:, :])], n=1)
                    P.fence()
                    return True

                qk_slots = [(pY0, 0), (pY0, 512), (pY0, 1024), (pY0, 1536),
                            (pY1, 0), (pY1, 512)]
                def apply_h(t):
                    P.g("vector", [zr(t), f"mu{t}", f"veps{t}"], ["htm2"],
                        lambda e, t=t: e.tensor_scalar(
                            htm2[:, :], zt(t), stat[:, 32 + t:33 + t],
                            stat3[:, 32 + t:33 + t], ALU.subtract, ALU.mult),
                        f"{tag}h{t}")
                    hb = hfmt[t % 2]
                    transpose_tile(f"{tag}ht{t}", htm2, "htm2",
                                   lambda c, t=t: (
                                       hfmt[t % 2][:, c * 128:(c + 1) * 128],
                                       [f"hfmt{t % 2}"]),
                                   gcol=g1c, bcol=b1c, gres=gres)
                    sq = qk_slots[(2 * t) % 6]
                    sk = qk_slots[(2 * t + 1) % 6]
                    qpa, qrs = pv(sq[0], sq[1], sq[1] + ADIM)
                    kpa, krs2 = pv(sk[0], sk[1], sk[1] + ADIM)
                    for nm, wl, ppx, rsx, wres in (
                            ("q", wq_l[p], qpa, qrs, f"wq{p}"),
                            ("k", wk_l[p], kpa, krs2, f"wk{p}")):
                        def qmm(e, wl=wl, ppx=ppx, t=t, nm=nm, p=p):
                            for c in range(NCH):
                                e.matmul(ppx, hfmt[t % 2][:, c * 128:(c + 1) * 128],
                                         wl[:, c * ADIM:(c + 1) * ADIM],
                                         start=(c == 0), stop=False)
                            boff = 0 if nm == "q" else ADIM
                            return e.matmul(ppx, ones_row[0:1, 0:128],
                                            brow[p][0:1, boff:boff + ADIM]
                                            if nm == "q" else
                                            brow[p][0:1, ADIM:2 * ADIM],
                                            start=False, stop=True)
                        P.g("tensor", [f"hfmt{t % 2}", wres, f"brow{p}", "consts"],
                            rsx, qmm, f"{tag}{nm}mm{t}")
                        scr = qs_scr if nm == "q" else ks_scr
                        scol = (t % 4) * ADIM
                        P.g("scalar", rsx, [f"{nm}s{t % 4}"],
                            lambda e, ppx=ppx, scr=scr, scol=scol: e.activation(
                                scr[:, scol:scol + ADIM], ppx, AF.Identity),
                            f"{tag}{nm}ev{t}")
                        ssqc = 48 + (t % 4) if nm == "q" else 52 + (t % 4)
                        P.g("vector", [f"{nm}s{t % 4}"], [f"{nm}ssq{t % 4}"],
                            lambda e, scr=scr, scol=scol, ssqc=ssqc:
                            e.scalar_tensor_tensor(
                                ds_scr[:, :], scr[:, scol:scol + ADIM], 1.0,
                                scr[:, scol:scol + ADIM], ALU.mult, ALU.mult,
                                accum_out=stat[:, ssqc:ssqc + 1]),
                            f"{tag}{nm}sq{t}")

                def finish_quad(qd):
                    P.g("vector", [f"qssq{i}" for i in range(4)] +
                        [f"kssq{i}" for i in range(4)], ["qkss"],
                        lambda e: e.tensor_scalar_max(stat[:, 48:56],
                                                      stat[:, 48:56], 1e-24),
                        f"{tag}mx{qd}")
                    nr_rsqrt(stat[:, 48:56], 8, ["qkss"])
                    for i in range(4):
                        t = qd * 4 + i
                        for nm, scr, store, sres, ssqc in (
                                ("q", qs_scr, qn, "qn", 48 + i),
                                ("k", ks_scr, kn, "kn", 52 + i)):
                            P.g("vector", ["qkss", f"{nm}s{i}"], [sres],
                                lambda e, scr=scr, i=i, t=t, store=store,
                                       ssqc=ssqc: e.tensor_scalar(
                                    store[:, t * ADIM:(t + 1) * ADIM],
                                    scr[:, i * ADIM:(i + 1) * ADIM],
                                    stat[:, ssqc:ssqc + 1], None, ALU.mult),
                                f"{tag}{nm}n{t}")
                        P.g("vector", ["qn", f"wgb{p}"], [f"Acol{t}"],
                            lambda e, t=t, p=p: e.scalar_tensor_tensor(
                                ds_scr[:, :], qn[:, t * ADIM:(t + 1) * ADIM], 1.0,
                                wgb[p][:, :], ALU.mult, ALU.mult,
                                accum_out=stat2[:, t:t + 1]), f"{tag}A{t}")

                for qd in range(4):
                    ln_ln(f"{tag}l1q{qd}", list(range(qd * 4, qd * 4 + 4)),
                          lambda t: (zt(t), zr(t)), EPS_LN1, apply_h)
                    finish_quad(qd)
                    if debug_stop in (f"s1r_{l}", "gbs1") and qd == 0:
                        P.dma("gpsimd", "dbg",
                              ["qs0", "qs1", "qs2", "qs3", "qkss", "qn", "hfmt0",
                               "gb0_0", "gb0_8", "gb0_16", "gb0_24"],
                              ["outd"],
                              lambda e: [
                                  e.dma_start(out[0:128, :], qs_scr[:, :]),
                                  e.dma_start(out[512:640, 0:8], stat[:, 48:56]),
                                  e.dma_start(out[640:768, :],
                                              hfmt[0][:, :]),
                                  e.dma_start(out[768:896, 0:32],
                                              gbcol[0][:, :])], n=4)
                        P.fence()
                        return True

                if debug_stop == f"s1_{l}":
                    P.dma("gpsimd", "dbg", ["qn", "kn"], ["outd"],
                          lambda e: [
                              e.dma_start(out[0:512, :].rearrange(
                                  "(a p) b -> p a b", p=128),
                                  qn[:, :].rearrange("p (a b) -> p a b", a=4)),
                              e.dma_start(out[512:1024, :].rearrange(
                                  "(a p) b -> p a b", p=128),
                                  kn[:, :].rearrange("p (a b) -> p a b", a=4))],
                          n=2)
                    P.fence()
                    return True

                P.g("vector", [f"Acol{t}" for t in range(NT)], ["A_bf"],
                    lambda e: e.tensor_copy(A_bf[:, :], stat2[:, 0:NT]),
                    f"{tag}Abf")
                def ssA(e):
                    last = None
                    for t in range(NT):
                        last = e.matmul(pM2[0:1, 0:1], A_bf[:, t:t + 1],
                                        A_bf[:, t:t + 1], start=(t == 0),
                                        stop=(t == NT - 1))
                    return last
                P.g("tensor", ["A_bf"], ["pM2:b0"], ssA, f"{tag}ssA")
                P.g("vector", ["pM2:b0"], ["ssa"],
                    lambda e: e.tensor_scalar_max(stat2[0:1, 20:21], pM2[0:1, 0:1],
                                                  1e-24), f"{tag}ssAe")
                nr_rsqrt(stat2[0:1, 20:21], 1, ["ssa"], parts=1)
                for fc in range(2):
                    ppg, grs = pv(pY1, fc * 512, fc * 512 + 1)
                    def gmm(e, fc=fc, ppg=ppg):
                        last = None
                        for t in range(NT):
                            last = e.matmul(
                                ppg, qn[:, t * ADIM + fc * 128:
                                        t * ADIM + (fc + 1) * 128],
                                A_bf[:, t:t + 1], start=(t == 0),
                                stop=(t == NT - 1))
                        return last
                    P.g("tensor", ["qn", "A_bf"], grs, gmm, f"{tag}G{fc}")
                    P.g("vector", grs, [f"Gc{fc}"],
                        lambda e, fc=fc, ppg=ppg: e.tensor_copy(
                            gcols[:, fc:fc + 1], ppg), f"{tag}Gev{fc}")
                P.g("vector", ["ssa"], ["rowbf"],
                    lambda e: e.tensor_copy(rowbf[0:1, 0:1], stat2[0:1, 20:21]),
                    f"{tag}rnbf")
                P.g("tensor", ["rowbf", "consts"], ["pM2:b0"],
                    lambda e: e.matmul(pM2[:, 0:1], ones_row[0:1, 0:128],
                                       rowbf[0:1, 0:1], start=True, stop=True),
                    f"{tag}rnmm")
                P.g("vector", ["pM2:b0"], ["rncol"],
                    lambda e: e.tensor_copy(stat2[:, 21:22], pM2[:, 0:1]),
                    f"{tag}rncol")
                P.g("vector", ["rncol", "Gc0", "Gc1"], ["Gs"],
                    lambda e: e.tensor_scalar(gcols[:, 0:2], gcols[:, 0:2],
                                              stat2[:, 21:22], None, ALU.mult),
                    f"{tag}Gs")
                for c in range(2):
                    P.g("gpsimd", ["Gs", f"wpb{p}"], ["wp_s"],
                        lambda e, c=c: e.tensor_scalar(
                            wp_s[:, c * ADIM:(c + 1) * ADIM],
                            wp_l[p][:, c * ADIM:(c + 1) * ADIM],
                            gcols[:, c:c + 1], None, ALU.mult), f"{tag}wps{c}")

                def s3_stageA(t):
                    sl = t % 2
                    def trk(e, t=t):
                        last = None
                        for c in range(2):
                            last = e.transpose(
                                pT2[:, c * 128:(c + 1) * 128],
                                kn[:, t * ADIM + c * 128: t * ADIM + (c + 1) * 128],
                                id_bf[:, :])
                        return last
                    P.g("tensor", ["kn", "consts"], ["pT2"], trk, f"{tag}ktr{t}")
                    P.g("vector", ["pT2"], [f"kfmt{sl}"],
                        lambda e, sl=sl: e.tensor_copy(kfmt[sl][:, :],
                                                       pT2[:, 0:ADIM]),
                        f"{tag}kev{t}")
                    so = qk_slots[t % 6]
                    opa, ors = pv(so[0], so[1], so[1] + ADIM)
                    def omm(e, t=t, opa=opa, p=p, sl=sl):
                        for c in range(2):
                            e.matmul(opa, kfmt[sl][:, c * 128:(c + 1) * 128],
                                     wp_s[:, c * ADIM:(c + 1) * ADIM],
                                     start=(c == 0), stop=False)
                        e.matmul(opa, ones_row[0:1, 0:128],
                                 brow[p][0:1, 2 * ADIM:3 * ADIM], start=False,
                                 stop=False)
                        return e.matmul(opa, id_bf[:, :],
                                        qn[:, t * ADIM:(t + 1) * ADIM],
                                        start=False, stop=True)
                    P.g("tensor", [f"kfmt{sl}", "wp_s", f"brow{p}", "qn", "consts"],
                        ors, omm, f"{tag}omm{t}")
                    P.g("vector", ors, [f"otb{sl}"],
                        lambda e, opa=opa, sl=sl: e.tensor_copy(otb[sl][:, :], opa),
                        f"{tag}oev{t}")

                def s3_stageB(t):
                    sl = t % 2
                    def tro(e, sl=sl):
                        last = None
                        for c in range(2):
                            last = e.transpose(pT2[:, c * 128:(c + 1) * 128],
                                               otb[sl][:, c * 128:(c + 1) * 128],
                                               id_bf[:, :])
                        return last
                    P.g("tensor", [f"otb{sl}", "consts"], ["pT2"], tro,
                        f"{tag}otr{t}")
                    P.g("vector", ["pT2"], [f"ofmt{sl}"],
                        lambda e, sl=sl: e.tensor_copy(ofmt[sl][:, :],
                                                       pT2[:, 0:ADIM]),
                        f"{tag}oevt{t}")
                    apa, ars = pv(pY1, 0, 1024) if t % 2 == 0 else pv(pY0, 1024, 2048)
                    def amm(e, apa=apa, p=p, sl=sl):
                        last = None
                        for n in range(2):
                            for c in range(2):
                                e.matmul(apa[:, n * 512:(n + 1) * 512],
                                         ofmt[sl][:, c * 128:(c + 1) * 128],
                                         wf_l[p][:, c * D + n * 512:
                                                 c * D + (n + 1) * 512],
                                         start=(c == 0), stop=False)
                            last = e.matmul(
                                apa[:, n * 512:(n + 1) * 512],
                                ones_row[0:1, 0:128],
                                brow[p][0:1, 3 * ADIM + n * 512:
                                        3 * ADIM + (n + 1) * 512],
                                start=False, stop=True)
                        return last
                    P.g("tensor", [f"ofmt{sl}", f"wfb{p}", f"brow{p}", "consts"],
                        ars, amm, f"{tag}fmm{t}")
                    P.g("vector", ars + [zr(t)], [zr(t)],
                        lambda e, t=t, apa=apa: e.tensor_add(zt(t), zt(t), apa),
                        f"{tag}zadd{t}")

                for t in range(NT):
                    s3_stageA(t)
                    if t > 0:
                        s3_stageB(t - 1)
                s3_stageB(NT - 1)

                for g in range(4):
                    def apply_h2(t):
                        P.g("vector", [zr(t), f"mu{t}", f"veps{t}"], ["htm2"],
                            lambda e, t=t: e.tensor_scalar(
                                htm2[:, :], zt(t), stat[:, 32 + t:33 + t],
                                stat3[:, 32 + t:33 + t], ALU.subtract, ALU.mult),
                            f"{tag}h2{t}")
                        tr_ = t % 4
                        transpose_tile(f"{tag}h2t{t}", htm2, "htm2",
                                       lambda c, tr_=tr_: (
                                           h2fm[:, c * 512 + tr_ * 128:
                                                c * 512 + tr_ * 128 + 128],
                                           ["h2fm"]),
                                       gcol=g2c, bcol=b2c, gres=gres)
                    ln_ln(f"{tag}l2g{g}", list(range(g * 4, g * 4 + 4)),
                          lambda t: (zt(t), zr(t)), EPS_LN1, apply_h2)

                    rounds = []
                    fc0 = 0
                    tog = 0
                    while fc0 < 32:
                        sz = 4 if tog == 0 else 2
                        sz = min(sz, 32 - fc0)
                        rounds.append((fc0, sz, tog))
                        fc0 += sz
                        tog ^= 1
                    for (fc0, sz, tog) in rounds:
                        for fc in range(fc0, fc0 + sz):
                            sl = fc // 4
                            if fc % 4 == 0:
                                P.dma("gpsimd", f"w1{sl % 2}", [f"w1bf{l}"],
                                      [f"w1s{sl % 2}"],
                                      lambda e, l=l, sl=sl: [e.dma_start(
                                          w1buf[sl % 2][:, :].rearrange(
                                              "p (c f) -> p c f", c=NCH),
                                          w1bf_d.ap()[l][:, sl * 512:(sl + 1) * 512]
                                          .rearrange("(c p) f -> p c f", p=128))],
                                      note=f"{tag}w1s{g}.{sl}")
                        ppt = pY0 if tog == 0 else pY1
                        for i, fc in enumerate(range(fc0, fc0 + sz)):
                            sl = fc // 4
                            ppa, prs = pv(ppt, i * 512, (i + 1) * 512)
                            def ymm(e, fc=fc, sl=sl, ppa=ppa, p=p):
                                for c in range(NCH):
                                    e.matmul(
                                        ppa,
                                        w1buf[sl % 2][:, c * 512 + (fc % 4) * 128:
                                                      c * 512 + (fc % 4) * 128 + 128],
                                        h2fm[:, c * 512:(c + 1) * 512],
                                        start=(c == 0), stop=False)
                                return e.matmul(
                                    ppa, b1row[p][0:1, fc * 128:(fc + 1) * 128],
                                    ones_row[0:1, 0:512], start=False, stop=True)
                            P.g("tensor", [f"w1s{sl % 2}", "h2fm", f"b1row{p}",
                                           "consts"], prs, ymm, f"{tag}y1.{g}.{fc}")
                        allrs = bank_resources(ppt.name, 0, sz * 512)
                        P.g("scalar", allrs, [f"gfm{fc0}"],
                            lambda e, fc0=fc0, sz=sz, ppt=ppt: e.activation(
                                gfm[:, fc0 * 512:(fc0 + sz) * 512],
                                ppt[:, 0:sz * 512], AF.Gelu), f"{tag}gel{g}.{fc0}")

                    if debug_stop == f"s4g_{l}" and g == 0:
                        P.dma("gpsimd", "dbg",
                              [f"gfm{fc0}" for (fc0, _, _) in rounds] + ["h2fm"],
                              ["outd"],
                              lambda e: [
                                  e.dma_start(out[0:2048, 0:512].rearrange(
                                      "(a p) b -> p a b", p=128),
                                      gfm[:, 0:16 * 512].rearrange(
                                          "p (a b) -> p a b", b=512)),
                                  e.dma_start(out[0:128, 512:1024],
                                              h2fm[:, 0:512])], n=2)
                        P.fence()
                        return True

                    gres_all = [f"gfm{fc0}" for (fc0, _, _) in rounds]
                    for n in range(2):
                        for qtr in range(8):
                            P.dma("gpsimd", f"w2{qtr % 2}", [f"w2bf{l}"],
                                  [f"w2s{qtr % 2}"],
                                  lambda e, l=l, n=n, qtr=qtr: [e.dma_start(
                                      w2buf[qtr % 2][:, :].rearrange(
                                          "p (c f) -> p c f", c=4),
                                      w2bf_d.ap()[l][qtr * 512:(qtr + 1) * 512,
                                                     n * 512:(n + 1) * 512]
                                      .rearrange("(c p) f -> p c f", p=128))],
                                  note=f"{tag}w2s{g}.{n}.{qtr}")
                            for tr_ in range(4):
                                t = g * 4 + tr_
                                ppa, prs = pv(pY0, tr_ * 512, (tr_ + 1) * 512)
                                def y2mm(e, n=n, qtr=qtr, tr_=tr_, ppa=ppa, p=p):
                                    last = None
                                    for kq in range(4):
                                        kc = qtr * 4 + kq
                                        last = e.matmul(
                                            ppa,
                                            gfm[:, kc * 512 + tr_ * 128:
                                                kc * 512 + tr_ * 128 + 128],
                                            w2buf[qtr % 2][:, kq * 512:
                                                           (kq + 1) * 512],
                                            start=(qtr == 0 and kq == 0),
                                            stop=False)
                                    if qtr == 7:
                                        last = e.matmul(
                                            ppa, ones_row[0:1, 0:128],
                                            brow[p][0:1, 3 * ADIM + D + n * 512:
                                                    3 * ADIM + D + (n + 1) * 512],
                                            start=False, stop=True)
                                    return last
                                P.g("tensor",
                                    gres_all + [f"w2s{qtr % 2}", f"brow{p}",
                                                "consts"],
                                    prs, y2mm, f"{tag}y2.{g}.{n}.{qtr}.{tr_}")
                        for tr_ in range(4):
                            t = g * 4 + tr_
                            ppa, prs = pv(pY0, tr_ * 512, (tr_ + 1) * 512)
                            P.g("vector", prs + [zr(t)], [zr(t)],
                                lambda e, t=t, n=n, ppa=ppa: e.tensor_add(
                                    zt(t)[:, n * 512:(n + 1) * 512],
                                    zt(t)[:, n * 512:(n + 1) * 512], ppa),
                                f"{tag}z2{t}.{n}")

                if l + 1 < DEPTH_B:
                    load_layer_weights(l + 1)

                if debug_stop == f"zb{l}":
                    for t in range(NT):
                        P.dma("sync", "out", [zr(t)], [f"out{t}"],
                              lambda e, t=t: [e.dma_start(
                                  out[t * 128:(t + 1) * 128, :], zt(t))])
                    P.fence()
                    return True
                return False

            stopped = False
            for l in range(DEPTH_B):
                if emit_attnb_layer(l):
                    stopped = True
                    break
            with nc.Block() as block:
                P.flush(block)
            sB2.__exit__(None, None, None)
            if stopped:
                return nc

            for t in range(NT):
                P.dma("sync", "xin", [f"x2d{t}"], ["cscr"],
                      lambda e, t=t: [e.dma_start(
                          cscr[:, :], x2_d.ap()[t * 128:(t + 1) * 128, :])],
                      note=f"x3r{t}")
                P.g("vector", ["cscr", zr(t)], [zr(t)],
                    lambda e, t=t: e.tensor_add(zt(t), zt(t), cscr[:, :]),
                    f"x3{t}")

            if debug_stop == "x3":
                for t in range(NT):
                    P.dma("sync", "out", [zr(t)], [f"out{t}"],
                          lambda e, t=t: [e.dma_start(out[t * 128:(t + 1) * 128, :],
                                                      zt(t))])
                P.fence()
                with nc.Block() as block:
                    P.flush(block)
                return nc

            sF2 = ExitStack()
            sF2.__enter__()
            (ffn_w, mucols, yfm, dscr, xrb, xkb, kab, rtm) = alloc_ffn_bufs(sF2, "_2")
            emit_ffn("f2")

            for t in range(NT):
                P.dma("sync", "out", [zr(t)], [f"out{t}"],
                      lambda e, t=t: [e.dma_start(out[t * 128:(t + 1) * 128, :],
                                                  zt(t))], note=f"fin{t}")
            P.fence()
            with nc.Block() as block:
                P.flush(block)
            sF2.__exit__(None, None, None)

    return nc



def kernel(**inputs):
    n = 8
    x = np.ascontiguousarray(inputs["L0_attn_a_x"])
    in_map_common = {
        "wqkvT": np.ascontiguousarray(inputs["L0_attn_a_w_qkv"].T),
        "wkyT": np.ascontiguousarray(inputs["L0_attn_a_w_ky"].T),
        "wprojT": np.ascontiguousarray(inputs["L0_attn_a_w_proj"].T),
        "wrT": np.ascontiguousarray(inputs["L0_ffn_W_r"].T),
        "wk2T": np.ascontiguousarray(inputs["L0_ffn_W_k"].T),
        "wvT": np.ascontiguousarray(inputs["L0_ffn_W_v"].T),
        "mu_r": np.ascontiguousarray(inputs["L0_ffn_mu_r"]),
        "mu_k": np.ascontiguousarray(inputs["L0_ffn_mu_k"]),
        "mu_r_c": np.ascontiguousarray(
            inputs["L0_ffn_mu_r"].reshape(8, 128).T),
        "mu_k_c": np.ascontiguousarray(
            inputs["L0_ffn_mu_k"].reshape(8, 128).T),
        "ln1_g_c": np.ascontiguousarray(
            inputs["L0_attn_b_ln1_g"].reshape(4, 8, 128).transpose(0, 2, 1)),
        "ln1_b_c": np.ascontiguousarray(
            inputs["L0_attn_b_ln1_b"].reshape(4, 8, 128).transpose(0, 2, 1)),
        "ln2_g_c": np.ascontiguousarray(
            inputs["L0_attn_b_ln2_g"].reshape(4, 8, 128).transpose(0, 2, 1)),
        "ln2_b_c": np.ascontiguousarray(
            inputs["L0_attn_b_ln2_b"].reshape(4, 8, 128).transpose(0, 2, 1)),
        "ln1_g": np.ascontiguousarray(inputs["L0_attn_b_ln1_g"]),
        "ln1_b": np.ascontiguousarray(inputs["L0_attn_b_ln1_b"]),
        "Wq": np.ascontiguousarray(inputs["L0_attn_b_Wq"]),
        "bq": np.ascontiguousarray(inputs["L0_attn_b_bq"]),
        "Wk": np.ascontiguousarray(inputs["L0_attn_b_Wk"]),
        "bk": np.ascontiguousarray(inputs["L0_attn_b_bk"]),
        "w_g": np.ascontiguousarray(inputs["L0_attn_b_w_g"][:, :, 0]),
        "Wp": np.ascontiguousarray(inputs["L0_attn_b_Wp"]),
        "bp": np.ascontiguousarray(inputs["L0_attn_b_bp"]),
        "Wf": np.ascontiguousarray(inputs["L0_attn_b_Wf"]),
        "bf": np.ascontiguousarray(inputs["L0_attn_b_bf"]),
        "ln2_g": np.ascontiguousarray(inputs["L0_attn_b_ln2_g"]),
        "ln2_b": np.ascontiguousarray(inputs["L0_attn_b_ln2_b"]),
        "W1": np.ascontiguousarray(inputs["L0_attn_b_W1"]),
        "b1": np.ascontiguousarray(inputs["L0_attn_b_b1"]),
        "W2": np.ascontiguousarray(inputs["L0_attn_b_W2"]),
        "b2": np.ascontiguousarray(inputs["L0_attn_b_b2"]),
    }
    nc = build_program(debug_stop=DEBUG_STOP)
    in_maps = []
    for i in range(n):
        m = dict(in_map_common)
        m["x"] = np.ascontiguousarray(x[i])
        in_maps.append(m)
    res = run_bass_kernel_spmd(nc, in_maps, core_ids=list(range(n)))
    return np.stack([r["out"] for r in res.results], axis=0)
